# version 23
# baseline (speedup 1.0000x reference)
import numpy as np
from contextlib import ExitStack

import concourse.bass as bass
import concourse.mybir as mybir
from concourse.bass_utils import run_bass_kernel_spmd

F32 = mybir.dt.float32
BF16 = mybir.dt.bfloat16
I8 = mybir.dt.int8
AF = mybir.ActivationFunctionType
ALU = mybir.AluOpType
AXX = mybir.AxisListType.X

NCORE = 8
NT = 2048
D = 1024
DC = 8
H = 8
DK = 128
DV = 128
NG = 4
GT = 512
CH = 128
II = 2048
IC = 16
NE = 9
HALF = 1024


class _Sem:
    __slots__ = ("name", "n", "h")
    def __init__(self, name):
        self.name, self.n, self.h = name, 0, None


class _Buf:
    __slots__ = ("r",)
    def __init__(self):
        self.r = []
    def take(self):
        r, self.r = self.r, []
        return r


class _Plan:
    def __init__(self):
        self.sems = []
        self.blocks = [[]]
        self.self_sems = {}

    def sem(self, name):
        s = _Sem(name)
        self.sems.append(s)
        return s

    def barrier(self):
        if self.blocks[-1]:
            self.blocks.append([])

    def op(self, eng, fn, waits=(), incs=()):
        ss = self.self_sems.get(eng)
        waits = list(waits)
        incs = list(incs)
        if ss is not None:
            waits.append((ss, ss.n))
            if not any(s is ss for s, _ in incs):
                incs.append((ss, 1))
        wd = {}
        for s, v in waits:
            if v <= 0:
                continue
            assert v <= s.n, f"wait-ahead on {s.name}: {v} > {s.n}"
            wd[s] = max(wd.get(s, 0), v)
        self.blocks[-1].append((eng, fn, tuple(wd.items()), tuple(incs)))
        out = []
        for s, k in incs:
            s.n += k
            out.append((s, s.n))
        return out

    def realize(self, nc):
        with ExitStack() as es:
            for s in self.sems:
                s.h = es.enter_context(nc.semaphore(s.name))
            for ops in self.blocks:
                engs = []
                for o in ops:
                    if o[0] not in engs:
                        engs.append(o[0])
                with nc.Block() as blk:
                    for en in engs:
                        eng_ops = [o for o in ops if o[0] == en]

                        def body(eng, eng_ops=eng_ops):
                            for _, fn, waits, incs in eng_ops:
                                for s, v in waits:
                                    eng.wait_ge(s.h, v)
                                ins = fn(eng)
                                for s, k in incs:
                                    ins = ins.then_inc(s.h, k)
                        getattr(blk, en)(body)


def build_program(stage="full"):
    nc = bass.Bass("TRN2", target_bir_lowering=False)
    P = _Plan()
    ld = P.sem("ld")
    wd = P.sem("wd")
    lx = [P.sem("lxa"), P.sem("lxb")]
    lcc = P.sem("lcc")
    lx2 = P.sem("lx2")
    ls1 = P.sem("ls1")
    lxt = P.sem("lxt")
    lxh = P.sem("lxh")
    lr1 = P.sem("lr1")
    lcr = P.sem("lcr")
    wdp = [P.sem("wda"), P.sem("wdb"), P.sem("wdc")]
    pe = P.sem("pe")
    ve = P.sem("ve")
    se = P.sem("se")
    ge = P.sem("ge")
    cc = P.sem("cc")
    od = P.sem("od")
    P.self_sems = {"vector": ve, "scalar": se}

    xsl = nc.dram_tensor("xsl", [NT, D], F32, kind="ExternalInput").ap()
    wq_d = nc.dram_tensor("wq", [D, D], F32, kind="ExternalInput").ap()
    wk_d = nc.dram_tensor("wk", [D, D], F32, kind="ExternalInput").ap()
    wv_d = nc.dram_tensor("wv", [D, D], F32, kind="ExternalInput").ap()
    wg_d = nc.dram_tensor("wg", [D, D], F32, kind="ExternalInput").ap()
    wo_d = nc.dram_tensor("wo", [D, D], F32, kind="ExternalInput").ap()
    gwT_d = nc.dram_tensor("gwT", [D, H], F32, kind="ExternalInput").ap()
    gbias_d = nc.dram_tensor("gbias", [128, 8], F32, kind="ExternalInput").ap()
    w1T_d = nc.dram_tensor("w1T", [NE, D, II], F32, kind="ExternalInput").ap()
    w3T_d = nc.dram_tensor("w3T", [NE, D, II], F32, kind="ExternalInput").ap()
    w2T_d = nc.dram_tensor("w2T", [NE, II, D], F32, kind="ExternalInput").ap()
    um_d = nc.dram_tensor("um", [128, 6], F32, kind="ExternalInput").ap()
    out_d = nc.dram_tensor("out", [NT, D], F32, kind="ExternalOutput").ap()

    if stage == "fulldbg":
        f_y = nc.dram_tensor("f_y", [128, DC * HALF], F32, kind="ExternalOutput").ap()
        f_cfm = nc.dram_tensor("f_cfm", [8, NT], F32, kind="ExternalOutput").ap()
        f_xth = nc.dram_tensor("f_xth", [128, DC * HALF], BF16, kind="ExternalOutput").ap()
        f_h1s = nc.dram_tensor("f_h1s", [128, IC * HALF], BF16, kind="ExternalOutput").ap()
        f_cbc = nc.dram_tensor("f_cbc", [128, HALF], F32, kind="ExternalOutput").ap()
    if stage == "x1":
        dbg_o = nc.dram_tensor("dbg_o", [128, H * NT], BF16, kind="ExternalOutput").ap()
        dbg_q = nc.dram_tensor("dbg_q", [128, H * NT], BF16, kind="ExternalOutput").ap()
        dbg_s = nc.dram_tensor("dbg_s", [128, H * 128], F32, kind="ExternalOutput").ap()
        dbg_p = nc.dram_tensor("dbg_p", [128, H * 16], F32, kind="ExternalOutput").ap()
        dbg_u = nc.dram_tensor("dbg_u", [128, GT], F32, kind="ExternalOutput").ap()
        dbg_qn = nc.dram_tensor("dbg_qn", [128, GT], BF16, kind="ExternalOutput").ap()
        dbg_kn = nc.dram_tensor("dbg_kn", [128, GT], BF16, kind="ExternalOutput").ap()
        dbg_xnf = nc.dram_tensor("dbg_xnf", [128, DC * GT], BF16, kind="ExternalOutput").ap()
        dbg_v = nc.dram_tensor("dbg_v", [128, 4 * 1024], BF16, kind="ExternalOutput").ap()
    x1sp = nc.dram_tensor("x1sp", [NT, D], F32).ap()
    xtsp = nc.dram_tensor("xtsp", [128, DC, NT], BF16).ap()
    ccin = nc.dram_tensor("ccin", [128, 1032], F32)
    ccout = nc.dram_tensor("ccout", [512, 1032], F32)

    es = ExitStack()
    sb = lambda n, s, d=F32: es.enter_context(nc.sbuf_tensor(n, s, d))

    with es:
        ident_f = sb("ident_f", [128, 128])
        ident_b = sb("ident_b", [128, 128], BF16)
        mask_f = sb("mask_f", [128, 128])
        ones_f = sb("ones_f", [128, 128])
        zeros_b = sb("zeros_b", [128, 128], BF16)
        z512 = sb("z512", [128, 512])
        zcol = sb("zcol", [128, 1])
        epscol = sb("epscol", [128, 1])
        ones1_b = sb("ones1_b", [1, 128], BF16)
        ones1_f = sb("ones1_f", [1, 128])
        onesk_b = sb("onesk_b", [128, 1], BF16)
        gbias_s = sb("gbias_s", [128, 8])
        um_s = sb("um_s", [128, 6])
        sqdump = sb("sqdump", [128, 1024], BF16)
        cols = sb("cols", [128, 16])

        pb = [es.enter_context(nc.psum_tensor(f"pb{i}", [128, 512], F32))
              for i in range(7)]
        pbt_b = es.enter_context(nc.psum_tensor("pbt_b", [128, 128], BF16))
        bankw = [_Buf() for _ in range(8)]

        P.op("gpsimd", lambda g: g.memset(ones_f[:], 1.0), incs=[(ge, 1)])

        P.op("gpsimd", lambda g: g.memset(zeros_b[:], 0.0), incs=[(ge, 1)])
        P.op("gpsimd", lambda g: g.memset(z512[:], 0.0), incs=[(ge, 1)])
        P.op("gpsimd", lambda g: g.memset(zcol[:], 0.0), incs=[(ge, 1)])
        P.op("gpsimd", lambda g: g.memset(epscol[:], 1e-5), incs=[(ge, 1)])
        P.op("gpsimd", lambda g: g.memset(ones1_b[:], 1.0), incs=[(ge, 1)])
        P.op("gpsimd", lambda g: g.memset(ones1_f[:], 1.0), incs=[(ge, 1)])
        P.op("gpsimd", lambda g: g.memset(onesk_b[:], 1.0), incs=[(ge, 1)])
        P.op("gpsimd", lambda g: g.affine_select(
            ident_f[:], ones_f[:], [[1, 128]], ALU.is_equal, 0.0,
            channel_multiplier=-1), waits=[(ge, 8)], incs=[(ge, 1)])
        P.op("gpsimd", lambda g: g.affine_select(
            mask_f[:], ones_f[:], [[1, 128]], ALU.is_ge, 0.0,
            channel_multiplier=-1), waits=[(ge, 9)], incs=[(ge, 1)])
        P.op("vector", lambda v: v.tensor_copy(ident_b[:], ident_f[:]),
             waits=[(ge, 10)], incs=[(ve, 1)])
        P.op("vector", lambda v: v.memset(pb[0][64:128, 0:64], 0.0),
             incs=[(ve, 1)])
        P.op("sync", lambda s: s.dma_start(out=gbias_s[:], in_=gbias_d[:, :]),
             incs=[(ld, 16)])
        P.op("sync", lambda s: s.dma_start(out=um_s[:], in_=um_d[:, :]),
             incs=[(ld, 16)])
        ld_base = ld.n

        col_i = [0]
        def col():
            c = col_i[0] % 16
            col_i[0] += 1
            return cols[:, c:c + 1]

        def mk_dma(eng_dst, eng_src):
            return lambda e: e.dma_start(out=eng_dst, in_=eng_src)

        def ln_tile(src, dst, extra_waits=()):
            sc, mc, lc, rc = col(), col(), col(), col()
            P.op("vector", lambda v: v.reduce_sum(sc, src, axis=AXX),
                 waits=list(extra_waits), incs=[(ve, 1)])
            P.op("vector", lambda v: v.tensor_scalar(
                mc, sc, 1.0 / D, None, ALU.mult),
                 waits=[(ve, ve.n)], incs=[(ve, 1)])
            P.op("vector", lambda v: v.tensor_scalar(
                dst, src, mc, None, ALU.subtract),
                 waits=[(ve, ve.n)], incs=[(ve, 1)])
            t0 = ve.n
            P.op("scalar", lambda s: s.activation(
                out=sqdump[:], in_=dst, func=AF.Square, accum_out=lc),
                 waits=[(ve, t0)], incs=[(se, 1)])
            P.op("scalar", lambda s: s.activation(
                out=lc, in_=lc, func=AF.Ln, scale=1.0 / D, bias=epscol[:]),
                 waits=[(se, se.n)], incs=[(se, 1)])
            P.op("scalar", lambda s: s.activation(
                out=rc, in_=lc, func=AF.Exp, scale=-0.5),
                 waits=[(se, se.n)], incs=[(se, 1)])
            t1 = se.n
            tick = P.op("vector", lambda v: v.tensor_scalar(
                dst, dst, rc, None, ALU.mult), waits=[(se, t1)], incs=[(ve, 1)])
            return tick

        with ExitStack() as ctx1:
            sb1 = lambda n, s, d=F32: ctx1.enter_context(nc.sbuf_tensor(n, s, d))
            o_all = sb1("o_all", [128, H * NT], BF16)
            qh_all = sb1("qh_all", [128, H * NT], BF16)
            S_f = sb1("S_f", [128, H * 128])
            S_b = sb1("S_b", [128, H * 128], BF16)
            P_all = sb1("P_all", [128, H * 16])
            Pcur = sb1("Pcur", [128, H])
            Dsl = sb1("Dsl", [128, H])
            o_w = _Buf()
            qh_w = _Buf()

            P.op("vector", lambda v: v.memset(S_f[:], 0.0), incs=[(ve, 1)])
            P.op("vector", lambda v: v.memset(S_b[:], 0.0), incs=[(ve, 1)])
            P.op("vector", lambda v: v.memset(Pcur[:], 1.0), incs=[(ve, 1)])
            sb_tick = {h: 0 for h in range(H)}

            with ExitStack() as ctxA:
                sba = lambda n, s, d=F32: ctxA.enter_context(nc.sbuf_tensor(n, s, d))
                wq_s = sba("wq_s", [128, DC * D], BF16)
                wk_s = sba("wk_s", [128, DC * D], BF16)
                wv_s = sba("wv_s", [128, DC * D], BF16)
                wg_s = sba("wg_s", [128, DC * D], BF16)
                x_st = [sba(f"x_st{i}", [128, 1024]) for i in range(2)]
                xn_st = [sba(f"xn_st{i}", [128, 1024]) for i in range(2)]
                xnf = sba("xnf", [128, DC * GT], BF16)
                v_tm = sba("v_tm", [128, 4 * 1024], BF16)
                qn = [sba(f"qn{i}", [128, GT], BF16) for i in range(4)]
                kn = [sba(f"kn{i}", [128, GT], BF16) for i in range(4)]
                t_st = sba("t_st", [128, GT])
                u_st = sba("u_st", [128, GT])
                U_st = [sba(f"U_st{i}", [128, GT]) for i in range(4)]
                sq_b = sba("sq_b", [128, GT], BF16)
                ssq_s = u_st
                rn_b = sba("rn_b", [1, GT], BF16)
                d1m = sba("d1m", [128, GT])
                e1a = sba("e1a", [128, GT])
                qt_a = [sba(f"qt_{i}", [128, GT], BF16) for i in range(2)]
                kt_a = [sba(f"kt_{i}", [128, GT], BF16) for i in range(2)]
                kh_a = [sba(f"kh_{i}", [128, GT], BF16) for i in range(2)]
                scm = [sba(f"scm{i}", [128, 8]) for i in range(2)]
                am_p = [sba(f"am_{i}", [128, 128], BF16) for i in range(2)]
                khtm_p = [sba(f"khtm_{i}", [128, 128], BF16) for i in range(2)]
                e12_w = [_Buf(), _Buf(), _Buf(), _Buf()]
                x_w = [_Buf(), _Buf()]
                xn_w = [_Buf(), _Buf()]
                xnf_w = _Buf()
                vtm_w = _Buf()
                head_w = _Buf()

                def wload(dst, src):
                    s3 = src.rearrange("(dc p) n -> p dc n", p=128)
                    return P.op("gpsimd",
                                lambda g, d=dst, s=s3: g.dma_start(out=d.ap().rearrange(
                                    "p (dc n) -> p dc n", dc=DC), in_=s),
                                incs=[(wd, 16)])
                wload(wq_s, wq_d)
                wload(wk_s, wk_d)
                wload(wv_s, wv_d)
                wload(wg_s, wg_d)
                wd_attn = wd.n

                for g in range(NG):
                    for t in range(4):
                        bi = t % 2
                        tok0 = g * GT + t * 128
                        P.op("sync", lambda s, b=bi, o=tok0: s.dma_start(
                            out=x_st[b][:], in_=xsl[o:o + 128, :]),
                             waits=x_w[bi].take() + xn_w[bi].take(),
                             incs=[(lx[bi], 16)])
                        lw = lx[bi].n
                        tick = ln_tile(x_st[bi][:], xn_st[bi][:],
                                       extra_waits=[(lx[bi], lw)])
                        x_w[bi].r.extend(tick)
                        for dc in range(DC):
                            bk = dc % 2
                            P.op("tensor", lambda te, b=bi, d=dc, k=bk: te.transpose(
                                pb[k][:, 0:128], xn_st[b][:, d * 128:(d + 1) * 128],
                                ident_f[:]),
                                 waits=[(ve, ve.n)] + bankw[bk].take(),
                                 incs=[(pe, 1)])
                            pw = pe.n
                            tk = P.op("vector", lambda v, b=bi, d=dc, k=bk, tt=t: v.tensor_copy(
                                xnf[:, d * GT + tt * 128:d * GT + tt * 128 + 128],
                                pb[k][:, 0:128]),
                                 waits=[(pe, pw)] + xnf_w.take(), incs=[(ve, 1)])
                            bankw[bk].r.extend(tk)
                            xn_w[bi].r.extend(tk)
                    xnf_done = ve.n

                    for t in range(4):
                        for hf in range(2):
                            bk = 2 + hf
                            for dc in range(DC):
                                P.op("tensor", lambda te, tt=t, h2=hf, d=dc, k=bk: te.matmul(
                                    pb[k][:, 0:512],
                                    xnf[:, d * GT + tt * 128:d * GT + tt * 128 + 128],
                                    wv_s[:, d * D + h2 * 512:d * D + h2 * 512 + 512],
                                    start=(d == 0), stop=(d == DC - 1)),
                                     waits=([(ve, xnf_done), (wd, wd_attn)]
                                            + bankw[bk].take()) if dc == 0 else [],
                                     incs=[(pe, 1)] if dc == DC - 1 else [])
                            pw = pe.n
                            tk = P.op("vector", lambda v, tt=t, h2=hf, k=bk: v.tensor_copy(
                                v_tm[:, tt * 1024 + h2 * 512:tt * 1024 + h2 * 512 + 512],
                                pb[k][:, 0:512]),
                                 waits=[(pe, pw)] + vtm_w.take(), incs=[(ve, 1)])
                            bankw[bk].r.extend(tk)

                    U_done = {}
                    scale_done = {}

                    def emit_proj(h):
                        par = h % 4
                        for dc in range(DC):
                            P.op("tensor", lambda te, d=dc, hh=h: te.matmul(
                                pb[4][:, 0:512],
                                wg_s[:, d * D + hh * 128:d * D + hh * 128 + 128],
                                xnf[:, d * GT:(d + 1) * GT],
                                start=(d == 0), stop=(d == DC - 1)),
                                 waits=([(ve, xnf_done)] + bankw[4].take())
                                 if dc == 0 else [],
                                 incs=[(pe, 1)] if dc == DC - 1 else [])
                        pw = pe.n
                        P.op("scalar", lambda s: s.activation(
                            out=t_st[:], in_=pb[4][:, 0:512], func=AF.Exp,
                            scale=-1.0), waits=[(pe, pw)], incs=[(se, 1)])
                        tk = P.op("scalar", lambda s: s.activation(
                            out=u_st[:], in_=t_st[:], func=AF.Ln, bias=1.0),
                             incs=[(se, 1)])
                        bankw[4].r.extend(tk)
                        sw = se.n
                        P.op("vector", lambda v, p=par: v.tensor_tensor_scan(
                            U_st[p][:], u_st[:], z512[:], 0.0, ALU.add, ALU.add),
                             waits=[(se, sw)], incs=[(ve, 1)])
                        U_done[h] = ve.n

                        for (wmat, dstn, bq) in ((wq_s, qn[par], 2 + (h % 2)),
                                                 (wk_s, kn[par], 3 - (h % 2))):
                            for dc in range(DC):
                                P.op("tensor", lambda te, d=dc, hh=h, w=wmat, k=bq: te.matmul(
                                    pb[k][:, 0:512],
                                    w[:, d * D + hh * 128:d * D + hh * 128 + 128],
                                    xnf[:, d * GT:(d + 1) * GT],
                                    start=(d == 0), stop=(d == DC - 1)),
                                     waits=([(ve, xnf_done)] + bankw[bq].take())
                                     if dc == 0 else [],
                                     incs=[(pe, 1)] if dc == DC - 1 else [])
                            pw = pe.n
                            P.op("scalar", lambda s, k=bq: s.activation(
                                out=sq_b[:], in_=pb[k][:, 0:512], func=AF.Square),
                                 waits=[(pe, pw)] + bankw[5].take(), incs=[(se, 1)])
                            sw = se.n
                            P.op("tensor", lambda te: te.matmul(
                                pb[5][0:1, 0:512], onesk_b[:], sq_b[:],
                                start=True, stop=True),
                                 waits=[(se, sw)], incs=[(pe, 1)])
                            pw = pe.n
                            tk = P.op("vector", lambda v: v.tensor_scalar_max(
                                ssq_s[0:1, :], pb[5][0:1, 0:512], 1e-24),
                                 waits=[(pe, pw)], incs=[(ve, 1)])
                            vw = ve.n
                            P.op("scalar", lambda s: s.activation(
                                out=ssq_s[0:1, :], in_=ssq_s[0:1, :], func=AF.Ln),
                                 waits=[(ve, vw)], incs=[(se, 1)])
                            P.op("scalar", lambda s: s.activation(
                                out=rn_b[0:1, :], in_=ssq_s[0:1, :], func=AF.Exp,
                                scale=-0.5),
                                 waits=[(se, se.n)], incs=[(se, 1)])
                            sw = se.n
                            P.op("tensor", lambda te: te.matmul(
                                pb[5][:, 0:512], ones1_b[0:1, :], rn_b[0:1, :],
                                start=True, stop=True),
                                 waits=[(se, sw)], incs=[(pe, 1)])
                            pw = pe.n
                            P.op("vector", lambda v, dst=dstn, k=bq: v.tensor_copy(
                                dst[:], pb[k][:, 0:512]),
                                 waits=[(pe, pw)], incs=[(ve, 1)])
                            tk = P.op("vector", lambda v, dst=dstn: v.tensor_tensor(
                                dst[:], dst[:], pb[5][:, 0:512], ALU.mult),
                                 waits=[(ve, ve.n)], incs=[(ve, 1)])
                            bankw[5].r.extend(tk)
                            bankw[bq].r.extend(tk)

                    def emit_scalings(h):
                        par = h % 4
                        sl = h % 2
                        ew = e12_w[sl].take()
                        for c in range(4):
                            P.op("vector", lambda v, c0=c * 128, p=par: v.tensor_scalar(
                                d1m[:, c0:c0 + 128], U_st[p][:, c0:c0 + 128],
                                U_st[p][:, c0 + 63:c0 + 64], None, ALU.subtract),
                                 waits=[(ve, U_done[h])] + (ew if c == 0 else []),
                                 incs=[(ve, 1)])
                        for c in range(4):
                            up = (zcol[:, 0:1] if c == 0
                                  else U_st[par][:, c * 128 - 1:c * 128])
                            P.op("vector", lambda v, c2=c, u=up, p=par, s2=sl: v.tensor_tensor(
                                scm[s2][:, c2:c2 + 1], u,
                                U_st[p][:, c2 * 128 + 63:c2 * 128 + 64],
                                ALU.subtract), incs=[(ve, 1)])
                        vw = ve.n
                        P.op("scalar", lambda s, s2=sl: s.activation(
                            out=scm[s2][:, 0:4], in_=scm[s2][:, 0:4], func=AF.Exp),
                             waits=[(ve, vw)], incs=[(se, 1)])
                        P.op("scalar", lambda s: s.activation(
                            out=e1a[:], in_=d1m[:], func=AF.Exp, scale=-1.0),
                             incs=[(se, 1)])
                        P.op("scalar", lambda s: s.activation(
                            out=d1m[:], in_=d1m[:], func=AF.Exp),
                             waits=[(se, se.n)], incs=[(se, 1)])
                        sw = se.n
                        P.op("vector", lambda v, p=par, s2=sl: v.tensor_tensor(
                            qt_a[s2][:], qn[p][:], e1a[:], ALU.mult),
                             waits=[(se, sw)], incs=[(ve, 1)])
                        P.op("vector", lambda v, p=par, s2=sl: v.tensor_tensor(
                            kt_a[s2][:], kn[p][:], d1m[:], ALU.mult),
                             incs=[(ve, 1)])
                        for c in range(4):
                            c0 = c * 128
                            qh_ap = qh_all[:, h * NT + g * GT + c0:
                                           h * NT + g * GT + c0 + 128]
                            P.op("vector", lambda v, qa=qh_ap, c0=c0, s2=sl: v.tensor_scalar(
                                qa, qt_a[s2][:, c0:c0 + 128],
                                scm[s2][:, c0 // 128:c0 // 128 + 1], None, ALU.mult),
                                 waits=qh_w.take(), incs=[(ve, 1)])
                            P.op("vector", lambda v, c0=c0, s2=sl: v.tensor_scalar(
                                kh_a[s2][:, c0:c0 + 128], kt_a[s2][:, c0:c0 + 128],
                                e1a[:, c0 + 127:c0 + 128], None, ALU.mult),
                                 incs=[(ve, 1)])
                            P.op("vector", lambda v, c2=c, s2=sl: v.tensor_scalar(
                                scm[s2][:, 4 + c2:5 + c2], e1a[:, c2 * 128 + 127:
                                                              c2 * 128 + 128],
                                scm[s2][:, c2:c2 + 1], None, ALU.mult),
                                 incs=[(ve, 1)])
                        scale_done[h] = ve.n

                    def chunk_body(h, c):
                        cp = h % 2
                        am_b, khtm_b = am_p[cp], khtm_p[cp]
                        nch = g * 4 + c
                        c0 = c * 128
                        qh_ap = qh_all[:, h * NT + g * GT + c0:
                                       h * NT + g * GT + c0 + 128]
                        qt_s = qt_a[cp][:, c0:c0 + 128]
                        kt_s = kt_a[cp][:, c0:c0 + 128]
                        kh_s = kh_a[cp][:, c0:c0 + 128]
                        dcol = scm[cp][:, 4 + c:5 + c]
                        bw0 = bankw[0].take()
                        P.op("tensor", lambda te, t=kt_s, u=qt_s: te.matmul(
                            pb[0][0:64, 0:64], t[:, 0:64], u[:, 0:64],
                            start=True, stop=True),
                             waits=[(ve, scale_done[h])] + bw0, incs=[(pe, 1)])
                        P.op("tensor", lambda te, t=kt_s, u=qt_s: te.matmul(
                            pb[0][64:128, 64:128], t[:, 64:128],
                            u[:, 64:128], start=True, stop=True),
                             incs=[(pe, 1)])
                        P.op("tensor", lambda te, t=kt_s, u=qt_s: te.matmul(
                            pb[0][0:64, 64:128], t[:, 0:64],
                            u[:, 64:128], start=True, stop=True),
                             incs=[(pe, 1)])
                        P.op("tensor", lambda te, t=kh_s: te.transpose(
                            pbt_b[:, 0:128], t, ident_b[:]),
                             waits=bankw[7].take(), incs=[(pe, 1)])
                        pw = pe.n
                        tk = P.op("vector", lambda v, t=am_b: v.tensor_tensor(
                            t[:], pb[0][:, 0:128], mask_f[:], ALU.mult),
                             waits=[(pe, pw)], incs=[(ve, 1)])
                        bankw[0].r.extend(tk)
                        tk = P.op("scalar", lambda s, t=khtm_b: s.copy(
                            t[:], pbt_b[:, 0:128]),
                             waits=[(pe, pw)], incs=[(se, 1)])
                        bankw[7].r.extend(tk)
                        kh_se = se.n
                        vw = ve.n
                        vsl = v_tm[:, c * 1024 + h * 128:c * 1024 + h * 128 + 128]
                        P.op("tensor", lambda te, hh=h, qa=qh_ap: te.matmul(
                            pb[6][:, 0:128],
                            S_b[:, hh * 128:hh * 128 + 128],
                            qa, start=True, stop=False),
                             waits=[(ve, vw), (se, sb_tick[h])] + bankw[6].take(),
                             incs=[(pe, 1)])
                        P.op("tensor", lambda te, vs=vsl, t=am_b: te.matmul(
                            pb[6][:, 0:128], vs, t[:],
                            start=False, stop=True), incs=[(pe, 1)])
                        o_pe = pe.n
                        P.op("tensor", lambda te, vs=vsl, t=khtm_b: te.matmul(
                            pb[1][:, 0:128], t[:], vs,
                            start=True, stop=True),
                             waits=[(se, kh_se)] + bankw[1].take(),
                             incs=[(pe, 1)])
                        m_pe = pe.n
                        o_ap = o_all[:, h * NT + g * GT + c0:
                                     h * NT + g * GT + c0 + 128]
                        tk = P.op("vector", lambda v, oa=o_ap: v.tensor_copy(
                            oa, pb[6][:, 0:128]),
                             waits=[(pe, o_pe)] + o_w.take(), incs=[(ve, 1)])
                        bankw[6].r.extend(tk)
                        pidx = P_all[:, h * 16 + nch:h * 16 + nch + 1]
                        P.op("vector", lambda v, pi=pidx, hh=h: v.tensor_copy(
                            pi, Pcur[:, hh:hh + 1]), incs=[(ve, 1)])
                        tk = P.op("vector", lambda v, hh=h, dc2=dcol:
                                  v.scalar_tensor_tensor(
                                      S_f[:, hh * 128:hh * 128 + 128],
                                      S_f[:, hh * 128:hh * 128 + 128],
                                      dc2, pb[1][:, 0:128],
                                      ALU.mult, ALU.add),
                                  waits=[(pe, m_pe)], incs=[(ve, 1)])
                        bankw[1].r.extend(tk)
                        sv = ve.n
                        tk = P.op("scalar", lambda s, hh=h: s.copy(
                            S_b[:, hh * 128:hh * 128 + 128],
                            S_f[:, hh * 128:hh * 128 + 128]),
                             waits=[(ve, sv)], incs=[(se, 1)])
                        sb_tick[h] = se.n
                        tk2 = P.op("vector", lambda v, hh=h, dc2=dcol: v.tensor_tensor(
                            Pcur[:, hh:hh + 1], Pcur[:, hh:hh + 1], dc2,
                            ALU.mult), incs=[(ve, 1)])
                        if c == 3:
                            e12_w[h % 2].r.extend(tk2)

                    def chunks_pair(hA, hB):
                        emit_scalings(hA)
                        chunk_body(hA, 0)
                        emit_scalings(hB)
                        for c in range(3):
                            chunk_body(hA, c + 1)
                            chunk_body(hB, c)
                        chunk_body(hB, 3)

                    for i in range(4):
                        emit_proj(2 * i)
                        emit_proj(2 * i + 1)
                        if i >= 1:
                            chunks_pair(2 * i - 2, 2 * i - 1)
                    chunks_pair(6, 7)
                P.op("vector", lambda v: v.tensor_copy(Dsl[:], Pcur[:]),
                     waits=[(ve, ve.n)], incs=[(ve, 1)])
                if stage == "x1":
                    vend = ve.n
                    for (dst, srcb) in ((dbg_o, o_all), (dbg_q, qh_all),
                                        (dbg_s, S_f), (dbg_p, P_all),
                                        (dbg_u, U_st[3]), (dbg_qn, qn[3]),
                                        (dbg_kn, kn[3]), (dbg_xnf, xnf),
                                        (dbg_v, v_tm)):
                        P.op("sync", lambda s, d=dst, sr=srcb: s.dma_start(
                            out=d[:, :], in_=sr[:]),
                             waits=[(ve, vend)], incs=[(od, 16)])

            P.barrier()
            with ExitStack() as ctxB:
                sbb = lambda n, s, d=F32: ctxB.enter_context(nc.sbuf_tensor(n, s, d))
                ccst = sbb("ccst", [128, 1032])
                gth = sbb("gth", [128, 3 * 1032])
                Sst_f = sbb("Sst_f", [128, H * 128])
                Sst_b = sbb("Sst_b", [128, H * 128], BF16)
                DmS = sbb("DmS", [128, 8])
                mm_scr = sbb("mm_scr", [128, 128])
                qp_b = [sbb(f"qp_b{i}", [128, 512], BF16) for i in range(2)]

                P.op("vector", lambda v: v.tensor_copy(ccst[:, 0:1024], S_f[:]),
                     incs=[(ve, 1)])
                P.op("vector", lambda v: v.tensor_copy(ccst[:, 1024:1032], Dsl[:]),
                     waits=[(ve, ve.n)], incs=[(ve, 1)])
                vw = ve.n
                P.op("sync", lambda s: s.dma_start(out=ccin.ap()[:, :], in_=ccst[:]),
                     waits=[(ve, vw)], incs=[(lcc, 16)])
                lw = lcc.n
                P.op("gpsimd", lambda g: g.collective_compute(
                    "AllGather", ALU.bypass,
                    replica_groups=[[0, 1, 2, 3], [4, 5, 6, 7]],
                    ins=[ccin.ap().opt()], outs=[ccout.ap().opt()]),
                     waits=[(lcc, lw)], incs=[(cc, 1)])
                for j in range(3):
                    P.op("sync", lambda s, jj=j: s.dma_start(
                        out=gth[:, jj * 1032:(jj + 1) * 1032],
                        in_=ccout.ap()[jj * 128:(jj + 1) * 128, :]),
                         waits=[(cc, 1)], incs=[(lcc, 16)])
                lw = lcc.n
                P.op("vector", lambda v: v.memset(Sst_f[:], 0.0),
                     waits=[(lcc, lw), (ld, ld_base)], incs=[(ve, 1)])
                for j in range(3):
                    P.op("vector", lambda v, jj=j: v.tensor_scalar(
                        DmS[:], gth[:, jj * 1032 + 1024:jj * 1032 + 1032],
                        um_s[:, jj:jj + 1], um_s[:, 3 + jj:4 + jj],
                        ALU.mult, ALU.add), incs=[(ve, 1)])
                    dms_tick = ve.n
                    for h in range(H):
                        P.op("vector", lambda v, jj=j, hh=h: v.tensor_scalar(
                            mm_scr[:],
                            gth[:, jj * 1032 + hh * 128:jj * 1032 + hh * 128 + 128],
                            um_s[:, jj:jj + 1], None, ALU.mult), incs=[(ve, 1)])
                        P.op("vector", lambda v, hh=h: v.scalar_tensor_tensor(
                            Sst_f[:, hh * 128:hh * 128 + 128],
                            Sst_f[:, hh * 128:hh * 128 + 128],
                            DmS[:, hh:hh + 1], mm_scr[:],
                            ALU.mult, ALU.add),
                             waits=[(ve, dms_tick)] if h == 0 else [],
                             incs=[(ve, 1)])
                P.op("vector", lambda v: v.tensor_copy(Sst_b[:], Sst_f[:]),
                     incs=[(ve, 1)])
                for h in range(H):
                    for nb in range(4):
                        np_ = nb % 2
                        bw = bankw[np_].take()
                        for j in range(4):
                            n = nb * 4 + j
                            P.op("vector", lambda v, hh=h, nn=n, jj=j,
                                 q2=qp_b[np_]: v.tensor_scalar(
                                     q2[:, jj * 128:jj * 128 + 128],
                                     qh_all[:, hh * NT + nn * 128:
                                            hh * NT + nn * 128 + 128],
                                     P_all[:, hh * 16 + nn:hh * 16 + nn + 1],
                                     None, ALU.mult),
                                 waits=bw if j == 0 else [], incs=[(ve, 1)])
                        vw = ve.n
                        P.op("tensor", lambda te, hh=h, k=np_, q2=qp_b[np_]: te.matmul(
                            pb[k][:, 0:512], Sst_b[:, hh * 128:hh * 128 + 128],
                            q2[:], start=True, stop=True),
                             waits=[(ve, vw)], incs=[(pe, 1)])
                        pw = pe.n
                        oa = o_all[:, h * NT + nb * 512:h * NT + nb * 512 + 512]
                        tk = P.op("vector", lambda v, o2=oa, k=np_: v.tensor_tensor(
                            o2, o2, pb[k][:, 0:512], ALU.add),
                             waits=[(pe, pw)], incs=[(ve, 1)])
                        bankw[np_].r.extend(tk)

            P.barrier()
            with ExitStack() as ctxC:
                sbc = lambda n, s, d=F32: ctxC.enter_context(nc.sbuf_tensor(n, s, d))
                wo_s = sbc("wo_s", [128, DC * D], BF16)
                am_f = sbc("am_f", [128, DC * 512])
                amtm = [sbc(f"amtm{i}", [128, 1024]) for i in range(2)]
                x_st2 = [sbc(f"x_st2_{i}", [128, 1024]) for i in range(2)]
                x1_st = [sbc(f"x1_st{i}", [128, 1024]) for i in range(2)]
                xt_st = [sbc(f"xt_st{i}", [128, 1024]) for i in range(2)]
                xtf_st = [sbc(f"xtf_st{i}", [128, DC * 128], BF16) for i in range(2)]
                amf_w = _Buf()
                amtm_w = [_Buf(), _Buf()]
                x1_w = [_Buf(), _Buf()]
                xt_w = [_Buf(), _Buf()]
                xtf_w = [_Buf(), _Buf()]
                x2_w = [_Buf(), _Buf()]

                s3 = wo_d.rearrange("(hc p) n -> p hc n", p=128)
                P.op("gpsimd", lambda g: g.dma_start(
                    out=wo_s.ap().rearrange("p (hc n) -> p hc n", hc=DC), in_=s3),
                     incs=[(wd, 16)])
                wd_wo = wd.n

                for tt in range(4):
                    for dc in range(DC):
                        bk = dc % 2
                        for h in range(H):
                            P.op("tensor", lambda te, hh=h, d=dc, t4=tt, k=bk: te.matmul(
                                pb[k][:, 0:512],
                                wo_s[:, hh * D + d * 128:hh * D + d * 128 + 128],
                                o_all[:, hh * NT + t4 * 512:hh * NT + t4 * 512 + 512],
                                start=(hh == 0), stop=(hh == H - 1)),
                                 waits=([(wd, wd_wo), (ve, ve.n)]
                                        + bankw[bk].take()) if h == 0 else [],
                                 incs=[(pe, 1)] if h == H - 1 else [])
                        pw = pe.n
                        tk = P.op("vector", lambda v, d=dc, k=bk: v.tensor_copy(
                            am_f[:, d * 512:d * 512 + 512], pb[k][:, 0:512]),
                             waits=[(pe, pw)] + (amf_w.take() if dc == 0 else []),
                             incs=[(ve, 1)])
                        bankw[bk].r.extend(tk)
                    amf_done = ve.n
                    def am_part(sub):
                        sp = sub % 2
                        tok0 = tt * 512 + sub * 128
                        amt, xs2, x1s = amtm[sp], x_st2[sp], x1_st[sp]
                        for dc in range(DC):
                            bk = 2 + dc % 2
                            P.op("tensor", lambda te, d=dc, s4=sub, k=bk: te.transpose(
                                pb[k][:, 0:128],
                                am_f[:, d * 512 + s4 * 128:d * 512 + s4 * 128 + 128],
                                ident_f[:]),
                                 waits=[(ve, amf_done)] + bankw[bk].take(),
                                 incs=[(pe, 1)])
                            pw = pe.n
                            tk = P.op("vector", lambda v, d=dc, k=bk, t=amt: v.tensor_copy(
                                t[:, d * 128:d * 128 + 128], pb[k][:, 0:128]),
                                 waits=[(pe, pw)] + (amtm_w[sp].take()
                                                     if dc == 0 else []),
                                 incs=[(ve, 1)])
                            bankw[bk].r.extend(tk)
                        amf_w.r.append((ve, ve.n))
                        tick = ln_tile(amt[:], amt[:])
                        amtm_w[sp].r.extend(tick)
                        P.op("sync", lambda s, o=tok0, t=xs2: s.dma_start(
                            out=t[:], in_=xsl[o:o + 128, :]),
                             waits=x2_w[sp].take(), incs=[(lx2, 16)])
                        lw = lx2.n
                        tk = P.op("vector", lambda v, t=x1s, u=xs2, w2=amt: v.tensor_tensor(
                            t[:], u[:], w2[:], ALU.add),
                             waits=[(lx2, lw), (ve, ve.n)] + x1_w[sp].take(),
                             incs=[(ve, 1)])
                        x2_w[sp].r.extend(tk)
                        amtm_w[sp].r.extend(tk)
                        vw = ve.n
                        P.op("sync", lambda s, o=tok0, t=x1s: s.dma_start(
                            out=x1sp[o:o + 128, :], in_=t[:]),
                             waits=[(ve, vw)], incs=[(ls1, 16)])
                        x1_w[sp].r.append((ls1, ls1.n))
                        return vw

                    def xt_part(sub, x1vw):
                        sp = sub % 2
                        tok0 = tt * 512 + sub * 128
                        x1s, xts, xtf = x1_st[sp], xt_st[sp], xtf_st[sp]
                        P.op("scalar", lambda s, t=xts, u=x1s: s.copy(t[:], u[:]),
                             waits=[(ve, x1vw)] + xt_w[sp].take(), incs=[(se, 1)])
                        x1_w[sp].r.append((se, se.n))
                        tick = ln_tile(xts[:], xts[:], extra_waits=[(se, se.n)])
                        for dc in range(DC):
                            bk = 2 + dc % 2
                            P.op("tensor", lambda te, d=dc, k=bk, t=xts: te.transpose(
                                pb[k][:, 0:128], t[:, d * 128:d * 128 + 128],
                                ident_f[:]),
                                 waits=[(ve, ve.n)] + bankw[bk].take(),
                                 incs=[(pe, 1)])
                            pw = pe.n
                            tk = P.op("vector", lambda v, d=dc, k=bk, t=xtf: v.tensor_copy(
                                t[:, d * 128:d * 128 + 128], pb[k][:, 0:128]),
                                 waits=[(pe, pw)] + (xtf_w[sp].take()
                                                     if dc == 0 else []),
                                 incs=[(ve, 1)])
                            bankw[bk].r.extend(tk)
                            xt_w[sp].r.extend(tk)
                        vw = ve.n
                        P.op("sync", lambda s, o=tok0, t=xtf: s.dma_start(
                            out=xtsp[:, :, o:o + 128],
                            in_=t.ap().rearrange("p (dc n) -> p dc n", dc=DC)),
                             waits=[(ve, vw)], incs=[(lxt, 16)])
                        xtf_w[sp].r.append((lxt, lxt.n))

                    x1vws = {}
                    for sub in range(4):
                        x1vws[sub] = am_part(sub)
                        if sub >= 1:
                            xt_part(sub - 1, x1vws[sub - 1])
                    xt_part(3, x1vws[3])

        P.barrier()
        if stage == "x1":
            with ExitStack() as ctxX:
                xo = ctxX.enter_context(nc.sbuf_tensor("xo", [128, 1024], F32))
                first = [(ls1, ls1.n)]
                for t in range(16):
                    P.op("sync", lambda s, o=t * 128: s.dma_start(
                        out=xo[:], in_=x1sp[o:o + 128, :]),
                         waits=[(od, od.n)] + first, incs=[(lr1, 16)])
                    first = []
                    lw = lr1.n
                    P.op("sync", lambda s, o=t * 128: s.dma_start(
                        out=out_d[o:o + 128, :], in_=xo[:]),
                         waits=[(lr1, lw)], incs=[(od, 16)])
                P.op("sync", lambda s: s.nop(), waits=[(od, od.n)])
            P.realize(nc)
            return nc

        with ExitStack() as ctxD:
            sbd = lambda n, s, d=F32: ctxD.enter_context(nc.sbuf_tensor(n, s, d))
            xt_half = sbd("xt_half", [128, DC * HALF], BF16)
            h1s = sbd("h1s", [128, IC * HALF], BF16)
            y_f = sbd("y_f", [128, DC * HALF])
            wA = sbd("wA", [128, 16 * 1024], BF16)
            wB = sbd("wB", [128, 16 * 1024], BF16)
            wC = sbd("wC", [128, 16 * 1024], BF16)
            cbc = sbd("cbc", [128, HALF])
            gwT_s = sbd("gwT_s", [128, DC * 8], BF16)
            c_tok = sbd("c_tok", [128, 16 * 8])
            c_fm = sbd("c_fm", [8, HALF])
            sm_scr = sbd("sm_scr", [128, 8])
            h3c = sbd("h3c", [128, 512], BF16)
            c_row = cbc
            ytm = [sbd(f"ytm{i}", [128, 1024]) for i in range(2)]
            x1r = sbd("x1r", [128, 1024])
            wbuf = [wA, wB, wC]
            wb_w = [_Buf(), _Buf(), _Buf()]
            cfence = [(ls1, ls1.n), (lxt, lxt.n), (lx2, lx2.n)]
            wb_w[0].r += cfence
            wb_w[1].r += cfence
            wb_w[2].r += cfence
            xth_w = _Buf()
            h1_w = _Buf()
            y_w = _Buf()
            cbc_w = _Buf()
            crow_w = _Buf()
            ctok_w = _Buf()
            cfm_w = _Buf()
            ytm_w = [_Buf(), _Buf()]
            x1r_w = _Buf()
            outst_w = _Buf()

            P.op("gpsimd", lambda g: g.dma_start(
                out=gwT_s.ap().rearrange("p (dc n) -> p dc n", dc=DC),
                in_=gwT_d.rearrange("(dc p) n -> p dc n", p=128)),
                 waits=list(cfence), incs=[(wd, 16)])
            wd_gw = wd.n

            e_order = [8] + list(range(8))

            for half in range(2):
                htok0 = half * HALF
                P.op("sync", lambda s, o=htok0: s.dma_start(
                    out=xt_half.ap().rearrange("p (dc n) -> p dc n", dc=DC),
                    in_=xtsp[:, :, o:o + HALF]),
                     waits=xth_w.take() + list(cfence), incs=[(lxh, 16)])
                xt_done_ld = lxh.n

                for t8 in range(8):
                    bk = t8 % 2
                    for dc in range(DC):
                        P.op("tensor", lambda te, t=t8, d=dc, k=bk: te.matmul(
                            pb[k][:, 0:8],
                            xt_half[:, d * HALF + t * 128:d * HALF + t * 128 + 128],
                            gwT_s[:, d * 8:d * 8 + 8],
                            start=(d == 0), stop=(d == DC - 1)),
                             waits=([(lxh, xt_done_ld), (wd, wd_gw)]
                                    + bankw[bk].take()) if dc == 0 else [],
                             incs=[(pe, 1)] if dc == DC - 1 else [])
                    pw = pe.n
                    mx, sμ, m1c, m2c = col(), col(), col(), col()
                    P.op("vector", lambda v, k=bk, m=mx: v.reduce_max(
                        m, pb[k][:, 0:8], axis=AXX),
                         waits=[(pe, pw)] + ctok_w.take(), incs=[(ve, 1)])
                    nmx = col()
                    P.op("vector", lambda v, m=mx, nm=nmx: v.tensor_scalar(
                        nm, m, -1.0, None, ALU.mult),
                         waits=[(ve, ve.n)], incs=[(ve, 1)])
                    vw = ve.n
                    P.op("scalar", lambda s, k=bk, nm=nmx: s.activation(
                        out=sm_scr[:], in_=pb[k][:, 0:8], func=AF.Exp, bias=nm),
                         waits=[(ve, vw)], incs=[(se, 1)])
                    sw = se.n
                    tk = P.op("vector", lambda v, m=sμ: v.reduce_sum(
                        m, sm_scr[:], axis=AXX), waits=[(se, sw)], incs=[(ve, 1)])
                    bankw[bk].r.extend(tk)
                    P.op("vector", lambda v, m=sμ: v.reciprocal(m, m),
                         waits=[(ve, ve.n)], incs=[(ve, 1)])
                    P.op("vector", lambda v, m=sμ: v.tensor_scalar(
                        sm_scr[:], sm_scr[:], m, None, ALU.mult),
                         waits=[(ve, ve.n)], incs=[(ve, 1)])
                    ctk = c_tok[:, t8 * 8:t8 * 8 + 8]
                    P.op("vector", lambda v, ct=ctk: v.tensor_tensor(
                        ct, sm_scr[:], gbias_s[:], ALU.add),
                         waits=[(ve, ve.n), (ld, ld_base)], incs=[(ve, 1)])
                    P.op("vector", lambda v, ct=ctk, m=m1c: v.reduce_max(
                        m, ct, axis=AXX), waits=[(ve, ve.n)], incs=[(ve, 1)])
                    P.op("vector", lambda v, ct=ctk, m=m1c: v.tensor_scalar(
                        sm_scr[:], ct, m, None, ALU.is_ge),
                         waits=[(ve, ve.n)], incs=[(ve, 1)])
                    P.op("vector", lambda v, ct=ctk: v.scalar_tensor_tensor(
                        sm_scr[:], sm_scr[:], -1e30, ct, ALU.mult, ALU.add),
                         waits=[(ve, ve.n)], incs=[(ve, 1)])
                    P.op("vector", lambda v, m=m2c: v.reduce_max(
                        m, sm_scr[:], axis=AXX),
                         waits=[(ve, ve.n)], incs=[(ve, 1)])
                    P.op("vector", lambda v, ct=ctk, m=m2c: v.tensor_scalar(
                        sm_scr[:], ct, m, None, ALU.is_ge),
                         waits=[(ve, ve.n)], incs=[(ve, 1)])
                    P.op("vector", lambda v, ct=ctk: v.tensor_tensor(
                        ct, ct, gbias_s[:], ALU.subtract),
                         waits=[(ve, ve.n)], incs=[(ve, 1)])
                    P.op("vector", lambda v, ct=ctk: v.tensor_tensor(
                        ct, ct, sm_scr[:], ALU.mult),
                         waits=[(ve, ve.n)], incs=[(ve, 1)])
                    vw = ve.n
                    P.op("tensor", lambda te, ct=ctk, k=2 + bk: te.transpose(
                        pb[k][0:8, 0:128], ct, ident_f[:]),
                         waits=[(ve, vw)] + bankw[2 + bk].take(), incs=[(pe, 1)])
                    pw = pe.n
                    tk = P.op("vector", lambda v, t=t8, k=2 + bk: v.tensor_copy(
                        c_fm[0:8, t * 128:t * 128 + 128],
                        pb[k][0:8, 0:128]),
                         waits=[(pe, pw)] + (cfm_w.take() if t8 == 0 else []),
                         incs=[(ve, 1)])
                    bankw[2 + bk].r.extend(tk)
                    ctok_w.r.extend(tk)
                cfm_done = ve.n

                lctr = [0]

                def wstream(src4, parity, d4):
                    P.op("gpsimd", lambda g, s=src4, d=d4: g.dma_start(
                        out=d, in_=s),
                         waits=wb_w[parity].take(), incs=[(wdp[parity], 16)])
                    return wdp[parity].n

                for ei, e in enumerate(e_order):
                    par1 = lctr[0] % 3
                    wd1s = wdp[par1]
                    wd1 = wstream(
                        w1T_d[e].rearrange("(dc p) (ic n) -> p dc ic n",
                                           p=128, n=128),
                        par1,
                        wbuf[par1].ap().rearrange("p (dc ic n) -> p dc ic n",
                                                  dc=DC, ic=IC))
                    lctr[0] += 1
                    par3 = lctr[0] % 3
                    wd3s = wdp[par3]
                    wd3 = wstream(
                        w3T_d[e].rearrange("(dc p) (ic n) -> p dc ic n",
                                           p=128, n=128),
                        par3,
                        wbuf[par3].ap().rearrange("p (dc ic n) -> p dc ic n",
                                                  dc=DC, ic=IC))
                    lctr[0] += 1
                    par2 = lctr[0] % 3
                    wd2s = wdp[par2]
                    wd2 = wstream(
                        w2T_d[e].rearrange("(ic p) (dc n) -> p ic dc n",
                                           p=128, n=128),
                        par2,
                        wbuf[par2].ap().rearrange("p (ic dc n) -> p ic dc n",
                                                  ic=IC, dc=DC))
                    lctr[0] += 1

                    if e != 8:
                        P.op("sync", lambda s, ee=e: s.dma_start(
                            out=c_row[0:1, :],
                            in_=c_fm[ee:ee + 1, 0:HALF]),
                             waits=[(ve, cfm_done)] + crow_w.take(),
                             incs=[(lcr, 16)])
                        wd_cr = lcr.n
                        for t2 in range(2):
                            tkp = P.op("tensor", lambda te, t=t2: te.matmul(
                                pb[4][:, 0:512], ones1_f[0:1, :],
                                c_row[0:1, t * 512:t * 512 + 512],
                                start=True, stop=True),
                                 waits=[(lcr, wd_cr)] + bankw[4].take(),
                                 incs=[(pe, 1)])
                            crow_w.r.extend(tkp)
                            pw = pe.n
                            tk = P.op("vector", lambda v, t=t2: v.tensor_copy(
                                cbc[:, t * 512:t * 512 + 512], pb[4][:, 0:512]),
                                 waits=[(pe, pw)] + (cbc_w.take()
                                                     if t2 == 0 else []),
                                 incs=[(ve, 1)])
                            bankw[4].r.extend(tk)
                        cbc_done = ve.n
                    else:
                        cbc_done = 0

                    for ic in range(IC):
                        for t2 in range(2):
                            bk = (ic * 2 + t2) % 2
                            for dc in range(DC):
                                P.op("tensor", lambda te, i=ic, t=t2, d=dc, k=bk,
                                     p=par1: te.matmul(
                                         pb[k][:, 0:512],
                                         wbuf[p][:, d * 2048 + i * 128:
                                                 d * 2048 + i * 128 + 128],
                                         xt_half[:, d * HALF + t * 512:
                                                 d * HALF + t * 512 + 512],
                                         start=(d == 0), stop=(d == DC - 1)),
                                     waits=([(wd1s, wd1), (lxh, xt_done_ld)]
                                            + bankw[bk].take()) if dc == 0 else [],
                                     incs=[(pe, 1)] if dc == DC - 1 else [])
                            pw = pe.n
                            tk = P.op("scalar", lambda s, i=ic, t=t2, k=bk: s.activation(
                                out=h1s[:, i * HALF + t * 512:i * HALF + t * 512 + 512],
                                in_=pb[k][:, 0:512], func=AF.Silu),
                                 waits=[(pe, pw)] + (h1_w.take()
                                                     if ic == 0 and t2 == 0 else []),
                                 incs=[(se, 1)])
                            bankw[bk].r.extend(tk)
                            wb_w[par1].r.extend(tk)
                    for ic in range(IC):
                        for t2 in range(2):
                            bk = (ic * 2 + t2) % 2
                            for dc in range(DC):
                                P.op("tensor", lambda te, i=ic, t=t2, d=dc, k=bk,
                                     p=par3: te.matmul(
                                         pb[k][:, 0:512],
                                         wbuf[p][:, d * 2048 + i * 128:
                                                 d * 2048 + i * 128 + 128],
                                         xt_half[:, d * HALF + t * 512:
                                                 d * HALF + t * 512 + 512],
                                         start=(d == 0), stop=(d == DC - 1)),
                                     waits=([(wd3s, wd3)] + bankw[bk].take())
                                     if dc == 0 else [],
                                     incs=[(pe, 1)] if dc == DC - 1 else [])
                            pw = pe.n
                            hsl = h1s[:, ic * HALF + t2 * 512:
                                      ic * HALF + t2 * 512 + 512]
                            if e != 8:
                                tkc = P.op("vector", lambda v, t=t2, k=bk: v.tensor_tensor(
                                    h3c[:], pb[k][:, 0:512],
                                    cbc[:, t * 512:t * 512 + 512], ALU.mult),
                                     waits=[(pe, pw), (ve, cbc_done), (se, se.n)],
                                     incs=[(ve, 1)])
                                crow_w.r.extend(tkc)
                                tk = P.op("vector", lambda v, hs=hsl: v.tensor_tensor(
                                    hs, h3c[:], hs, ALU.mult), incs=[(ve, 1)])
                            else:
                                tk = P.op("vector", lambda v, hs=hsl, k=bk: v.tensor_tensor(
                                    hs, pb[k][:, 0:512], hs, ALU.mult),
                                     waits=[(pe, pw), (se, se.n)], incs=[(ve, 1)])
                            bankw[bk].r.extend(tk)
                            wb_w[par3].r.extend(tk)
                            if e != 8:
                                cbc_w.r.extend(tk)
                    ve_p2 = ve.n
                    if ei == 0:
                        ydc_tick = {}
                    for dc in range(DC):
                        for t2 in range(2):
                            bk = (dc * 2 + t2) % 2
                            for ic in range(IC):
                                P.op("tensor", lambda te, i=ic, t=t2, d=dc, k=bk,
                                     p=par2: te.matmul(
                                         pb[k][:, 0:512],
                                         wbuf[p][:, i * 1024 + d * 128:
                                                 i * 1024 + d * 128 + 128],
                                         h1s[:, i * HALF + t * 512:
                                             i * HALF + t * 512 + 512],
                                         start=(i == 0), stop=(i == IC - 1)),
                                     waits=([(wd2s, wd2), (ve, ve_p2)]
                                            + bankw[bk].take()) if ic == 0 else [],
                                     incs=[(pe, 1)] if ic == IC - 1 else [])
                            pw = pe.n
                            ysl = y_f[:, dc * HALF + t2 * 512:
                                      dc * HALF + t2 * 512 + 512]
                            if ei == 0:
                                tk = P.op("vector", lambda v, ys=ysl, k=bk: v.tensor_copy(
                                    ys, pb[k][:, 0:512]),
                                     waits=[(pe, pw)] + (y_w.take()
                                                         if dc == 0 and t2 == 0 else []),
                                     incs=[(ve, 1)])
                            else:
                                tk = P.op("vector", lambda v, ys=ysl, k=bk: v.tensor_tensor(
                                    ys, ys, pb[k][:, 0:512], ALU.add),
                                     waits=[(pe, pw)], incs=[(ve, 1)])
                            bankw[bk].r.extend(tk)
                            wb_w[par2].r.extend(tk)
                            h1_w.r.extend(tk)
                            ydc_tick[dc] = ve.n
                y_done = ve.n

                for sub in range(8):
                    sp = sub % 2
                    tok0 = htok0 + sub * 128
                    yt, xr = ytm[sp], x1r
                    for dc in range(DC):
                        bk = 2 + dc % 2
                        P.op("tensor", lambda te, d=dc, s4=sub, k=bk: te.transpose(
                            pb[k][:, 0:128],
                            y_f[:, d * HALF + s4 * 128:d * HALF + s4 * 128 + 128],
                            ident_f[:]),
                             waits=[(ve, ydc_tick[dc])] + bankw[bk].take(),
                             incs=[(pe, 1)])
                        pw = pe.n
                        tk = P.op("vector", lambda v, d=dc, k=bk, t=yt: v.tensor_copy(
                            t[:, d * 128:d * 128 + 128], pb[k][:, 0:128]),
                             waits=[(pe, pw)] + (ytm_w[sp].take()
                                                 if dc == 0 else []),
                             incs=[(ve, 1)])
                        bankw[bk].r.extend(tk)
                        y_w.r.extend(tk)
                    tick = ln_tile(yt[:], yt[:])
                    ytm_w[sp].r.extend(tick)
                    P.op("sync", lambda s, o=tok0, t=xr: s.dma_start(
                        out=t[:], in_=x1sp[o:o + 128, :]),
                         waits=x1r_w.take() + [(ls1, ls1.n)], incs=[(lr1, 16)])
                    lw = lr1.n
                    tk = P.op("vector", lambda v, t=xr, u=yt: v.tensor_tensor(
                        t[:], t[:], u[:], ALU.add),
                         waits=[(lr1, lw), (ve, ve.n)], incs=[(ve, 1)])
                    ytm_w[sp].r.extend(tk)
                    vw = ve.n
                    P.op("sync", lambda s, o=tok0, t=xr: s.dma_start(
                        out=out_d[o:o + 128, :], in_=t[:]),
                         waits=[(ve, vw)], incs=[(od, 16)])
                    x1r_w.r.append((od, od.n))
                xth_w.r.append((pe, pe.n))

            if stage == "fulldbg":
                vend, pend, send = ve.n, pe.n, se.n
                for (dst, srcb) in ((f_y, y_f), (f_cfm, c_fm), (f_xth, xt_half),
                                    (f_h1s, h1s), (f_cbc, cbc)):
                    P.op("sync", lambda s, d=dst, sr=srcb: s.dma_start(
                        out=d[:, :], in_=sr.ap()),
                         waits=[(ve, vend), (pe, pend), (se, send)],
                         incs=[(od, 16)])
            P.op("sync", lambda s: s.nop(), waits=[(od, od.n)])

        P.realize(nc)
    return nc


_NC_CACHE = {}


def _get_nc(stage="full"):
    if stage not in _NC_CACHE:
        _NC_CACHE[stage] = build_program(stage)
    return _NC_CACHE[stage]


def stage_inputs(inputs):
    f = lambda a: np.ascontiguousarray(np.asarray(a, dtype=np.float32))
    x = f(inputs["L0_attn_x"])
    wq = f(inputs["L0_attn_Wq"])
    wk = f(inputs["L0_attn_Wk"])
    wv = f(inputs["L0_attn_Wv"])
    wg = f(inputs["L0_attn_Wg"])
    wo = f(inputs["L0_attn_Wo"])
    gw = f(inputs["L0_ffn_gate_w"])
    gb = f(inputs["L0_ffn_gate_bias"])
    w1 = f(inputs["L0_ffn_w1"])
    w2 = f(inputs["L0_ffn_w2"])
    w3 = f(inputs["L0_ffn_w3"])
    sw1 = f(inputs["L0_ffn_sw1"])
    sw2 = f(inputs["L0_ffn_sw2"])
    sw3 = f(inputs["L0_ffn_sw3"])

    w1T = np.ascontiguousarray(
        np.concatenate([w1.transpose(0, 2, 1), sw1.T[None]], axis=0))
    w3T = np.ascontiguousarray(
        np.concatenate([w3.transpose(0, 2, 1), sw3.T[None]], axis=0))
    w2T = np.ascontiguousarray(
        np.concatenate([w2.transpose(0, 2, 1), sw2.T[None]], axis=0))
    gwT = np.ascontiguousarray(gw.T)
    gbias = np.ascontiguousarray(np.tile(gb[None, :], (128, 1)))

    in_maps = []
    for c in range(NCORE):
        b, s = c // 4, c % 4
        u = np.zeros((128, 6), dtype=np.float32)
        for j in range(3):
            if j < s:
                u[:, j] = 1.0
            else:
                u[:, 3 + j] = 1.0
        u[:, 3:6] = 1.0 - u[:, 0:3]
        in_maps.append({
            "xsl": np.ascontiguousarray(x[b, s * NT:(s + 1) * NT, :]),
            "wq": wq, "wk": wk, "wv": wv, "wg": wg, "wo": wo,
            "gwT": gwT, "gbias": gbias,
            "w1T": w1T, "w3T": w3T, "w2T": w2T,
            "um": np.ascontiguousarray(u),
        })
    return in_maps


def kernel(**inputs):
    in_maps = stage_inputs(inputs)
    nc = _get_nc("full")
    res = run_bass_kernel_spmd(nc, in_maps, core_ids=list(range(NCORE)))
    out = np.empty((2, 8192, D), dtype=np.float32)
    for c in range(NCORE):
        b, s = c // 4, c % 4
        out[b, s * NT:(s + 1) * NT, :] = res.results[c]["out"]
    return out


# version 24
# speedup vs baseline: 1.0131x; 1.0131x over previous
import numpy as np
from contextlib import ExitStack

import concourse.bass as bass
import concourse.mybir as mybir
from concourse.bass_utils import run_bass_kernel_spmd

F32 = mybir.dt.float32
BF16 = mybir.dt.bfloat16
I8 = mybir.dt.int8
AF = mybir.ActivationFunctionType
ALU = mybir.AluOpType
AXX = mybir.AxisListType.X

NCORE = 8
NT = 2048
D = 1024
DC = 8
H = 8
DK = 128
DV = 128
NG = 4
GT = 512
CH = 128
II = 2048
IC = 16
NE = 9
HALF = 1024


class _Sem:
    __slots__ = ("name", "n", "h")
    def __init__(self, name):
        self.name, self.n, self.h = name, 0, None


class _Buf:
    __slots__ = ("r",)
    def __init__(self):
        self.r = []
    def take(self):
        r, self.r = self.r, []
        return r


class _Plan:
    def __init__(self):
        self.sems = []
        self.blocks = [[]]
        self.self_sems = {}

    def sem(self, name):
        s = _Sem(name)
        self.sems.append(s)
        return s

    def barrier(self):
        if self.blocks[-1]:
            self.blocks.append([])

    def op(self, eng, fn, waits=(), incs=()):
        ss = self.self_sems.get(eng)
        waits = list(waits)
        incs = list(incs)
        if ss is not None:
            waits.append((ss, ss.n))
            if not any(s is ss for s, _ in incs):
                incs.append((ss, 1))
        wd = {}
        for s, v in waits:
            if v <= 0:
                continue
            assert v <= s.n, f"wait-ahead on {s.name}: {v} > {s.n}"
            wd[s] = max(wd.get(s, 0), v)
        self.blocks[-1].append((eng, fn, tuple(wd.items()), tuple(incs)))
        out = []
        for s, k in incs:
            s.n += k
            out.append((s, s.n))
        return out

    def realize(self, nc):
        with ExitStack() as es:
            for s in self.sems:
                s.h = es.enter_context(nc.semaphore(s.name))
            for ops in self.blocks:
                engs = []
                for o in ops:
                    if o[0] not in engs:
                        engs.append(o[0])
                with nc.Block() as blk:
                    for en in engs:
                        eng_ops = [o for o in ops if o[0] == en]

                        def body(eng, eng_ops=eng_ops):
                            for _, fn, waits, incs in eng_ops:
                                for s, v in waits:
                                    eng.wait_ge(s.h, v)
                                ins = fn(eng)
                                for s, k in incs:
                                    ins = ins.then_inc(s.h, k)
                        getattr(blk, en)(body)


def build_program(stage="full"):
    nc = bass.Bass("TRN2", target_bir_lowering=False)
    P = _Plan()
    ld = P.sem("ld")
    wd = P.sem("wd")
    lx = [P.sem("lxa"), P.sem("lxb")]
    lcc = P.sem("lcc")
    lx2 = [P.sem("lx2a"), P.sem("lx2b")]
    ls1 = [P.sem("ls1a"), P.sem("ls1b")]
    lxt = [P.sem("lxta"), P.sem("lxtb")]
    lxh = P.sem("lxh")
    lr1 = P.sem("lr1")
    lcr = P.sem("lcr")
    wdp = [P.sem("wda"), P.sem("wdb"), P.sem("wdc")]
    pe = P.sem("pe")
    ve = P.sem("ve")
    se = P.sem("se")
    ge = P.sem("ge")
    cc = P.sem("cc")
    od = P.sem("od")
    P.self_sems = {"vector": ve, "scalar": se}

    xsl = nc.dram_tensor("xsl", [NT, D], F32, kind="ExternalInput").ap()
    wq_d = nc.dram_tensor("wq", [D, D], F32, kind="ExternalInput").ap()
    wk_d = nc.dram_tensor("wk", [D, D], F32, kind="ExternalInput").ap()
    wv_d = nc.dram_tensor("wv", [D, D], F32, kind="ExternalInput").ap()
    wg_d = nc.dram_tensor("wg", [D, D], F32, kind="ExternalInput").ap()
    wo_d = nc.dram_tensor("wo", [D, D], F32, kind="ExternalInput").ap()
    gwT_d = nc.dram_tensor("gwT", [D, H], F32, kind="ExternalInput").ap()
    gbias_d = nc.dram_tensor("gbias", [128, 8], F32, kind="ExternalInput").ap()
    w1T_d = nc.dram_tensor("w1T", [NE, D, II], F32, kind="ExternalInput").ap()
    w3T_d = nc.dram_tensor("w3T", [NE, D, II], F32, kind="ExternalInput").ap()
    w2T_d = nc.dram_tensor("w2T", [NE, II, D], F32, kind="ExternalInput").ap()
    um_d = nc.dram_tensor("um", [128, 6], F32, kind="ExternalInput").ap()
    out_d = nc.dram_tensor("out", [NT, D], F32, kind="ExternalOutput").ap()

    if stage == "fulldbg":
        f_y = nc.dram_tensor("f_y", [128, DC * HALF], F32, kind="ExternalOutput").ap()
        f_cfm = nc.dram_tensor("f_cfm", [8, NT], F32, kind="ExternalOutput").ap()
        f_xth = nc.dram_tensor("f_xth", [128, DC * HALF], BF16, kind="ExternalOutput").ap()
        f_h1s = nc.dram_tensor("f_h1s", [128, IC * HALF], BF16, kind="ExternalOutput").ap()
        f_cbc = nc.dram_tensor("f_cbc", [128, HALF], F32, kind="ExternalOutput").ap()
    if stage == "x1":
        dbg_o = nc.dram_tensor("dbg_o", [128, H * NT], BF16, kind="ExternalOutput").ap()
        dbg_q = nc.dram_tensor("dbg_q", [128, H * NT], BF16, kind="ExternalOutput").ap()
        dbg_s = nc.dram_tensor("dbg_s", [128, H * 128], F32, kind="ExternalOutput").ap()
        dbg_p = nc.dram_tensor("dbg_p", [128, H * 16], F32, kind="ExternalOutput").ap()
        dbg_u = nc.dram_tensor("dbg_u", [128, GT], F32, kind="ExternalOutput").ap()
        dbg_qn = nc.dram_tensor("dbg_qn", [128, GT], BF16, kind="ExternalOutput").ap()
        dbg_kn = nc.dram_tensor("dbg_kn", [128, GT], BF16, kind="ExternalOutput").ap()
        dbg_xnf = nc.dram_tensor("dbg_xnf", [128, DC * GT], BF16, kind="ExternalOutput").ap()
        dbg_v = nc.dram_tensor("dbg_v", [128, 4 * 1024], BF16, kind="ExternalOutput").ap()
    x1sp = nc.dram_tensor("x1sp", [NT, D], F32).ap()
    xtsp = nc.dram_tensor("xtsp", [128, DC, NT], BF16).ap()
    ccin = nc.dram_tensor("ccin", [128, 1032], F32)
    ccout = nc.dram_tensor("ccout", [512, 1032], F32)

    es = ExitStack()
    sb = lambda n, s, d=F32: es.enter_context(nc.sbuf_tensor(n, s, d))

    with es:
        ident_f = sb("ident_f", [128, 128])
        ident_b = sb("ident_b", [128, 128], BF16)
        mask_f = sb("mask_f", [128, 128])
        ones_f = sb("ones_f", [128, 128])
        zeros_b = sb("zeros_b", [128, 128], BF16)
        z512 = sb("z512", [128, 512])
        zcol = sb("zcol", [128, 1])
        epscol = sb("epscol", [128, 1])
        ones1_b = sb("ones1_b", [1, 128], BF16)
        ones1_f = sb("ones1_f", [1, 128])
        onesk_b = sb("onesk_b", [128, 1], BF16)
        gbias_s = sb("gbias_s", [128, 8])
        um_s = sb("um_s", [128, 6])
        sqdump = sb("sqdump", [128, 1024], BF16)
        cols = sb("cols", [128, 16])

        pb = [es.enter_context(nc.psum_tensor(f"pb{i}", [128, 512], F32))
              for i in range(7)]
        pbt_b = es.enter_context(nc.psum_tensor("pbt_b", [128, 128], BF16))
        bankw = [_Buf() for _ in range(8)]

        P.op("gpsimd", lambda g: g.memset(ones_f[:], 1.0), incs=[(ge, 1)])

        P.op("gpsimd", lambda g: g.memset(zeros_b[:], 0.0), incs=[(ge, 1)])
        P.op("gpsimd", lambda g: g.memset(z512[:], 0.0), incs=[(ge, 1)])
        P.op("gpsimd", lambda g: g.memset(zcol[:], 0.0), incs=[(ge, 1)])
        P.op("gpsimd", lambda g: g.memset(epscol[:], 1e-5), incs=[(ge, 1)])
        P.op("gpsimd", lambda g: g.memset(ones1_b[:], 1.0), incs=[(ge, 1)])
        P.op("gpsimd", lambda g: g.memset(ones1_f[:], 1.0), incs=[(ge, 1)])
        P.op("gpsimd", lambda g: g.memset(onesk_b[:], 1.0), incs=[(ge, 1)])
        P.op("gpsimd", lambda g: g.affine_select(
            ident_f[:], ones_f[:], [[1, 128]], ALU.is_equal, 0.0,
            channel_multiplier=-1), waits=[(ge, 8)], incs=[(ge, 1)])
        P.op("gpsimd", lambda g: g.affine_select(
            mask_f[:], ones_f[:], [[1, 128]], ALU.is_ge, 0.0,
            channel_multiplier=-1), waits=[(ge, 9)], incs=[(ge, 1)])
        P.op("vector", lambda v: v.tensor_copy(ident_b[:], ident_f[:]),
             waits=[(ge, 10)], incs=[(ve, 1)])
        P.op("vector", lambda v: v.memset(pb[0][64:128, 0:64], 0.0),
             incs=[(ve, 1)])
        P.op("sync", lambda s: s.dma_start(out=gbias_s[:], in_=gbias_d[:, :]),
             incs=[(ld, 16)])
        P.op("sync", lambda s: s.dma_start(out=um_s[:], in_=um_d[:, :]),
             incs=[(ld, 16)])
        ld_base = ld.n

        col_i = [0]
        def col():
            c = col_i[0] % 16
            col_i[0] += 1
            return cols[:, c:c + 1]

        def mk_dma(eng_dst, eng_src):
            return lambda e: e.dma_start(out=eng_dst, in_=eng_src)

        def ln_tile(src, dst, extra_waits=()):
            sc, mc, lc, rc = col(), col(), col(), col()
            P.op("vector", lambda v: v.reduce_sum(sc, src, axis=AXX),
                 waits=list(extra_waits), incs=[(ve, 1)])
            P.op("vector", lambda v: v.tensor_scalar(
                mc, sc, 1.0 / D, None, ALU.mult),
                 waits=[(ve, ve.n)], incs=[(ve, 1)])
            P.op("vector", lambda v: v.tensor_scalar(
                dst, src, mc, None, ALU.subtract),
                 waits=[(ve, ve.n)], incs=[(ve, 1)])
            t0 = ve.n
            P.op("scalar", lambda s: s.activation(
                out=sqdump[:], in_=dst, func=AF.Square, accum_out=lc),
                 waits=[(ve, t0)], incs=[(se, 1)])
            P.op("scalar", lambda s: s.activation(
                out=lc, in_=lc, func=AF.Ln, scale=1.0 / D, bias=epscol[:]),
                 waits=[(se, se.n)], incs=[(se, 1)])
            P.op("scalar", lambda s: s.activation(
                out=rc, in_=lc, func=AF.Exp, scale=-0.5),
                 waits=[(se, se.n)], incs=[(se, 1)])
            t1 = se.n
            tick = P.op("vector", lambda v: v.tensor_scalar(
                dst, dst, rc, None, ALU.mult), waits=[(se, t1)], incs=[(ve, 1)])
            return tick

        with ExitStack() as ctx1:
            sb1 = lambda n, s, d=F32: ctx1.enter_context(nc.sbuf_tensor(n, s, d))
            o_all = sb1("o_all", [128, H * NT], BF16)
            qh_all = sb1("qh_all", [128, H * NT], BF16)
            S_f = sb1("S_f", [128, H * 128])
            S_b = sb1("S_b", [128, H * 128], BF16)
            P_all = sb1("P_all", [128, H * 16])
            Pcur = sb1("Pcur", [128, H])
            Dsl = sb1("Dsl", [128, H])
            o_w = _Buf()
            qh_w = _Buf()

            P.op("vector", lambda v: v.memset(S_f[:], 0.0), incs=[(ve, 1)])
            P.op("vector", lambda v: v.memset(S_b[:], 0.0), incs=[(ve, 1)])
            P.op("vector", lambda v: v.memset(Pcur[:], 1.0), incs=[(ve, 1)])
            sb_tick = {h: 0 for h in range(H)}

            with ExitStack() as ctxA:
                sba = lambda n, s, d=F32: ctxA.enter_context(nc.sbuf_tensor(n, s, d))
                wq_s = sba("wq_s", [128, DC * D], BF16)
                wk_s = sba("wk_s", [128, DC * D], BF16)
                wv_s = sba("wv_s", [128, DC * D], BF16)
                wg_s = sba("wg_s", [128, DC * D], BF16)
                x_st = [sba(f"x_st{i}", [128, 1024]) for i in range(2)]
                xn_st = [sba(f"xn_st{i}", [128, 1024]) for i in range(2)]
                xnf = sba("xnf", [128, DC * GT], BF16)
                v_tm = sba("v_tm", [128, 4 * 1024], BF16)
                qn = [sba(f"qn{i}", [128, GT], BF16) for i in range(4)]
                kn = [sba(f"kn{i}", [128, GT], BF16) for i in range(4)]
                t_st = sba("t_st", [128, GT])
                u_st = sba("u_st", [128, GT])
                U_st = [sba(f"U_st{i}", [128, GT]) for i in range(4)]
                sq_b = sba("sq_b", [128, GT], BF16)
                ssq_s = u_st
                rn_b = sba("rn_b", [1, GT], BF16)
                d1m = sba("d1m", [128, GT])
                e1a = sba("e1a", [128, GT])
                qt_a = [sba(f"qt_{i}", [128, GT], BF16) for i in range(2)]
                kt_a = [sba(f"kt_{i}", [128, GT], BF16) for i in range(2)]
                kh_a = [sba(f"kh_{i}", [128, GT], BF16) for i in range(2)]
                scm = [sba(f"scm{i}", [128, 8]) for i in range(2)]
                am_p = [sba(f"am_{i}", [128, 128], BF16) for i in range(2)]
                khtm_p = [sba(f"khtm_{i}", [128, 128], BF16) for i in range(2)]
                e12_w = [_Buf(), _Buf(), _Buf(), _Buf()]
                x_w = [_Buf(), _Buf()]
                xn_w = [_Buf(), _Buf()]
                xnf_w = _Buf()
                vtm_w = _Buf()
                head_w = _Buf()

                def wload(dst, src):
                    s3 = src.rearrange("(dc p) n -> p dc n", p=128)
                    return P.op("gpsimd",
                                lambda g, d=dst, s=s3: g.dma_start(out=d.ap().rearrange(
                                    "p (dc n) -> p dc n", dc=DC), in_=s),
                                incs=[(wd, 16)])
                wload(wq_s, wq_d)
                wload(wk_s, wk_d)
                wload(wv_s, wv_d)
                wload(wg_s, wg_d)
                wd_attn = wd.n

                for g in range(NG):
                    for t in range(4):
                        bi = t % 2
                        tok0 = g * GT + t * 128
                        P.op("sync", lambda s, b=bi, o=tok0: s.dma_start(
                            out=x_st[b][:], in_=xsl[o:o + 128, :]),
                             waits=x_w[bi].take() + xn_w[bi].take(),
                             incs=[(lx[bi], 16)])
                        lw = lx[bi].n
                        tick = ln_tile(x_st[bi][:], xn_st[bi][:],
                                       extra_waits=[(lx[bi], lw)])
                        x_w[bi].r.extend(tick)
                        for dc in range(DC):
                            bk = dc % 2
                            P.op("tensor", lambda te, b=bi, d=dc, k=bk: te.transpose(
                                pb[k][:, 0:128], xn_st[b][:, d * 128:(d + 1) * 128],
                                ident_f[:]),
                                 waits=[(ve, ve.n)] + bankw[bk].take(),
                                 incs=[(pe, 1)])
                            pw = pe.n
                            tk = P.op("vector", lambda v, b=bi, d=dc, k=bk, tt=t: v.tensor_copy(
                                xnf[:, d * GT + tt * 128:d * GT + tt * 128 + 128],
                                pb[k][:, 0:128]),
                                 waits=[(pe, pw)] + xnf_w.take(), incs=[(ve, 1)])
                            bankw[bk].r.extend(tk)
                            xn_w[bi].r.extend(tk)
                    xnf_done = ve.n

                    for t in range(4):
                        for hf in range(2):
                            bk = 2 + hf
                            for dc in range(DC):
                                P.op("tensor", lambda te, tt=t, h2=hf, d=dc, k=bk: te.matmul(
                                    pb[k][:, 0:512],
                                    xnf[:, d * GT + tt * 128:d * GT + tt * 128 + 128],
                                    wv_s[:, d * D + h2 * 512:d * D + h2 * 512 + 512],
                                    start=(d == 0), stop=(d == DC - 1)),
                                     waits=([(ve, xnf_done), (wd, wd_attn)]
                                            + bankw[bk].take()) if dc == 0 else [],
                                     incs=[(pe, 1)] if dc == DC - 1 else [])
                            pw = pe.n
                            tk = P.op("vector", lambda v, tt=t, h2=hf, k=bk: v.tensor_copy(
                                v_tm[:, tt * 1024 + h2 * 512:tt * 1024 + h2 * 512 + 512],
                                pb[k][:, 0:512]),
                                 waits=[(pe, pw)] + vtm_w.take(), incs=[(ve, 1)])
                            bankw[bk].r.extend(tk)

                    U_done = {}
                    scale_done = {}

                    def emit_proj(h):
                        par = h % 4
                        for dc in range(DC):
                            P.op("tensor", lambda te, d=dc, hh=h: te.matmul(
                                pb[4][:, 0:512],
                                wg_s[:, d * D + hh * 128:d * D + hh * 128 + 128],
                                xnf[:, d * GT:(d + 1) * GT],
                                start=(d == 0), stop=(d == DC - 1)),
                                 waits=([(ve, xnf_done)] + bankw[4].take())
                                 if dc == 0 else [],
                                 incs=[(pe, 1)] if dc == DC - 1 else [])
                        pw = pe.n
                        P.op("scalar", lambda s: s.activation(
                            out=t_st[:], in_=pb[4][:, 0:512], func=AF.Exp,
                            scale=-1.0), waits=[(pe, pw)], incs=[(se, 1)])
                        tk = P.op("scalar", lambda s: s.activation(
                            out=u_st[:], in_=t_st[:], func=AF.Ln, bias=1.0),
                             incs=[(se, 1)])
                        bankw[4].r.extend(tk)
                        sw = se.n
                        P.op("vector", lambda v, p=par: v.tensor_tensor_scan(
                            U_st[p][:], u_st[:], z512[:], 0.0, ALU.add, ALU.add),
                             waits=[(se, sw)], incs=[(ve, 1)])
                        U_done[h] = ve.n

                        for (wmat, dstn, bq) in ((wq_s, qn[par], 2 + (h % 2)),
                                                 (wk_s, kn[par], 3 - (h % 2))):
                            for dc in range(DC):
                                P.op("tensor", lambda te, d=dc, hh=h, w=wmat, k=bq: te.matmul(
                                    pb[k][:, 0:512],
                                    w[:, d * D + hh * 128:d * D + hh * 128 + 128],
                                    xnf[:, d * GT:(d + 1) * GT],
                                    start=(d == 0), stop=(d == DC - 1)),
                                     waits=([(ve, xnf_done)] + bankw[bq].take())
                                     if dc == 0 else [],
                                     incs=[(pe, 1)] if dc == DC - 1 else [])
                            pw = pe.n
                            P.op("scalar", lambda s, k=bq: s.activation(
                                out=sq_b[:], in_=pb[k][:, 0:512], func=AF.Square),
                                 waits=[(pe, pw)] + bankw[5].take(), incs=[(se, 1)])
                            sw = se.n
                            P.op("tensor", lambda te: te.matmul(
                                pb[5][0:1, 0:512], onesk_b[:], sq_b[:],
                                start=True, stop=True),
                                 waits=[(se, sw)], incs=[(pe, 1)])
                            pw = pe.n
                            tk = P.op("vector", lambda v: v.tensor_scalar_max(
                                ssq_s[0:1, :], pb[5][0:1, 0:512], 1e-24),
                                 waits=[(pe, pw)], incs=[(ve, 1)])
                            vw = ve.n
                            P.op("scalar", lambda s: s.activation(
                                out=ssq_s[0:1, :], in_=ssq_s[0:1, :], func=AF.Ln),
                                 waits=[(ve, vw)], incs=[(se, 1)])
                            P.op("scalar", lambda s: s.activation(
                                out=rn_b[0:1, :], in_=ssq_s[0:1, :], func=AF.Exp,
                                scale=-0.5),
                                 waits=[(se, se.n)], incs=[(se, 1)])
                            sw = se.n
                            P.op("tensor", lambda te: te.matmul(
                                pb[5][:, 0:512], ones1_b[0:1, :], rn_b[0:1, :],
                                start=True, stop=True),
                                 waits=[(se, sw)], incs=[(pe, 1)])
                            pw = pe.n
                            P.op("vector", lambda v, dst=dstn, k=bq: v.tensor_copy(
                                dst[:], pb[k][:, 0:512]),
                                 waits=[(pe, pw)], incs=[(ve, 1)])
                            tk = P.op("vector", lambda v, dst=dstn: v.tensor_tensor(
                                dst[:], dst[:], pb[5][:, 0:512], ALU.mult),
                                 waits=[(ve, ve.n)], incs=[(ve, 1)])
                            bankw[5].r.extend(tk)
                            bankw[bq].r.extend(tk)

                    def emit_scalings(h):
                        par = h % 4
                        sl = h % 2
                        ew = e12_w[sl].take()
                        for c in range(4):
                            P.op("vector", lambda v, c0=c * 128, p=par: v.tensor_scalar(
                                d1m[:, c0:c0 + 128], U_st[p][:, c0:c0 + 128],
                                U_st[p][:, c0 + 63:c0 + 64], None, ALU.subtract),
                                 waits=[(ve, U_done[h])] + (ew if c == 0 else []),
                                 incs=[(ve, 1)])
                        for c in range(4):
                            up = (zcol[:, 0:1] if c == 0
                                  else U_st[par][:, c * 128 - 1:c * 128])
                            P.op("vector", lambda v, c2=c, u=up, p=par, s2=sl: v.tensor_tensor(
                                scm[s2][:, c2:c2 + 1], u,
                                U_st[p][:, c2 * 128 + 63:c2 * 128 + 64],
                                ALU.subtract), incs=[(ve, 1)])
                        vw = ve.n
                        P.op("scalar", lambda s, s2=sl: s.activation(
                            out=scm[s2][:, 0:4], in_=scm[s2][:, 0:4], func=AF.Exp),
                             waits=[(ve, vw)], incs=[(se, 1)])
                        P.op("scalar", lambda s: s.activation(
                            out=e1a[:], in_=d1m[:], func=AF.Exp, scale=-1.0),
                             incs=[(se, 1)])
                        P.op("scalar", lambda s: s.activation(
                            out=d1m[:], in_=d1m[:], func=AF.Exp),
                             waits=[(se, se.n)], incs=[(se, 1)])
                        sw = se.n
                        P.op("vector", lambda v, p=par, s2=sl: v.tensor_tensor(
                            qt_a[s2][:], qn[p][:], e1a[:], ALU.mult),
                             waits=[(se, sw)], incs=[(ve, 1)])
                        P.op("vector", lambda v, p=par, s2=sl: v.tensor_tensor(
                            kt_a[s2][:], kn[p][:], d1m[:], ALU.mult),
                             incs=[(ve, 1)])
                        for c in range(4):
                            c0 = c * 128
                            qh_ap = qh_all[:, h * NT + g * GT + c0:
                                           h * NT + g * GT + c0 + 128]
                            P.op("vector", lambda v, qa=qh_ap, c0=c0, s2=sl: v.tensor_scalar(
                                qa, qt_a[s2][:, c0:c0 + 128],
                                scm[s2][:, c0 // 128:c0 // 128 + 1], None, ALU.mult),
                                 waits=qh_w.take(), incs=[(ve, 1)])
                            P.op("vector", lambda v, c0=c0, s2=sl: v.tensor_scalar(
                                kh_a[s2][:, c0:c0 + 128], kt_a[s2][:, c0:c0 + 128],
                                e1a[:, c0 + 127:c0 + 128], None, ALU.mult),
                                 incs=[(ve, 1)])
                            P.op("vector", lambda v, c2=c, s2=sl: v.tensor_scalar(
                                scm[s2][:, 4 + c2:5 + c2], e1a[:, c2 * 128 + 127:
                                                              c2 * 128 + 128],
                                scm[s2][:, c2:c2 + 1], None, ALU.mult),
                                 incs=[(ve, 1)])
                        scale_done[h] = ve.n

                    def chunk_body(h, c):
                        cp = h % 2
                        am_b, khtm_b = am_p[cp], khtm_p[cp]
                        nch = g * 4 + c
                        c0 = c * 128
                        qh_ap = qh_all[:, h * NT + g * GT + c0:
                                       h * NT + g * GT + c0 + 128]
                        qt_s = qt_a[cp][:, c0:c0 + 128]
                        kt_s = kt_a[cp][:, c0:c0 + 128]
                        kh_s = kh_a[cp][:, c0:c0 + 128]
                        dcol = scm[cp][:, 4 + c:5 + c]
                        bw0 = bankw[0].take()
                        P.op("tensor", lambda te, t=kt_s, u=qt_s: te.matmul(
                            pb[0][0:64, 0:64], t[:, 0:64], u[:, 0:64],
                            start=True, stop=True),
                             waits=[(ve, scale_done[h])] + bw0, incs=[(pe, 1)])
                        P.op("tensor", lambda te, t=kt_s, u=qt_s: te.matmul(
                            pb[0][64:128, 64:128], t[:, 64:128],
                            u[:, 64:128], start=True, stop=True),
                             incs=[(pe, 1)])
                        P.op("tensor", lambda te, t=kt_s, u=qt_s: te.matmul(
                            pb[0][0:64, 64:128], t[:, 0:64],
                            u[:, 64:128], start=True, stop=True),
                             incs=[(pe, 1)])
                        P.op("tensor", lambda te, t=kh_s: te.transpose(
                            pbt_b[:, 0:128], t, ident_b[:]),
                             waits=bankw[7].take(), incs=[(pe, 1)])
                        pw = pe.n
                        tk = P.op("vector", lambda v, t=am_b: v.tensor_tensor(
                            t[:], pb[0][:, 0:128], mask_f[:], ALU.mult),
                             waits=[(pe, pw)], incs=[(ve, 1)])
                        bankw[0].r.extend(tk)
                        tk = P.op("scalar", lambda s, t=khtm_b: s.copy(
                            t[:], pbt_b[:, 0:128]),
                             waits=[(pe, pw)], incs=[(se, 1)])
                        bankw[7].r.extend(tk)
                        kh_se = se.n
                        vw = ve.n
                        vsl = v_tm[:, c * 1024 + h * 128:c * 1024 + h * 128 + 128]
                        P.op("tensor", lambda te, hh=h, qa=qh_ap: te.matmul(
                            pb[6][:, 0:128],
                            S_b[:, hh * 128:hh * 128 + 128],
                            qa, start=True, stop=False),
                             waits=[(ve, vw), (se, sb_tick[h])] + bankw[6].take(),
                             incs=[(pe, 1)])
                        P.op("tensor", lambda te, vs=vsl, t=am_b: te.matmul(
                            pb[6][:, 0:128], vs, t[:],
                            start=False, stop=True), incs=[(pe, 1)])
                        o_pe = pe.n
                        P.op("tensor", lambda te, vs=vsl, t=khtm_b: te.matmul(
                            pb[1][:, 0:128], t[:], vs,
                            start=True, stop=True),
                             waits=[(se, kh_se)] + bankw[1].take(),
                             incs=[(pe, 1)])
                        m_pe = pe.n
                        o_ap = o_all[:, h * NT + g * GT + c0:
                                     h * NT + g * GT + c0 + 128]
                        tk = P.op("vector", lambda v, oa=o_ap: v.tensor_copy(
                            oa, pb[6][:, 0:128]),
                             waits=[(pe, o_pe)] + o_w.take(), incs=[(ve, 1)])
                        bankw[6].r.extend(tk)
                        pidx = P_all[:, h * 16 + nch:h * 16 + nch + 1]
                        P.op("vector", lambda v, pi=pidx, hh=h: v.tensor_copy(
                            pi, Pcur[:, hh:hh + 1]), incs=[(ve, 1)])
                        tk = P.op("vector", lambda v, hh=h, dc2=dcol:
                                  v.scalar_tensor_tensor(
                                      S_f[:, hh * 128:hh * 128 + 128],
                                      S_f[:, hh * 128:hh * 128 + 128],
                                      dc2, pb[1][:, 0:128],
                                      ALU.mult, ALU.add),
                                  waits=[(pe, m_pe)], incs=[(ve, 1)])
                        bankw[1].r.extend(tk)
                        sv = ve.n
                        tk = P.op("scalar", lambda s, hh=h: s.copy(
                            S_b[:, hh * 128:hh * 128 + 128],
                            S_f[:, hh * 128:hh * 128 + 128]),
                             waits=[(ve, sv)], incs=[(se, 1)])
                        sb_tick[h] = se.n
                        tk2 = P.op("vector", lambda v, hh=h, dc2=dcol: v.tensor_tensor(
                            Pcur[:, hh:hh + 1], Pcur[:, hh:hh + 1], dc2,
                            ALU.mult), incs=[(ve, 1)])
                        if c == 3:
                            e12_w[h % 2].r.extend(tk2)

                    def chunks_pair(hA, hB):
                        emit_scalings(hA)
                        chunk_body(hA, 0)
                        emit_scalings(hB)
                        for c in range(3):
                            chunk_body(hA, c + 1)
                            chunk_body(hB, c)
                        chunk_body(hB, 3)

                    for i in range(4):
                        emit_proj(2 * i)
                        emit_proj(2 * i + 1)
                        if i >= 1:
                            chunks_pair(2 * i - 2, 2 * i - 1)
                    chunks_pair(6, 7)
                P.op("vector", lambda v: v.tensor_copy(Dsl[:], Pcur[:]),
                     waits=[(ve, ve.n)], incs=[(ve, 1)])
                if stage == "x1":
                    vend = ve.n
                    for (dst, srcb) in ((dbg_o, o_all), (dbg_q, qh_all),
                                        (dbg_s, S_f), (dbg_p, P_all),
                                        (dbg_u, U_st[3]), (dbg_qn, qn[3]),
                                        (dbg_kn, kn[3]), (dbg_xnf, xnf),
                                        (dbg_v, v_tm)):
                        P.op("sync", lambda s, d=dst, sr=srcb: s.dma_start(
                            out=d[:, :], in_=sr[:]),
                             waits=[(ve, vend)], incs=[(od, 16)])

            P.barrier()
            with ExitStack() as ctxB:
                sbb = lambda n, s, d=F32: ctxB.enter_context(nc.sbuf_tensor(n, s, d))
                ccst = sbb("ccst", [128, 1032])
                gth = sbb("gth", [128, 3 * 1032])
                Sst_f = sbb("Sst_f", [128, H * 128])
                Sst_b = sbb("Sst_b", [128, H * 128], BF16)
                DmS = sbb("DmS", [128, 8])
                mm_scr = sbb("mm_scr", [128, 128])
                qp_b = [sbb(f"qp_b{i}", [128, 512], BF16) for i in range(2)]

                P.op("vector", lambda v: v.tensor_copy(ccst[:, 0:1024], S_f[:]),
                     incs=[(ve, 1)])
                P.op("vector", lambda v: v.tensor_copy(ccst[:, 1024:1032], Dsl[:]),
                     waits=[(ve, ve.n)], incs=[(ve, 1)])
                vw = ve.n
                P.op("sync", lambda s: s.dma_start(out=ccin.ap()[:, :], in_=ccst[:]),
                     waits=[(ve, vw)], incs=[(lcc, 16)])
                lw = lcc.n
                P.op("gpsimd", lambda g: g.collective_compute(
                    "AllGather", ALU.bypass,
                    replica_groups=[[0, 1, 2, 3], [4, 5, 6, 7]],
                    ins=[ccin.ap().opt()], outs=[ccout.ap().opt()]),
                     waits=[(lcc, lw)], incs=[(cc, 1)])
                for j in range(3):
                    P.op("sync", lambda s, jj=j: s.dma_start(
                        out=gth[:, jj * 1032:(jj + 1) * 1032],
                        in_=ccout.ap()[jj * 128:(jj + 1) * 128, :]),
                         waits=[(cc, 1)], incs=[(lcc, 16)])
                lw = lcc.n
                P.op("vector", lambda v: v.memset(Sst_f[:], 0.0),
                     waits=[(lcc, lw), (ld, ld_base)], incs=[(ve, 1)])
                for j in range(3):
                    P.op("vector", lambda v, jj=j: v.tensor_scalar(
                        DmS[:], gth[:, jj * 1032 + 1024:jj * 1032 + 1032],
                        um_s[:, jj:jj + 1], um_s[:, 3 + jj:4 + jj],
                        ALU.mult, ALU.add), incs=[(ve, 1)])
                    dms_tick = ve.n
                    for h in range(H):
                        P.op("vector", lambda v, jj=j, hh=h: v.tensor_scalar(
                            mm_scr[:],
                            gth[:, jj * 1032 + hh * 128:jj * 1032 + hh * 128 + 128],
                            um_s[:, jj:jj + 1], None, ALU.mult), incs=[(ve, 1)])
                        P.op("vector", lambda v, hh=h: v.scalar_tensor_tensor(
                            Sst_f[:, hh * 128:hh * 128 + 128],
                            Sst_f[:, hh * 128:hh * 128 + 128],
                            DmS[:, hh:hh + 1], mm_scr[:],
                            ALU.mult, ALU.add),
                             waits=[(ve, dms_tick)] if h == 0 else [],
                             incs=[(ve, 1)])
                P.op("vector", lambda v: v.tensor_copy(Sst_b[:], Sst_f[:]),
                     incs=[(ve, 1)])
                for h in range(H):
                    for nb in range(4):
                        np_ = nb % 2
                        bw = bankw[np_].take()
                        for j in range(4):
                            n = nb * 4 + j
                            P.op("vector", lambda v, hh=h, nn=n, jj=j,
                                 q2=qp_b[np_]: v.tensor_scalar(
                                     q2[:, jj * 128:jj * 128 + 128],
                                     qh_all[:, hh * NT + nn * 128:
                                            hh * NT + nn * 128 + 128],
                                     P_all[:, hh * 16 + nn:hh * 16 + nn + 1],
                                     None, ALU.mult),
                                 waits=bw if j == 0 else [], incs=[(ve, 1)])
                        vw = ve.n
                        P.op("tensor", lambda te, hh=h, k=np_, q2=qp_b[np_]: te.matmul(
                            pb[k][:, 0:512], Sst_b[:, hh * 128:hh * 128 + 128],
                            q2[:], start=True, stop=True),
                             waits=[(ve, vw)], incs=[(pe, 1)])
                        pw = pe.n
                        oa = o_all[:, h * NT + nb * 512:h * NT + nb * 512 + 512]
                        tk = P.op("vector", lambda v, o2=oa, k=np_: v.tensor_tensor(
                            o2, o2, pb[k][:, 0:512], ALU.add),
                             waits=[(pe, pw)], incs=[(ve, 1)])
                        bankw[np_].r.extend(tk)

            P.barrier()
            with ExitStack() as ctxC:
                sbc = lambda n, s, d=F32: ctxC.enter_context(nc.sbuf_tensor(n, s, d))
                wo_s = sbc("wo_s", [128, DC * D], BF16)
                am_f = sbc("am_f", [128, DC * 512])
                amtm = [sbc(f"amtm{i}", [128, 1024]) for i in range(2)]
                x_st2 = [sbc(f"x_st2_{i}", [128, 1024]) for i in range(2)]
                x1_st = [sbc(f"x1_st{i}", [128, 1024]) for i in range(2)]
                xt_st = [sbc(f"xt_st{i}", [128, 1024]) for i in range(2)]
                xtf_st = [sbc(f"xtf_st{i}", [128, DC * 128], BF16) for i in range(2)]
                amf_w = _Buf()
                amtm_w = [_Buf(), _Buf()]
                x1_w = [_Buf(), _Buf()]
                xt_w = [_Buf(), _Buf()]
                xtf_w = [_Buf(), _Buf()]
                x2_w = [_Buf(), _Buf()]

                s3 = wo_d.rearrange("(hc p) n -> p hc n", p=128)
                P.op("gpsimd", lambda g: g.dma_start(
                    out=wo_s.ap().rearrange("p (hc n) -> p hc n", hc=DC), in_=s3),
                     incs=[(wd, 16)])
                wd_wo = wd.n

                for tt in range(4):
                    for dc in range(DC):
                        bk = dc % 2
                        for h in range(H):
                            P.op("tensor", lambda te, hh=h, d=dc, t4=tt, k=bk: te.matmul(
                                pb[k][:, 0:512],
                                wo_s[:, hh * D + d * 128:hh * D + d * 128 + 128],
                                o_all[:, hh * NT + t4 * 512:hh * NT + t4 * 512 + 512],
                                start=(hh == 0), stop=(hh == H - 1)),
                                 waits=([(wd, wd_wo), (ve, ve.n)]
                                        + bankw[bk].take()) if h == 0 else [],
                                 incs=[(pe, 1)] if h == H - 1 else [])
                        pw = pe.n
                        tk = P.op("vector", lambda v, d=dc, k=bk: v.tensor_copy(
                            am_f[:, d * 512:d * 512 + 512], pb[k][:, 0:512]),
                             waits=[(pe, pw)] + (amf_w.take() if dc == 0 else []),
                             incs=[(ve, 1)])
                        bankw[bk].r.extend(tk)
                    amf_done = ve.n
                    def am_part(sub):
                        sp = sub % 2
                        tok0 = tt * 512 + sub * 128
                        amt, xs2, x1s = amtm[sp], x_st2[sp], x1_st[sp]
                        for dc in range(DC):
                            bk = 2 + dc % 2
                            P.op("tensor", lambda te, d=dc, s4=sub, k=bk: te.transpose(
                                pb[k][:, 0:128],
                                am_f[:, d * 512 + s4 * 128:d * 512 + s4 * 128 + 128],
                                ident_f[:]),
                                 waits=[(ve, amf_done)] + bankw[bk].take(),
                                 incs=[(pe, 1)])
                            pw = pe.n
                            tk = P.op("vector", lambda v, d=dc, k=bk, t=amt: v.tensor_copy(
                                t[:, d * 128:d * 128 + 128], pb[k][:, 0:128]),
                                 waits=[(pe, pw)] + (amtm_w[sp].take()
                                                     if dc == 0 else []),
                                 incs=[(ve, 1)])
                            bankw[bk].r.extend(tk)
                        amf_w.r.append((ve, ve.n))
                        tick = ln_tile(amt[:], amt[:])
                        amtm_w[sp].r.extend(tick)
                        P.op("sync", lambda s, o=tok0, t=xs2: s.dma_start(
                            out=t[:], in_=xsl[o:o + 128, :]),
                             waits=x2_w[sp].take(), incs=[(lx2[sp], 16)])
                        lw = lx2[sp].n
                        tk = P.op("vector", lambda v, t=x1s, u=xs2, w2=amt: v.tensor_tensor(
                            t[:], u[:], w2[:], ALU.add),
                             waits=[(lx2[sp], lw), (ve, ve.n)] + x1_w[sp].take(),
                             incs=[(ve, 1)])
                        x2_w[sp].r.extend(tk)
                        amtm_w[sp].r.extend(tk)
                        vw = ve.n
                        P.op("sync", lambda s, o=tok0, t=x1s: s.dma_start(
                            out=x1sp[o:o + 128, :], in_=t[:]),
                             waits=[(ve, vw)], incs=[(ls1[sp], 16)])
                        x1_w[sp].r.append((ls1[sp], ls1[sp].n))
                        return vw

                    def xt_part(sub, x1vw):
                        sp = sub % 2
                        tok0 = tt * 512 + sub * 128
                        x1s, xts, xtf = x1_st[sp], xt_st[sp], xtf_st[sp]
                        P.op("scalar", lambda s, t=xts, u=x1s: s.copy(t[:], u[:]),
                             waits=[(ve, x1vw)] + xt_w[sp].take(), incs=[(se, 1)])
                        x1_w[sp].r.append((se, se.n))
                        tick = ln_tile(xts[:], xts[:], extra_waits=[(se, se.n)])
                        for dc in range(DC):
                            bk = 2 + dc % 2
                            P.op("tensor", lambda te, d=dc, k=bk, t=xts: te.transpose(
                                pb[k][:, 0:128], t[:, d * 128:d * 128 + 128],
                                ident_f[:]),
                                 waits=[(ve, ve.n)] + bankw[bk].take(),
                                 incs=[(pe, 1)])
                            pw = pe.n
                            tk = P.op("vector", lambda v, d=dc, k=bk, t=xtf: v.tensor_copy(
                                t[:, d * 128:d * 128 + 128], pb[k][:, 0:128]),
                                 waits=[(pe, pw)] + (xtf_w[sp].take()
                                                     if dc == 0 else []),
                                 incs=[(ve, 1)])
                            bankw[bk].r.extend(tk)
                            xt_w[sp].r.extend(tk)
                        vw = ve.n
                        P.op("sync", lambda s, o=tok0, t=xtf: s.dma_start(
                            out=xtsp[:, :, o:o + 128],
                            in_=t.ap().rearrange("p (dc n) -> p dc n", dc=DC)),
                             waits=[(ve, vw)], incs=[(lxt[sp], 16)])
                        xtf_w[sp].r.append((lxt[sp], lxt[sp].n))

                    x1vws = {}
                    for sub in range(4):
                        x1vws[sub] = am_part(sub)
                        if sub >= 1:
                            xt_part(sub - 1, x1vws[sub - 1])
                    xt_part(3, x1vws[3])

        P.barrier()
        if stage == "x1":
            with ExitStack() as ctxX:
                xo = ctxX.enter_context(nc.sbuf_tensor("xo", [128, 1024], F32))
                first = [(ls1[0], ls1[0].n), (ls1[1], ls1[1].n)]
                for t in range(16):
                    P.op("sync", lambda s, o=t * 128: s.dma_start(
                        out=xo[:], in_=x1sp[o:o + 128, :]),
                         waits=[(od, od.n)] + first, incs=[(lr1, 16)])
                    first = []
                    lw = lr1.n
                    P.op("sync", lambda s, o=t * 128: s.dma_start(
                        out=out_d[o:o + 128, :], in_=xo[:]),
                         waits=[(lr1, lw)], incs=[(od, 16)])
                P.op("sync", lambda s: s.nop(), waits=[(od, od.n)])
            P.realize(nc)
            return nc

        with ExitStack() as ctxD:
            sbd = lambda n, s, d=F32: ctxD.enter_context(nc.sbuf_tensor(n, s, d))
            xt_half = sbd("xt_half", [128, DC * HALF], BF16)
            h1s = sbd("h1s", [128, IC * HALF], BF16)
            y_f = sbd("y_f", [128, DC * HALF])
            wA = sbd("wA", [128, 16 * 1024], BF16)
            wB = sbd("wB", [128, 16 * 1024], BF16)
            wC = sbd("wC", [128, 16 * 1024], BF16)
            cbc = sbd("cbc", [128, HALF])
            gwT_s = sbd("gwT_s", [128, DC * 8], BF16)
            c_tok = sbd("c_tok", [128, 16 * 8])
            c_fm = sbd("c_fm", [8, HALF])
            sm_scr = sbd("sm_scr", [128, 8])
            h3c = sbd("h3c", [128, 512], BF16)
            c_row = cbc
            ytm = [sbd(f"ytm{i}", [128, 1024]) for i in range(2)]
            x1r = sbd("x1r", [128, 1024])
            wbuf = [wA, wB, wC]
            wb_w = [_Buf(), _Buf(), _Buf()]
            cfence = [(ls1[0], ls1[0].n), (ls1[1], ls1[1].n),
                      (lxt[0], lxt[0].n), (lxt[1], lxt[1].n),
                      (lx2[0], lx2[0].n), (lx2[1], lx2[1].n)]
            wb_w[0].r += cfence
            wb_w[1].r += cfence
            wb_w[2].r += cfence
            xth_w = _Buf()
            h1_w = _Buf()
            y_w = _Buf()
            cbc_w = _Buf()
            crow_w = _Buf()
            ctok_w = _Buf()
            cfm_w = _Buf()
            ytm_w = [_Buf(), _Buf()]
            x1r_w = _Buf()
            outst_w = _Buf()

            P.op("gpsimd", lambda g: g.dma_start(
                out=gwT_s.ap().rearrange("p (dc n) -> p dc n", dc=DC),
                in_=gwT_d.rearrange("(dc p) n -> p dc n", p=128)),
                 waits=list(cfence), incs=[(wd, 16)])
            wd_gw = wd.n

            e_order = [8] + list(range(8))

            for half in range(2):
                htok0 = half * HALF
                P.op("sync", lambda s, o=htok0: s.dma_start(
                    out=xt_half.ap().rearrange("p (dc n) -> p dc n", dc=DC),
                    in_=xtsp[:, :, o:o + HALF]),
                     waits=xth_w.take() + list(cfence), incs=[(lxh, 16)])
                xt_done_ld = lxh.n

                for t8 in range(8):
                    bk = t8 % 2
                    for dc in range(DC):
                        P.op("tensor", lambda te, t=t8, d=dc, k=bk: te.matmul(
                            pb[k][:, 0:8],
                            xt_half[:, d * HALF + t * 128:d * HALF + t * 128 + 128],
                            gwT_s[:, d * 8:d * 8 + 8],
                            start=(d == 0), stop=(d == DC - 1)),
                             waits=([(lxh, xt_done_ld), (wd, wd_gw)]
                                    + bankw[bk].take()) if dc == 0 else [],
                             incs=[(pe, 1)] if dc == DC - 1 else [])
                    pw = pe.n
                    mx, sμ, m1c, m2c = col(), col(), col(), col()
                    P.op("vector", lambda v, k=bk, m=mx: v.reduce_max(
                        m, pb[k][:, 0:8], axis=AXX),
                         waits=[(pe, pw)] + ctok_w.take(), incs=[(ve, 1)])
                    nmx = col()
                    P.op("vector", lambda v, m=mx, nm=nmx: v.tensor_scalar(
                        nm, m, -1.0, None, ALU.mult),
                         waits=[(ve, ve.n)], incs=[(ve, 1)])
                    vw = ve.n
                    P.op("scalar", lambda s, k=bk, nm=nmx: s.activation(
                        out=sm_scr[:], in_=pb[k][:, 0:8], func=AF.Exp, bias=nm),
                         waits=[(ve, vw)], incs=[(se, 1)])
                    sw = se.n
                    tk = P.op("vector", lambda v, m=sμ: v.reduce_sum(
                        m, sm_scr[:], axis=AXX), waits=[(se, sw)], incs=[(ve, 1)])
                    bankw[bk].r.extend(tk)
                    P.op("vector", lambda v, m=sμ: v.reciprocal(m, m),
                         waits=[(ve, ve.n)], incs=[(ve, 1)])
                    P.op("vector", lambda v, m=sμ: v.tensor_scalar(
                        sm_scr[:], sm_scr[:], m, None, ALU.mult),
                         waits=[(ve, ve.n)], incs=[(ve, 1)])
                    ctk = c_tok[:, t8 * 8:t8 * 8 + 8]
                    P.op("vector", lambda v, ct=ctk: v.tensor_tensor(
                        ct, sm_scr[:], gbias_s[:], ALU.add),
                         waits=[(ve, ve.n), (ld, ld_base)], incs=[(ve, 1)])
                    P.op("vector", lambda v, ct=ctk, m=m1c: v.reduce_max(
                        m, ct, axis=AXX), waits=[(ve, ve.n)], incs=[(ve, 1)])
                    P.op("vector", lambda v, ct=ctk, m=m1c: v.tensor_scalar(
                        sm_scr[:], ct, m, None, ALU.is_ge),
                         waits=[(ve, ve.n)], incs=[(ve, 1)])
                    P.op("vector", lambda v, ct=ctk: v.scalar_tensor_tensor(
                        sm_scr[:], sm_scr[:], -1e30, ct, ALU.mult, ALU.add),
                         waits=[(ve, ve.n)], incs=[(ve, 1)])
                    P.op("vector", lambda v, m=m2c: v.reduce_max(
                        m, sm_scr[:], axis=AXX),
                         waits=[(ve, ve.n)], incs=[(ve, 1)])
                    P.op("vector", lambda v, ct=ctk, m=m2c: v.tensor_scalar(
                        sm_scr[:], ct, m, None, ALU.is_ge),
                         waits=[(ve, ve.n)], incs=[(ve, 1)])
                    P.op("vector", lambda v, ct=ctk: v.tensor_tensor(
                        ct, ct, gbias_s[:], ALU.subtract),
                         waits=[(ve, ve.n)], incs=[(ve, 1)])
                    P.op("vector", lambda v, ct=ctk: v.tensor_tensor(
                        ct, ct, sm_scr[:], ALU.mult),
                         waits=[(ve, ve.n)], incs=[(ve, 1)])
                    vw = ve.n
                    P.op("tensor", lambda te, ct=ctk, k=2 + bk: te.transpose(
                        pb[k][0:8, 0:128], ct, ident_f[:]),
                         waits=[(ve, vw)] + bankw[2 + bk].take(), incs=[(pe, 1)])
                    pw = pe.n
                    tk = P.op("vector", lambda v, t=t8, k=2 + bk: v.tensor_copy(
                        c_fm[0:8, t * 128:t * 128 + 128],
                        pb[k][0:8, 0:128]),
                         waits=[(pe, pw)] + (cfm_w.take() if t8 == 0 else []),
                         incs=[(ve, 1)])
                    bankw[2 + bk].r.extend(tk)
                    ctok_w.r.extend(tk)
                cfm_done = ve.n

                lctr = [0]

                def wstream(src4, parity, d4):
                    P.op("gpsimd", lambda g, s=src4, d=d4: g.dma_start(
                        out=d, in_=s),
                         waits=wb_w[parity].take(), incs=[(wdp[parity], 16)])
                    return wdp[parity].n

                for ei, e in enumerate(e_order):
                    par1 = lctr[0] % 3
                    wd1s = wdp[par1]
                    wd1 = wstream(
                        w1T_d[e].rearrange("(dc p) (ic n) -> p dc ic n",
                                           p=128, n=128),
                        par1,
                        wbuf[par1].ap().rearrange("p (dc ic n) -> p dc ic n",
                                                  dc=DC, ic=IC))
                    lctr[0] += 1
                    par3 = lctr[0] % 3
                    wd3s = wdp[par3]
                    wd3 = wstream(
                        w3T_d[e].rearrange("(dc p) (ic n) -> p dc ic n",
                                           p=128, n=128),
                        par3,
                        wbuf[par3].ap().rearrange("p (dc ic n) -> p dc ic n",
                                                  dc=DC, ic=IC))
                    lctr[0] += 1
                    par2 = lctr[0] % 3
                    wd2s = wdp[par2]
                    wd2 = wstream(
                        w2T_d[e].rearrange("(ic p) (dc n) -> p ic dc n",
                                           p=128, n=128),
                        par2,
                        wbuf[par2].ap().rearrange("p (ic dc n) -> p ic dc n",
                                                  ic=IC, dc=DC))
                    lctr[0] += 1

                    if e != 8:
                        P.op("sync", lambda s, ee=e: s.dma_start(
                            out=c_row[0:1, :],
                            in_=c_fm[ee:ee + 1, 0:HALF]),
                             waits=[(ve, cfm_done)] + crow_w.take(),
                             incs=[(lcr, 16)])
                        wd_cr = lcr.n
                        for t2 in range(2):
                            tkp = P.op("tensor", lambda te, t=t2: te.matmul(
                                pb[4][:, 0:512], ones1_f[0:1, :],
                                c_row[0:1, t * 512:t * 512 + 512],
                                start=True, stop=True),
                                 waits=[(lcr, wd_cr)] + bankw[4].take(),
                                 incs=[(pe, 1)])
                            crow_w.r.extend(tkp)
                            pw = pe.n
                            tk = P.op("vector", lambda v, t=t2: v.tensor_copy(
                                cbc[:, t * 512:t * 512 + 512], pb[4][:, 0:512]),
                                 waits=[(pe, pw)] + (cbc_w.take()
                                                     if t2 == 0 else []),
                                 incs=[(ve, 1)])
                            bankw[4].r.extend(tk)
                        cbc_done = ve.n
                    else:
                        cbc_done = 0

                    for ic in range(IC):
                        for t2 in range(2):
                            bk = (ic * 2 + t2) % 2
                            for dc in range(DC):
                                P.op("tensor", lambda te, i=ic, t=t2, d=dc, k=bk,
                                     p=par1: te.matmul(
                                         pb[k][:, 0:512],
                                         wbuf[p][:, d * 2048 + i * 128:
                                                 d * 2048 + i * 128 + 128],
                                         xt_half[:, d * HALF + t * 512:
                                                 d * HALF + t * 512 + 512],
                                         start=(d == 0), stop=(d == DC - 1)),
                                     waits=([(wd1s, wd1), (lxh, xt_done_ld)]
                                            + bankw[bk].take()) if dc == 0 else [],
                                     incs=[(pe, 1)] if dc == DC - 1 else [])
                            pw = pe.n
                            tk = P.op("scalar", lambda s, i=ic, t=t2, k=bk: s.activation(
                                out=h1s[:, i * HALF + t * 512:i * HALF + t * 512 + 512],
                                in_=pb[k][:, 0:512], func=AF.Silu),
                                 waits=[(pe, pw)] + (h1_w.take()
                                                     if ic == 0 and t2 == 0 else []),
                                 incs=[(se, 1)])
                            bankw[bk].r.extend(tk)
                            wb_w[par1].r.extend(tk)
                    for ic in range(IC):
                        for t2 in range(2):
                            bk = (ic * 2 + t2) % 2
                            for dc in range(DC):
                                P.op("tensor", lambda te, i=ic, t=t2, d=dc, k=bk,
                                     p=par3: te.matmul(
                                         pb[k][:, 0:512],
                                         wbuf[p][:, d * 2048 + i * 128:
                                                 d * 2048 + i * 128 + 128],
                                         xt_half[:, d * HALF + t * 512:
                                                 d * HALF + t * 512 + 512],
                                         start=(d == 0), stop=(d == DC - 1)),
                                     waits=([(wd3s, wd3)] + bankw[bk].take())
                                     if dc == 0 else [],
                                     incs=[(pe, 1)] if dc == DC - 1 else [])
                            pw = pe.n
                            hsl = h1s[:, ic * HALF + t2 * 512:
                                      ic * HALF + t2 * 512 + 512]
                            if e != 8:
                                tkc = P.op("vector", lambda v, t=t2, k=bk: v.tensor_tensor(
                                    h3c[:], pb[k][:, 0:512],
                                    cbc[:, t * 512:t * 512 + 512], ALU.mult),
                                     waits=[(pe, pw), (ve, cbc_done), (se, se.n)],
                                     incs=[(ve, 1)])
                                crow_w.r.extend(tkc)
                                tk = P.op("vector", lambda v, hs=hsl: v.tensor_tensor(
                                    hs, h3c[:], hs, ALU.mult), incs=[(ve, 1)])
                            else:
                                tk = P.op("vector", lambda v, hs=hsl, k=bk: v.tensor_tensor(
                                    hs, pb[k][:, 0:512], hs, ALU.mult),
                                     waits=[(pe, pw), (se, se.n)], incs=[(ve, 1)])
                            bankw[bk].r.extend(tk)
                            wb_w[par3].r.extend(tk)
                            if e != 8:
                                cbc_w.r.extend(tk)
                    ve_p2 = ve.n
                    if ei == 0:
                        ydc_tick = {}
                    for dc in range(DC):
                        for t2 in range(2):
                            bk = (dc * 2 + t2) % 2
                            for ic in range(IC):
                                P.op("tensor", lambda te, i=ic, t=t2, d=dc, k=bk,
                                     p=par2: te.matmul(
                                         pb[k][:, 0:512],
                                         wbuf[p][:, i * 1024 + d * 128:
                                                 i * 1024 + d * 128 + 128],
                                         h1s[:, i * HALF + t * 512:
                                             i * HALF + t * 512 + 512],
                                         start=(i == 0), stop=(i == IC - 1)),
                                     waits=([(wd2s, wd2), (ve, ve_p2)]
                                            + bankw[bk].take()) if ic == 0 else [],
                                     incs=[(pe, 1)] if ic == IC - 1 else [])
                            pw = pe.n
                            ysl = y_f[:, dc * HALF + t2 * 512:
                                      dc * HALF + t2 * 512 + 512]
                            if ei == 0:
                                tk = P.op("vector", lambda v, ys=ysl, k=bk: v.tensor_copy(
                                    ys, pb[k][:, 0:512]),
                                     waits=[(pe, pw)] + (y_w.take()
                                                         if dc == 0 and t2 == 0 else []),
                                     incs=[(ve, 1)])
                            else:
                                tk = P.op("vector", lambda v, ys=ysl, k=bk: v.tensor_tensor(
                                    ys, ys, pb[k][:, 0:512], ALU.add),
                                     waits=[(pe, pw)], incs=[(ve, 1)])
                            bankw[bk].r.extend(tk)
                            wb_w[par2].r.extend(tk)
                            h1_w.r.extend(tk)
                            ydc_tick[dc] = ve.n
                y_done = ve.n

                for sub in range(8):
                    sp = sub % 2
                    tok0 = htok0 + sub * 128
                    yt, xr = ytm[sp], x1r
                    for dc in range(DC):
                        bk = 2 + dc % 2
                        P.op("tensor", lambda te, d=dc, s4=sub, k=bk: te.transpose(
                            pb[k][:, 0:128],
                            y_f[:, d * HALF + s4 * 128:d * HALF + s4 * 128 + 128],
                            ident_f[:]),
                             waits=[(ve, ydc_tick[dc])] + bankw[bk].take(),
                             incs=[(pe, 1)])
                        pw = pe.n
                        tk = P.op("vector", lambda v, d=dc, k=bk, t=yt: v.tensor_copy(
                            t[:, d * 128:d * 128 + 128], pb[k][:, 0:128]),
                             waits=[(pe, pw)] + (ytm_w[sp].take()
                                                 if dc == 0 else []),
                             incs=[(ve, 1)])
                        bankw[bk].r.extend(tk)
                        y_w.r.extend(tk)
                    tick = ln_tile(yt[:], yt[:])
                    ytm_w[sp].r.extend(tick)
                    P.op("sync", lambda s, o=tok0, t=xr: s.dma_start(
                        out=t[:], in_=x1sp[o:o + 128, :]),
                         waits=x1r_w.take() + [(ls1[0], ls1[0].n),
                                               (ls1[1], ls1[1].n)],
                         incs=[(lr1, 16)])
                    lw = lr1.n
                    tk = P.op("vector", lambda v, t=xr, u=yt: v.tensor_tensor(
                        t[:], t[:], u[:], ALU.add),
                         waits=[(lr1, lw), (ve, ve.n)], incs=[(ve, 1)])
                    ytm_w[sp].r.extend(tk)
                    vw = ve.n
                    P.op("sync", lambda s, o=tok0, t=xr: s.dma_start(
                        out=out_d[o:o + 128, :], in_=t[:]),
                         waits=[(ve, vw)], incs=[(od, 16)])
                    x1r_w.r.append((od, od.n))
                xth_w.r.append((pe, pe.n))

            if stage == "fulldbg":
                vend, pend, send = ve.n, pe.n, se.n
                for (dst, srcb) in ((f_y, y_f), (f_cfm, c_fm), (f_xth, xt_half),
                                    (f_h1s, h1s), (f_cbc, cbc)):
                    P.op("sync", lambda s, d=dst, sr=srcb: s.dma_start(
                        out=d[:, :], in_=sr.ap()),
                         waits=[(ve, vend), (pe, pend), (se, send)],
                         incs=[(od, 16)])
            P.op("sync", lambda s: s.nop(), waits=[(od, od.n)])

        P.realize(nc)
    return nc


_NC_CACHE = {}


def _get_nc(stage="full"):
    if stage not in _NC_CACHE:
        _NC_CACHE[stage] = build_program(stage)
    return _NC_CACHE[stage]


def stage_inputs(inputs):
    f = lambda a: np.ascontiguousarray(np.asarray(a, dtype=np.float32))
    x = f(inputs["L0_attn_x"])
    wq = f(inputs["L0_attn_Wq"])
    wk = f(inputs["L0_attn_Wk"])
    wv = f(inputs["L0_attn_Wv"])
    wg = f(inputs["L0_attn_Wg"])
    wo = f(inputs["L0_attn_Wo"])
    gw = f(inputs["L0_ffn_gate_w"])
    gb = f(inputs["L0_ffn_gate_bias"])
    w1 = f(inputs["L0_ffn_w1"])
    w2 = f(inputs["L0_ffn_w2"])
    w3 = f(inputs["L0_ffn_w3"])
    sw1 = f(inputs["L0_ffn_sw1"])
    sw2 = f(inputs["L0_ffn_sw2"])
    sw3 = f(inputs["L0_ffn_sw3"])

    w1T = np.ascontiguousarray(
        np.concatenate([w1.transpose(0, 2, 1), sw1.T[None]], axis=0))
    w3T = np.ascontiguousarray(
        np.concatenate([w3.transpose(0, 2, 1), sw3.T[None]], axis=0))
    w2T = np.ascontiguousarray(
        np.concatenate([w2.transpose(0, 2, 1), sw2.T[None]], axis=0))
    gwT = np.ascontiguousarray(gw.T)
    gbias = np.ascontiguousarray(np.tile(gb[None, :], (128, 1)))

    in_maps = []
    for c in range(NCORE):
        b, s = c // 4, c % 4
        u = np.zeros((128, 6), dtype=np.float32)
        for j in range(3):
            if j < s:
                u[:, j] = 1.0
            else:
                u[:, 3 + j] = 1.0
        u[:, 3:6] = 1.0 - u[:, 0:3]
        in_maps.append({
            "xsl": np.ascontiguousarray(x[b, s * NT:(s + 1) * NT, :]),
            "wq": wq, "wk": wk, "wv": wv, "wg": wg, "wo": wo,
            "gwT": gwT, "gbias": gbias,
            "w1T": w1T, "w3T": w3T, "w2T": w2T,
            "um": np.ascontiguousarray(u),
        })
    return in_maps


def kernel(**inputs):
    in_maps = stage_inputs(inputs)
    nc = _get_nc("full")
    res = run_bass_kernel_spmd(nc, in_maps, core_ids=list(range(NCORE)))
    out = np.empty((2, 8192, D), dtype=np.float32)
    for c in range(NCORE):
        b, s = c // 4, c % 4
        out[b, s * NT:(s + 1) * NT, :] = res.results[c]["out"]
    return out


# version 25 (final)
# speedup vs baseline: 1.0148x; 1.0016x over previous
import numpy as np
from contextlib import ExitStack

import concourse.bass as bass
import concourse.mybir as mybir
from concourse.bass_utils import run_bass_kernel_spmd

F32 = mybir.dt.float32
BF16 = mybir.dt.bfloat16
I8 = mybir.dt.int8
AF = mybir.ActivationFunctionType
ALU = mybir.AluOpType
AXX = mybir.AxisListType.X

NCORE = 8
NT = 2048
D = 1024
DC = 8
H = 8
DK = 128
DV = 128
NG = 4
GT = 512
CH = 128
II = 2048
IC = 16
NE = 9
HALF = 1024


class _Sem:
    __slots__ = ("name", "n", "h")
    def __init__(self, name):
        self.name, self.n, self.h = name, 0, None


class _Buf:
    __slots__ = ("r",)
    def __init__(self):
        self.r = []
    def take(self):
        r, self.r = self.r, []
        return r


class _Plan:
    def __init__(self):
        self.sems = []
        self.blocks = [[]]
        self.self_sems = {}

    def sem(self, name):
        s = _Sem(name)
        self.sems.append(s)
        return s

    def barrier(self):
        if self.blocks[-1]:
            self.blocks.append([])

    def op(self, eng, fn, waits=(), incs=()):
        ss = self.self_sems.get(eng)
        waits = list(waits)
        incs = list(incs)
        if ss is not None:
            waits.append((ss, ss.n))
            if not any(s is ss for s, _ in incs):
                incs.append((ss, 1))
        wd = {}
        for s, v in waits:
            if v <= 0:
                continue
            assert v <= s.n, f"wait-ahead on {s.name}: {v} > {s.n}"
            wd[s] = max(wd.get(s, 0), v)
        self.blocks[-1].append((eng, fn, tuple(wd.items()), tuple(incs)))
        out = []
        for s, k in incs:
            s.n += k
            out.append((s, s.n))
        return out

    def realize(self, nc):
        with ExitStack() as es:
            for s in self.sems:
                s.h = es.enter_context(nc.semaphore(s.name))
            for ops in self.blocks:
                engs = []
                for o in ops:
                    if o[0] not in engs:
                        engs.append(o[0])
                with nc.Block() as blk:
                    for en in engs:
                        eng_ops = [o for o in ops if o[0] == en]

                        def body(eng, eng_ops=eng_ops):
                            for _, fn, waits, incs in eng_ops:
                                for s, v in waits:
                                    eng.wait_ge(s.h, v)
                                ins = fn(eng)
                                for s, k in incs:
                                    ins = ins.then_inc(s.h, k)
                        getattr(blk, en)(body)


def build_program(stage="full"):
    nc = bass.Bass("TRN2", target_bir_lowering=False)
    P = _Plan()
    ld = P.sem("ld")
    wd = P.sem("wd")
    lx = [P.sem("lxa"), P.sem("lxb")]
    lcc = P.sem("lcc")
    lx2 = [P.sem("lx2a"), P.sem("lx2b")]
    ls1 = [P.sem("ls1a"), P.sem("ls1b")]
    lxt = [P.sem("lxta"), P.sem("lxtb")]
    lxh = P.sem("lxh")
    lr1 = P.sem("lr1")
    lcr = P.sem("lcr")
    wdp = [P.sem("wda"), P.sem("wdb"), P.sem("wdc")]
    pe = P.sem("pe")
    ve = P.sem("ve")
    se = P.sem("se")
    ge = P.sem("ge")
    cc = P.sem("cc")
    od = P.sem("od")
    P.self_sems = {"vector": ve, "scalar": se}

    xsl = nc.dram_tensor("xsl", [NT, D], F32, kind="ExternalInput").ap()
    wq_d = nc.dram_tensor("wq", [D, D], F32, kind="ExternalInput").ap()
    wk_d = nc.dram_tensor("wk", [D, D], F32, kind="ExternalInput").ap()
    wv_d = nc.dram_tensor("wv", [D, D], F32, kind="ExternalInput").ap()
    wg_d = nc.dram_tensor("wg", [D, D], F32, kind="ExternalInput").ap()
    wo_d = nc.dram_tensor("wo", [D, D], F32, kind="ExternalInput").ap()
    gwT_d = nc.dram_tensor("gwT", [D, H], F32, kind="ExternalInput").ap()
    gbias_d = nc.dram_tensor("gbias", [128, 8], F32, kind="ExternalInput").ap()
    w1T_d = nc.dram_tensor("w1T", [NE, D, II], F32, kind="ExternalInput").ap()
    w3T_d = nc.dram_tensor("w3T", [NE, D, II], F32, kind="ExternalInput").ap()
    w2T_d = nc.dram_tensor("w2T", [NE, II, D], F32, kind="ExternalInput").ap()
    um_d = nc.dram_tensor("um", [128, 6], F32, kind="ExternalInput").ap()
    out_d = nc.dram_tensor("out", [NT, D], F32, kind="ExternalOutput").ap()

    if stage == "fulldbg":
        f_y = nc.dram_tensor("f_y", [128, DC * HALF], F32, kind="ExternalOutput").ap()
        f_cfm = nc.dram_tensor("f_cfm", [8, NT], F32, kind="ExternalOutput").ap()
        f_xth = nc.dram_tensor("f_xth", [128, DC * HALF], BF16, kind="ExternalOutput").ap()
        f_h1s = nc.dram_tensor("f_h1s", [128, IC * HALF], BF16, kind="ExternalOutput").ap()
        f_cbc = nc.dram_tensor("f_cbc", [128, HALF], F32, kind="ExternalOutput").ap()
    if stage == "x1":
        dbg_o = nc.dram_tensor("dbg_o", [128, H * NT], BF16, kind="ExternalOutput").ap()
        dbg_q = nc.dram_tensor("dbg_q", [128, H * NT], BF16, kind="ExternalOutput").ap()
        dbg_s = nc.dram_tensor("dbg_s", [128, H * 128], F32, kind="ExternalOutput").ap()
        dbg_p = nc.dram_tensor("dbg_p", [128, H * 16], F32, kind="ExternalOutput").ap()
        dbg_u = nc.dram_tensor("dbg_u", [128, GT], F32, kind="ExternalOutput").ap()
        dbg_qn = nc.dram_tensor("dbg_qn", [128, GT], BF16, kind="ExternalOutput").ap()
        dbg_kn = nc.dram_tensor("dbg_kn", [128, GT], BF16, kind="ExternalOutput").ap()
        dbg_xnf = nc.dram_tensor("dbg_xnf", [128, DC * GT], BF16, kind="ExternalOutput").ap()
        dbg_v = nc.dram_tensor("dbg_v", [128, 4 * 1024], BF16, kind="ExternalOutput").ap()
    x1sp = nc.dram_tensor("x1sp", [NT, D], F32).ap()
    xtsp = nc.dram_tensor("xtsp", [128, DC, NT], BF16).ap()
    ccin = nc.dram_tensor("ccin", [128, 1032], F32)
    ccout = nc.dram_tensor("ccout", [512, 1032], F32)

    es = ExitStack()
    sb = lambda n, s, d=F32: es.enter_context(nc.sbuf_tensor(n, s, d))

    with es:
        ident_f = sb("ident_f", [128, 128])
        ident_b = sb("ident_b", [128, 128], BF16)
        mask_f = sb("mask_f", [128, 128])
        ones_f = sb("ones_f", [128, 128])
        zeros_b = sb("zeros_b", [128, 128], BF16)
        z512 = sb("z512", [128, 512])
        zcol = sb("zcol", [128, 1])
        epscol = sb("epscol", [128, 1])
        ones1_b = sb("ones1_b", [1, 128], BF16)
        ones1_f = sb("ones1_f", [1, 128])
        onesk_b = sb("onesk_b", [128, 1], BF16)
        gbias_s = sb("gbias_s", [128, 8])
        um_s = sb("um_s", [128, 6])
        sqdump = sb("sqdump", [128, 1024], BF16)
        cols = sb("cols", [128, 16])

        pb = [es.enter_context(nc.psum_tensor(f"pb{i}", [128, 512], F32))
              for i in range(7)]
        pbt_b = es.enter_context(nc.psum_tensor("pbt_b", [128, 128], BF16))
        bankw = [_Buf() for _ in range(8)]

        P.op("gpsimd", lambda g: g.memset(ones_f[:], 1.0), incs=[(ge, 1)])

        P.op("gpsimd", lambda g: g.memset(zeros_b[:], 0.0), incs=[(ge, 1)])
        P.op("gpsimd", lambda g: g.memset(z512[:], 0.0), incs=[(ge, 1)])
        P.op("gpsimd", lambda g: g.memset(zcol[:], 0.0), incs=[(ge, 1)])
        P.op("gpsimd", lambda g: g.memset(epscol[:], 1e-5), incs=[(ge, 1)])
        P.op("gpsimd", lambda g: g.memset(ones1_b[:], 1.0), incs=[(ge, 1)])
        P.op("gpsimd", lambda g: g.memset(ones1_f[:], 1.0), incs=[(ge, 1)])
        P.op("gpsimd", lambda g: g.memset(onesk_b[:], 1.0), incs=[(ge, 1)])
        P.op("gpsimd", lambda g: g.affine_select(
            ident_f[:], ones_f[:], [[1, 128]], ALU.is_equal, 0.0,
            channel_multiplier=-1), waits=[(ge, 8)], incs=[(ge, 1)])
        P.op("gpsimd", lambda g: g.affine_select(
            mask_f[:], ones_f[:], [[1, 128]], ALU.is_ge, 0.0,
            channel_multiplier=-1), waits=[(ge, 9)], incs=[(ge, 1)])
        P.op("vector", lambda v: v.tensor_copy(ident_b[:], ident_f[:]),
             waits=[(ge, 10)], incs=[(ve, 1)])
        P.op("vector", lambda v: v.memset(pb[0][64:128, 0:64], 0.0),
             incs=[(ve, 1)])
        P.op("sync", lambda s: s.dma_start(out=gbias_s[:], in_=gbias_d[:, :]),
             incs=[(ld, 16)])
        P.op("sync", lambda s: s.dma_start(out=um_s[:], in_=um_d[:, :]),
             incs=[(ld, 16)])
        ld_base = ld.n

        col_i = [0]
        def col():
            c = col_i[0] % 16
            col_i[0] += 1
            return cols[:, c:c + 1]

        def mk_dma(eng_dst, eng_src):
            return lambda e: e.dma_start(out=eng_dst, in_=eng_src)

        def ln_tile(src, dst, extra_waits=()):
            sc, mc, lc, rc = col(), col(), col(), col()
            P.op("vector", lambda v: v.reduce_sum(sc, src, axis=AXX),
                 waits=list(extra_waits), incs=[(ve, 1)])
            P.op("vector", lambda v: v.tensor_scalar(
                mc, sc, 1.0 / D, None, ALU.mult),
                 waits=[(ve, ve.n)], incs=[(ve, 1)])
            P.op("vector", lambda v: v.tensor_scalar(
                dst, src, mc, None, ALU.subtract),
                 waits=[(ve, ve.n)], incs=[(ve, 1)])
            t0 = ve.n
            P.op("scalar", lambda s: s.activation(
                out=sqdump[:], in_=dst, func=AF.Square, accum_out=lc),
                 waits=[(ve, t0)], incs=[(se, 1)])
            P.op("scalar", lambda s: s.activation(
                out=lc, in_=lc, func=AF.Ln, scale=1.0 / D, bias=epscol[:]),
                 waits=[(se, se.n)], incs=[(se, 1)])
            P.op("scalar", lambda s: s.activation(
                out=rc, in_=lc, func=AF.Exp, scale=-0.5),
                 waits=[(se, se.n)], incs=[(se, 1)])
            t1 = se.n
            tick = P.op("vector", lambda v: v.tensor_scalar(
                dst, dst, rc, None, ALU.mult), waits=[(se, t1)], incs=[(ve, 1)])
            return tick

        with ExitStack() as ctx1:
            sb1 = lambda n, s, d=F32: ctx1.enter_context(nc.sbuf_tensor(n, s, d))
            o_all = sb1("o_all", [128, H * NT], BF16)
            qh_all = sb1("qh_all", [128, H * NT], BF16)
            S_f = sb1("S_f", [128, H * 128])
            S_b = sb1("S_b", [128, H * 128], BF16)
            P_all = sb1("P_all", [128, H * 16])
            Pcur = sb1("Pcur", [128, H])
            Dsl = sb1("Dsl", [128, H])
            o_w = _Buf()
            qh_w = _Buf()

            P.op("vector", lambda v: v.memset(S_f[:], 0.0), incs=[(ve, 1)])
            P.op("vector", lambda v: v.memset(S_b[:], 0.0), incs=[(ve, 1)])
            P.op("vector", lambda v: v.memset(Pcur[:], 1.0), incs=[(ve, 1)])
            sb_tick = {h: 0 for h in range(H)}

            with ExitStack() as ctxA:
                sba = lambda n, s, d=F32: ctxA.enter_context(nc.sbuf_tensor(n, s, d))
                wq_s = sba("wq_s", [128, DC * D], BF16)
                wk_s = sba("wk_s", [128, DC * D], BF16)
                wv_s = sba("wv_s", [128, DC * D], BF16)
                wg_s = sba("wg_s", [128, DC * D], BF16)
                x_st = [sba(f"x_st{i}", [128, 1024]) for i in range(2)]
                xn_st = [sba(f"xn_st{i}", [128, 1024]) for i in range(2)]
                xnf = sba("xnf", [128, DC * GT], BF16)
                v_tm = sba("v_tm", [128, 4 * 1024], BF16)
                qn = [sba(f"qn{i}", [128, GT], BF16) for i in range(4)]
                kn = [sba(f"kn{i}", [128, GT], BF16) for i in range(4)]
                t_st = sba("t_st", [128, GT])
                u_st = sba("u_st", [128, GT])
                U_st = [sba(f"U_st{i}", [128, GT]) for i in range(4)]
                sq_b = sba("sq_b", [128, GT], BF16)
                ssq_s = u_st
                rn_b = sba("rn_b", [1, GT], BF16)
                d1m = sba("d1m", [128, GT])
                e1a = sba("e1a", [128, GT])
                qt_a = [sba(f"qt_{i}", [128, GT], BF16) for i in range(2)]
                kt_a = [sba(f"kt_{i}", [128, GT], BF16) for i in range(2)]
                kh_a = [sba(f"kh_{i}", [128, GT], BF16) for i in range(2)]
                scm = [sba(f"scm{i}", [128, 8]) for i in range(2)]
                am_p = [sba(f"am_{i}", [128, 128], BF16) for i in range(2)]
                khtm_p = [sba(f"khtm_{i}", [128, 128], BF16) for i in range(2)]
                e12_w = [_Buf(), _Buf(), _Buf(), _Buf()]
                x_w = [_Buf(), _Buf()]
                xn_w = [_Buf(), _Buf()]
                xnf_w = _Buf()
                vtm_w = _Buf()
                head_w = _Buf()

                def wload(dst, src):
                    s3 = src.rearrange("(dc p) n -> p dc n", p=128)
                    return P.op("gpsimd",
                                lambda g, d=dst, s=s3: g.dma_start(out=d.ap().rearrange(
                                    "p (dc n) -> p dc n", dc=DC), in_=s),
                                incs=[(wd, 16)])
                wload(wq_s, wq_d)
                wload(wk_s, wk_d)
                wload(wv_s, wv_d)
                wload(wg_s, wg_d)
                wd_attn = wd.n

                for g in range(NG):
                    for t in range(4):
                        bi = t % 2
                        tok0 = g * GT + t * 128
                        P.op("sync", lambda s, b=bi, o=tok0: s.dma_start(
                            out=x_st[b][:], in_=xsl[o:o + 128, :]),
                             waits=x_w[bi].take() + xn_w[bi].take(),
                             incs=[(lx[bi], 16)])
                        lw = lx[bi].n
                        tick = ln_tile(x_st[bi][:], xn_st[bi][:],
                                       extra_waits=[(lx[bi], lw)])
                        x_w[bi].r.extend(tick)
                        for dc in range(DC):
                            bk = dc % 2
                            P.op("tensor", lambda te, b=bi, d=dc, k=bk: te.transpose(
                                pb[k][:, 0:128], xn_st[b][:, d * 128:(d + 1) * 128],
                                ident_f[:]),
                                 waits=[(ve, ve.n)] + bankw[bk].take(),
                                 incs=[(pe, 1)])
                            pw = pe.n
                            tk = P.op("vector", lambda v, b=bi, d=dc, k=bk, tt=t: v.tensor_copy(
                                xnf[:, d * GT + tt * 128:d * GT + tt * 128 + 128],
                                pb[k][:, 0:128]),
                                 waits=[(pe, pw)] + xnf_w.take(), incs=[(ve, 1)])
                            bankw[bk].r.extend(tk)
                            xn_w[bi].r.extend(tk)
                    xnf_done = ve.n

                    for t in range(4):
                        for hf in range(2):
                            bk = 2 + hf
                            for dc in range(DC):
                                P.op("tensor", lambda te, tt=t, h2=hf, d=dc, k=bk: te.matmul(
                                    pb[k][:, 0:512],
                                    xnf[:, d * GT + tt * 128:d * GT + tt * 128 + 128],
                                    wv_s[:, d * D + h2 * 512:d * D + h2 * 512 + 512],
                                    start=(d == 0), stop=(d == DC - 1)),
                                     waits=([(ve, xnf_done), (wd, wd_attn)]
                                            + bankw[bk].take()) if dc == 0 else [],
                                     incs=[(pe, 1)] if dc == DC - 1 else [])
                            pw = pe.n
                            tk = P.op("vector", lambda v, tt=t, h2=hf, k=bk: v.tensor_copy(
                                v_tm[:, tt * 1024 + h2 * 512:tt * 1024 + h2 * 512 + 512],
                                pb[k][:, 0:512]),
                                 waits=[(pe, pw)] + vtm_w.take(), incs=[(ve, 1)])
                            bankw[bk].r.extend(tk)

                    U_done = {}
                    scale_done = {}

                    def emit_proj(h):
                        par = h % 4
                        for dc in range(DC):
                            P.op("tensor", lambda te, d=dc, hh=h: te.matmul(
                                pb[4][:, 0:512],
                                wg_s[:, d * D + hh * 128:d * D + hh * 128 + 128],
                                xnf[:, d * GT:(d + 1) * GT],
                                start=(d == 0), stop=(d == DC - 1)),
                                 waits=([(ve, xnf_done)] + bankw[4].take())
                                 if dc == 0 else [],
                                 incs=[(pe, 1)] if dc == DC - 1 else [])
                        pw = pe.n
                        P.op("scalar", lambda s: s.activation(
                            out=t_st[:], in_=pb[4][:, 0:512], func=AF.Exp,
                            scale=-1.0), waits=[(pe, pw)], incs=[(se, 1)])
                        tk = P.op("scalar", lambda s: s.activation(
                            out=u_st[:], in_=t_st[:], func=AF.Ln, bias=1.0),
                             incs=[(se, 1)])
                        bankw[4].r.extend(tk)
                        sw = se.n
                        P.op("vector", lambda v, p=par: v.tensor_tensor_scan(
                            U_st[p][:], u_st[:], z512[:], 0.0, ALU.add, ALU.add),
                             waits=[(se, sw)], incs=[(ve, 1)])
                        U_done[h] = ve.n

                        for (wmat, dstn, bq) in ((wq_s, qn[par], 2 + (h % 2)),
                                                 (wk_s, kn[par], 3 - (h % 2))):
                            for dc in range(DC):
                                P.op("tensor", lambda te, d=dc, hh=h, w=wmat, k=bq: te.matmul(
                                    pb[k][:, 0:512],
                                    w[:, d * D + hh * 128:d * D + hh * 128 + 128],
                                    xnf[:, d * GT:(d + 1) * GT],
                                    start=(d == 0), stop=(d == DC - 1)),
                                     waits=([(ve, xnf_done)] + bankw[bq].take())
                                     if dc == 0 else [],
                                     incs=[(pe, 1)] if dc == DC - 1 else [])
                            pw = pe.n
                            P.op("scalar", lambda s, k=bq: s.activation(
                                out=sq_b[:], in_=pb[k][:, 0:512], func=AF.Square),
                                 waits=[(pe, pw)] + bankw[5].take(), incs=[(se, 1)])
                            sw = se.n
                            P.op("tensor", lambda te: te.matmul(
                                pb[5][0:1, 0:512], onesk_b[:], sq_b[:],
                                start=True, stop=True),
                                 waits=[(se, sw)], incs=[(pe, 1)])
                            pw = pe.n
                            tk = P.op("vector", lambda v: v.tensor_scalar_max(
                                ssq_s[0:1, :], pb[5][0:1, 0:512], 1e-24),
                                 waits=[(pe, pw)], incs=[(ve, 1)])
                            vw = ve.n
                            P.op("scalar", lambda s: s.activation(
                                out=ssq_s[0:1, :], in_=ssq_s[0:1, :], func=AF.Ln),
                                 waits=[(ve, vw)], incs=[(se, 1)])
                            P.op("scalar", lambda s: s.activation(
                                out=rn_b[0:1, :], in_=ssq_s[0:1, :], func=AF.Exp,
                                scale=-0.5),
                                 waits=[(se, se.n)], incs=[(se, 1)])
                            sw = se.n
                            P.op("tensor", lambda te: te.matmul(
                                pb[5][:, 0:512], ones1_b[0:1, :], rn_b[0:1, :],
                                start=True, stop=True),
                                 waits=[(se, sw)], incs=[(pe, 1)])
                            pw = pe.n
                            P.op("vector", lambda v, dst=dstn, k=bq: v.tensor_copy(
                                dst[:], pb[k][:, 0:512]),
                                 waits=[(pe, pw)], incs=[(ve, 1)])
                            tk = P.op("vector", lambda v, dst=dstn: v.tensor_tensor(
                                dst[:], dst[:], pb[5][:, 0:512], ALU.mult),
                                 waits=[(ve, ve.n)], incs=[(ve, 1)])
                            bankw[5].r.extend(tk)
                            bankw[bq].r.extend(tk)

                    def emit_scalings(h):
                        par = h % 4
                        sl = h % 2
                        ew = e12_w[sl].take()
                        for c in range(4):
                            P.op("vector", lambda v, c0=c * 128, p=par: v.tensor_scalar(
                                d1m[:, c0:c0 + 128], U_st[p][:, c0:c0 + 128],
                                U_st[p][:, c0 + 63:c0 + 64], None, ALU.subtract),
                                 waits=[(ve, U_done[h])] + (ew if c == 0 else []),
                                 incs=[(ve, 1)])
                        for c in range(4):
                            up = (zcol[:, 0:1] if c == 0
                                  else U_st[par][:, c * 128 - 1:c * 128])
                            P.op("vector", lambda v, c2=c, u=up, p=par, s2=sl: v.tensor_tensor(
                                scm[s2][:, c2:c2 + 1], u,
                                U_st[p][:, c2 * 128 + 63:c2 * 128 + 64],
                                ALU.subtract), incs=[(ve, 1)])
                        vw = ve.n
                        P.op("scalar", lambda s, s2=sl: s.activation(
                            out=scm[s2][:, 0:4], in_=scm[s2][:, 0:4], func=AF.Exp),
                             waits=[(ve, vw)], incs=[(se, 1)])
                        P.op("scalar", lambda s: s.activation(
                            out=e1a[:], in_=d1m[:], func=AF.Exp, scale=-1.0),
                             incs=[(se, 1)])
                        P.op("scalar", lambda s: s.activation(
                            out=d1m[:], in_=d1m[:], func=AF.Exp),
                             waits=[(se, se.n)], incs=[(se, 1)])
                        sw = se.n
                        P.op("vector", lambda v, p=par, s2=sl: v.tensor_tensor(
                            qt_a[s2][:], qn[p][:], e1a[:], ALU.mult),
                             waits=[(se, sw)], incs=[(ve, 1)])
                        P.op("vector", lambda v, p=par, s2=sl: v.tensor_tensor(
                            kt_a[s2][:], kn[p][:], d1m[:], ALU.mult),
                             incs=[(ve, 1)])
                        for c in range(4):
                            c0 = c * 128
                            qh_ap = qh_all[:, h * NT + g * GT + c0:
                                           h * NT + g * GT + c0 + 128]
                            P.op("vector", lambda v, qa=qh_ap, c0=c0, s2=sl: v.tensor_scalar(
                                qa, qt_a[s2][:, c0:c0 + 128],
                                scm[s2][:, c0 // 128:c0 // 128 + 1], None, ALU.mult),
                                 waits=qh_w.take(), incs=[(ve, 1)])
                            P.op("vector", lambda v, c0=c0, s2=sl: v.tensor_scalar(
                                kh_a[s2][:, c0:c0 + 128], kt_a[s2][:, c0:c0 + 128],
                                e1a[:, c0 + 127:c0 + 128], None, ALU.mult),
                                 incs=[(ve, 1)])
                            P.op("vector", lambda v, c2=c, s2=sl: v.tensor_scalar(
                                scm[s2][:, 4 + c2:5 + c2], e1a[:, c2 * 128 + 127:
                                                              c2 * 128 + 128],
                                scm[s2][:, c2:c2 + 1], None, ALU.mult),
                                 incs=[(ve, 1)])
                        scale_done[h] = ve.n

                    def chunk_body(h, c):
                        cp = h % 2
                        am_b, khtm_b = am_p[cp], khtm_p[cp]
                        nch = g * 4 + c
                        c0 = c * 128
                        qh_ap = qh_all[:, h * NT + g * GT + c0:
                                       h * NT + g * GT + c0 + 128]
                        qt_s = qt_a[cp][:, c0:c0 + 128]
                        kt_s = kt_a[cp][:, c0:c0 + 128]
                        kh_s = kh_a[cp][:, c0:c0 + 128]
                        dcol = scm[cp][:, 4 + c:5 + c]
                        bw0 = bankw[0].take()
                        P.op("tensor", lambda te, t=kt_s, u=qt_s: te.matmul(
                            pb[0][0:64, 0:64], t[:, 0:64], u[:, 0:64],
                            start=True, stop=True),
                             waits=[(ve, scale_done[h])] + bw0, incs=[(pe, 1)])
                        P.op("tensor", lambda te, t=kt_s, u=qt_s: te.matmul(
                            pb[0][64:128, 64:128], t[:, 64:128],
                            u[:, 64:128], start=True, stop=True),
                             incs=[(pe, 1)])
                        P.op("tensor", lambda te, t=kt_s, u=qt_s: te.matmul(
                            pb[0][0:64, 64:128], t[:, 0:64],
                            u[:, 64:128], start=True, stop=True),
                             incs=[(pe, 1)])
                        P.op("tensor", lambda te, t=kh_s: te.transpose(
                            pbt_b[:, 0:128], t, ident_b[:]),
                             waits=bankw[7].take(), incs=[(pe, 1)])
                        pw = pe.n
                        tk = P.op("vector", lambda v, t=am_b: v.tensor_tensor(
                            t[:], pb[0][:, 0:128], mask_f[:], ALU.mult),
                             waits=[(pe, pw)], incs=[(ve, 1)])
                        bankw[0].r.extend(tk)
                        tk = P.op("scalar", lambda s, t=khtm_b: s.copy(
                            t[:], pbt_b[:, 0:128]),
                             waits=[(pe, pw)], incs=[(se, 1)])
                        bankw[7].r.extend(tk)
                        kh_se = se.n
                        vw = ve.n
                        vsl = v_tm[:, c * 1024 + h * 128:c * 1024 + h * 128 + 128]
                        P.op("tensor", lambda te, hh=h, qa=qh_ap: te.matmul(
                            pb[6][:, 0:128],
                            S_b[:, hh * 128:hh * 128 + 128],
                            qa, start=True, stop=False),
                             waits=[(ve, vw), (se, sb_tick[h])] + bankw[6].take(),
                             incs=[(pe, 1)])
                        P.op("tensor", lambda te, vs=vsl, t=am_b: te.matmul(
                            pb[6][:, 0:128], vs, t[:],
                            start=False, stop=True), incs=[(pe, 1)])
                        o_pe = pe.n
                        P.op("tensor", lambda te, vs=vsl, t=khtm_b: te.matmul(
                            pb[1][:, 0:128], t[:], vs,
                            start=True, stop=True),
                             waits=[(se, kh_se)] + bankw[1].take(),
                             incs=[(pe, 1)])
                        m_pe = pe.n
                        o_ap = o_all[:, h * NT + g * GT + c0:
                                     h * NT + g * GT + c0 + 128]
                        tk = P.op("vector", lambda v, oa=o_ap: v.tensor_copy(
                            oa, pb[6][:, 0:128]),
                             waits=[(pe, o_pe)] + o_w.take(), incs=[(ve, 1)])
                        bankw[6].r.extend(tk)
                        pidx = P_all[:, h * 16 + nch:h * 16 + nch + 1]
                        P.op("vector", lambda v, pi=pidx, hh=h: v.tensor_copy(
                            pi, Pcur[:, hh:hh + 1]), incs=[(ve, 1)])
                        tk = P.op("vector", lambda v, hh=h, dc2=dcol:
                                  v.scalar_tensor_tensor(
                                      S_f[:, hh * 128:hh * 128 + 128],
                                      S_f[:, hh * 128:hh * 128 + 128],
                                      dc2, pb[1][:, 0:128],
                                      ALU.mult, ALU.add),
                                  waits=[(pe, m_pe)], incs=[(ve, 1)])
                        bankw[1].r.extend(tk)
                        sv = ve.n
                        tk = P.op("scalar", lambda s, hh=h: s.copy(
                            S_b[:, hh * 128:hh * 128 + 128],
                            S_f[:, hh * 128:hh * 128 + 128]),
                             waits=[(ve, sv)], incs=[(se, 1)])
                        sb_tick[h] = se.n
                        tk2 = P.op("vector", lambda v, hh=h, dc2=dcol: v.tensor_tensor(
                            Pcur[:, hh:hh + 1], Pcur[:, hh:hh + 1], dc2,
                            ALU.mult), incs=[(ve, 1)])
                        if c == 3:
                            e12_w[h % 2].r.extend(tk2)

                    def chunks_pair(hA, hB):
                        emit_scalings(hA)
                        chunk_body(hA, 0)
                        emit_scalings(hB)
                        for c in range(3):
                            chunk_body(hA, c + 1)
                            chunk_body(hB, c)
                        chunk_body(hB, 3)

                    for i in range(4):
                        emit_proj(2 * i)
                        emit_proj(2 * i + 1)
                        if i >= 1:
                            chunks_pair(2 * i - 2, 2 * i - 1)
                    chunks_pair(6, 7)
                P.op("vector", lambda v: v.tensor_copy(Dsl[:], Pcur[:]),
                     waits=[(ve, ve.n)], incs=[(ve, 1)])
                if stage == "x1":
                    vend = ve.n
                    for (dst, srcb) in ((dbg_o, o_all), (dbg_q, qh_all),
                                        (dbg_s, S_f), (dbg_p, P_all),
                                        (dbg_u, U_st[3]), (dbg_qn, qn[3]),
                                        (dbg_kn, kn[3]), (dbg_xnf, xnf),
                                        (dbg_v, v_tm)):
                        P.op("sync", lambda s, d=dst, sr=srcb: s.dma_start(
                            out=d[:, :], in_=sr[:]),
                             waits=[(ve, vend)], incs=[(od, 16)])

            P.barrier()
            with ExitStack() as ctxC:
                sbc = lambda n, s, d=F32: ctxC.enter_context(nc.sbuf_tensor(n, s, d))
                sbb = lambda n, s, d=F32: ctxC.enter_context(nc.sbuf_tensor(n, s, d))
                ccst = sbb("ccst", [128, 1032])
                gth = sbb("gth", [128, 3 * 1032])
                Sst_f = sbb("Sst_f", [128, H * 128])
                Sst_b = sbb("Sst_b", [128, H * 128], BF16)
                DmS = sbb("DmS", [128, 8])
                mm_scr = sbb("mm_scr", [128, 128])
                qp_b = [sbb(f"qp_b{i}", [128, 512], BF16) for i in range(2)]

                P.op("vector", lambda v: v.tensor_copy(ccst[:, 0:1024], S_f[:]),
                     incs=[(ve, 1)])
                P.op("vector", lambda v: v.tensor_copy(ccst[:, 1024:1032], Dsl[:]),
                     waits=[(ve, ve.n)], incs=[(ve, 1)])
                vw = ve.n
                P.op("sync", lambda s: s.dma_start(out=ccin.ap()[:, :], in_=ccst[:]),
                     waits=[(ve, vw)], incs=[(lcc, 16)])
                lw = lcc.n
                P.op("gpsimd", lambda g: g.collective_compute(
                    "AllGather", ALU.bypass,
                    replica_groups=[[0, 1, 2, 3], [4, 5, 6, 7]],
                    ins=[ccin.ap().opt()], outs=[ccout.ap().opt()]),
                     waits=[(lcc, lw)], incs=[(cc, 1)])
                for j in range(3):
                    P.op("sync", lambda s, jj=j: s.dma_start(
                        out=gth[:, jj * 1032:(jj + 1) * 1032],
                        in_=ccout.ap()[jj * 128:(jj + 1) * 128, :]),
                         waits=[(cc, 1)], incs=[(lcc, 16)])
                lw = lcc.n
                P.op("vector", lambda v: v.memset(Sst_f[:], 0.0),
                     waits=[(lcc, lw), (ld, ld_base)], incs=[(ve, 1)])
                for j in range(3):
                    P.op("vector", lambda v, jj=j: v.tensor_scalar(
                        DmS[:], gth[:, jj * 1032 + 1024:jj * 1032 + 1032],
                        um_s[:, jj:jj + 1], um_s[:, 3 + jj:4 + jj],
                        ALU.mult, ALU.add), incs=[(ve, 1)])
                    dms_tick = ve.n
                    for h in range(H):
                        P.op("vector", lambda v, jj=j, hh=h: v.tensor_scalar(
                            mm_scr[:],
                            gth[:, jj * 1032 + hh * 128:jj * 1032 + hh * 128 + 128],
                            um_s[:, jj:jj + 1], None, ALU.mult), incs=[(ve, 1)])
                        P.op("vector", lambda v, hh=h: v.scalar_tensor_tensor(
                            Sst_f[:, hh * 128:hh * 128 + 128],
                            Sst_f[:, hh * 128:hh * 128 + 128],
                            DmS[:, hh:hh + 1], mm_scr[:],
                            ALU.mult, ALU.add),
                             waits=[(ve, dms_tick)] if h == 0 else [],
                             incs=[(ve, 1)])
                P.op("vector", lambda v: v.tensor_copy(Sst_b[:], Sst_f[:]),
                     incs=[(ve, 1)])
                corr_done = {}
                for nb in range(4):
                    for h in range(H):
                        np_ = 4 + (h % 2)
                        bw = bankw[np_].take()
                        for j in range(4):
                            n = nb * 4 + j
                            P.op("vector", lambda v, hh=h, nn=n, jj=j,
                                 q2=qp_b[h % 2]: v.tensor_scalar(
                                     q2[:, jj * 128:jj * 128 + 128],
                                     qh_all[:, hh * NT + nn * 128:
                                            hh * NT + nn * 128 + 128],
                                     P_all[:, hh * 16 + nn:hh * 16 + nn + 1],
                                     None, ALU.mult),
                                 waits=bw if j == 0 else [], incs=[(ve, 1)])
                        vw = ve.n
                        P.op("tensor", lambda te, hh=h, k=np_, q2=qp_b[h % 2]: te.matmul(
                            pb[k][:, 0:512], Sst_b[:, hh * 128:hh * 128 + 128],
                            q2[:], start=True, stop=True),
                             waits=[(ve, vw)], incs=[(pe, 1)])
                        pw = pe.n
                        oa = o_all[:, h * NT + nb * 512:h * NT + nb * 512 + 512]
                        tk = P.op("vector", lambda v, o2=oa, k=np_: v.tensor_tensor(
                            o2, o2, pb[k][:, 0:512], ALU.add),
                             waits=[(pe, pw)], incs=[(ve, 1)])
                        bankw[np_].r.extend(tk)
                    corr_done[nb] = ve.n
                wo_s = sbc("wo_s", [128, DC * D], BF16)
                am_f = sbc("am_f", [128, DC * 512])
                amtm = [sbc(f"amtm{i}", [128, 1024]) for i in range(2)]
                x_st2 = [sbc(f"x_st2_{i}", [128, 1024]) for i in range(2)]
                x1_st = [sbc(f"x1_st{i}", [128, 1024]) for i in range(2)]
                xt_st = [sbc(f"xt_st{i}", [128, 1024]) for i in range(2)]
                xtf_st = [sbc(f"xtf_st{i}", [128, DC * 128], BF16) for i in range(2)]
                amf_w = _Buf()
                amtm_w = [_Buf(), _Buf()]
                x1_w = [_Buf(), _Buf()]
                xt_w = [_Buf(), _Buf()]
                xtf_w = [_Buf(), _Buf()]
                x2_w = [_Buf(), _Buf()]

                s3 = wo_d.rearrange("(hc p) n -> p hc n", p=128)
                P.op("gpsimd", lambda g: g.dma_start(
                    out=wo_s.ap().rearrange("p (hc n) -> p hc n", hc=DC), in_=s3),
                     incs=[(wd, 16)])
                wd_wo = wd.n

                for tt in range(4):
                    for dc in range(DC):
                        bk = dc % 2
                        for h in range(H):
                            P.op("tensor", lambda te, hh=h, d=dc, t4=tt, k=bk: te.matmul(
                                pb[k][:, 0:512],
                                wo_s[:, hh * D + d * 128:hh * D + d * 128 + 128],
                                o_all[:, hh * NT + t4 * 512:hh * NT + t4 * 512 + 512],
                                start=(hh == 0), stop=(hh == H - 1)),
                                 waits=([(wd, wd_wo), (ve, corr_done[tt])]
                                        + bankw[bk].take()) if h == 0 else [],
                                 incs=[(pe, 1)] if h == H - 1 else [])
                        pw = pe.n
                        tk = P.op("vector", lambda v, d=dc, k=bk: v.tensor_copy(
                            am_f[:, d * 512:d * 512 + 512], pb[k][:, 0:512]),
                             waits=[(pe, pw)] + (amf_w.take() if dc == 0 else []),
                             incs=[(ve, 1)])
                        bankw[bk].r.extend(tk)
                    amf_done = ve.n
                    def am_part(sub):
                        sp = sub % 2
                        tok0 = tt * 512 + sub * 128
                        amt, xs2, x1s = amtm[sp], x_st2[sp], x1_st[sp]
                        for dc in range(DC):
                            bk = 2 + dc % 2
                            P.op("tensor", lambda te, d=dc, s4=sub, k=bk: te.transpose(
                                pb[k][:, 0:128],
                                am_f[:, d * 512 + s4 * 128:d * 512 + s4 * 128 + 128],
                                ident_f[:]),
                                 waits=[(ve, amf_done)] + bankw[bk].take(),
                                 incs=[(pe, 1)])
                            pw = pe.n
                            tk = P.op("vector", lambda v, d=dc, k=bk, t=amt: v.tensor_copy(
                                t[:, d * 128:d * 128 + 128], pb[k][:, 0:128]),
                                 waits=[(pe, pw)] + (amtm_w[sp].take()
                                                     if dc == 0 else []),
                                 incs=[(ve, 1)])
                            bankw[bk].r.extend(tk)
                        amf_w.r.append((ve, ve.n))
                        tick = ln_tile(amt[:], amt[:])
                        amtm_w[sp].r.extend(tick)
                        P.op("sync", lambda s, o=tok0, t=xs2: s.dma_start(
                            out=t[:], in_=xsl[o:o + 128, :]),
                             waits=x2_w[sp].take(), incs=[(lx2[sp], 16)])
                        lw = lx2[sp].n
                        tk = P.op("vector", lambda v, t=x1s, u=xs2, w2=amt: v.tensor_tensor(
                            t[:], u[:], w2[:], ALU.add),
                             waits=[(lx2[sp], lw), (ve, ve.n)] + x1_w[sp].take(),
                             incs=[(ve, 1)])
                        x2_w[sp].r.extend(tk)
                        amtm_w[sp].r.extend(tk)
                        vw = ve.n
                        P.op("sync", lambda s, o=tok0, t=x1s: s.dma_start(
                            out=x1sp[o:o + 128, :], in_=t[:]),
                             waits=[(ve, vw)], incs=[(ls1[sp], 16)])
                        x1_w[sp].r.append((ls1[sp], ls1[sp].n))
                        return vw

                    def xt_part(sub, x1vw):
                        sp = sub % 2
                        tok0 = tt * 512 + sub * 128
                        x1s, xts, xtf = x1_st[sp], xt_st[sp], xtf_st[sp]
                        P.op("scalar", lambda s, t=xts, u=x1s: s.copy(t[:], u[:]),
                             waits=[(ve, x1vw)] + xt_w[sp].take(), incs=[(se, 1)])
                        x1_w[sp].r.append((se, se.n))
                        tick = ln_tile(xts[:], xts[:], extra_waits=[(se, se.n)])
                        for dc in range(DC):
                            bk = 2 + dc % 2
                            P.op("tensor", lambda te, d=dc, k=bk, t=xts: te.transpose(
                                pb[k][:, 0:128], t[:, d * 128:d * 128 + 128],
                                ident_f[:]),
                                 waits=[(ve, ve.n)] + bankw[bk].take(),
                                 incs=[(pe, 1)])
                            pw = pe.n
                            tk = P.op("vector", lambda v, d=dc, k=bk, t=xtf: v.tensor_copy(
                                t[:, d * 128:d * 128 + 128], pb[k][:, 0:128]),
                                 waits=[(pe, pw)] + (xtf_w[sp].take()
                                                     if dc == 0 else []),
                                 incs=[(ve, 1)])
                            bankw[bk].r.extend(tk)
                            xt_w[sp].r.extend(tk)
                        vw = ve.n
                        P.op("sync", lambda s, o=tok0, t=xtf: s.dma_start(
                            out=xtsp[:, :, o:o + 128],
                            in_=t.ap().rearrange("p (dc n) -> p dc n", dc=DC)),
                             waits=[(ve, vw)], incs=[(lxt[sp], 16)])
                        xtf_w[sp].r.append((lxt[sp], lxt[sp].n))

                    x1vws = {}
                    for sub in range(4):
                        x1vws[sub] = am_part(sub)
                        if sub >= 1:
                            xt_part(sub - 1, x1vws[sub - 1])
                    xt_part(3, x1vws[3])

        P.barrier()
        if stage == "x1":
            with ExitStack() as ctxX:
                xo = ctxX.enter_context(nc.sbuf_tensor("xo", [128, 1024], F32))
                first = [(ls1[0], ls1[0].n), (ls1[1], ls1[1].n)]
                for t in range(16):
                    P.op("sync", lambda s, o=t * 128: s.dma_start(
                        out=xo[:], in_=x1sp[o:o + 128, :]),
                         waits=[(od, od.n)] + first, incs=[(lr1, 16)])
                    first = []
                    lw = lr1.n
                    P.op("sync", lambda s, o=t * 128: s.dma_start(
                        out=out_d[o:o + 128, :], in_=xo[:]),
                         waits=[(lr1, lw)], incs=[(od, 16)])
                P.op("sync", lambda s: s.nop(), waits=[(od, od.n)])
            P.realize(nc)
            return nc

        with ExitStack() as ctxD:
            sbd = lambda n, s, d=F32: ctxD.enter_context(nc.sbuf_tensor(n, s, d))
            xt_half = sbd("xt_half", [128, DC * HALF], BF16)
            h1s = sbd("h1s", [128, IC * HALF], BF16)
            y_f = sbd("y_f", [128, DC * HALF])
            wA = sbd("wA", [128, 16 * 1024], BF16)
            wB = sbd("wB", [128, 16 * 1024], BF16)
            wC = sbd("wC", [128, 16 * 1024], BF16)
            cbc = sbd("cbc", [128, HALF])
            gwT_s = sbd("gwT_s", [128, DC * 8], BF16)
            c_tok = sbd("c_tok", [128, 16 * 8])
            c_fm = sbd("c_fm", [8, HALF])
            sm_scr = sbd("sm_scr", [128, 8])
            h3c = sbd("h3c", [128, 512], BF16)
            c_row = cbc
            ytm = [sbd(f"ytm{i}", [128, 1024]) for i in range(2)]
            x1r = sbd("x1r", [128, 1024])
            wbuf = [wA, wB, wC]
            wb_w = [_Buf(), _Buf(), _Buf()]
            cfence = [(ls1[0], ls1[0].n), (ls1[1], ls1[1].n),
                      (lxt[0], lxt[0].n), (lxt[1], lxt[1].n),
                      (lx2[0], lx2[0].n), (lx2[1], lx2[1].n)]
            wb_w[0].r += cfence
            wb_w[1].r += cfence
            wb_w[2].r += cfence
            xth_w = _Buf()
            h1_w = _Buf()
            y_w = _Buf()
            cbc_w = _Buf()
            crow_w = _Buf()
            ctok_w = _Buf()
            cfm_w = _Buf()
            ytm_w = [_Buf(), _Buf()]
            x1r_w = _Buf()
            outst_w = _Buf()

            P.op("gpsimd", lambda g: g.dma_start(
                out=gwT_s.ap().rearrange("p (dc n) -> p dc n", dc=DC),
                in_=gwT_d.rearrange("(dc p) n -> p dc n", p=128)),
                 waits=list(cfence), incs=[(wd, 16)])
            wd_gw = wd.n

            e_order = [8] + list(range(8))

            for half in range(2):
                htok0 = half * HALF
                P.op("sync", lambda s, o=htok0: s.dma_start(
                    out=xt_half.ap().rearrange("p (dc n) -> p dc n", dc=DC),
                    in_=xtsp[:, :, o:o + HALF]),
                     waits=xth_w.take() + list(cfence), incs=[(lxh, 16)])
                xt_done_ld = lxh.n

                for t8 in range(8):
                    bk = t8 % 2
                    for dc in range(DC):
                        P.op("tensor", lambda te, t=t8, d=dc, k=bk: te.matmul(
                            pb[k][:, 0:8],
                            xt_half[:, d * HALF + t * 128:d * HALF + t * 128 + 128],
                            gwT_s[:, d * 8:d * 8 + 8],
                            start=(d == 0), stop=(d == DC - 1)),
                             waits=([(lxh, xt_done_ld), (wd, wd_gw)]
                                    + bankw[bk].take()) if dc == 0 else [],
                             incs=[(pe, 1)] if dc == DC - 1 else [])
                    pw = pe.n
                    mx, sμ, m1c, m2c = col(), col(), col(), col()
                    P.op("vector", lambda v, k=bk, m=mx: v.reduce_max(
                        m, pb[k][:, 0:8], axis=AXX),
                         waits=[(pe, pw)] + ctok_w.take(), incs=[(ve, 1)])
                    nmx = col()
                    P.op("vector", lambda v, m=mx, nm=nmx: v.tensor_scalar(
                        nm, m, -1.0, None, ALU.mult),
                         waits=[(ve, ve.n)], incs=[(ve, 1)])
                    vw = ve.n
                    P.op("scalar", lambda s, k=bk, nm=nmx: s.activation(
                        out=sm_scr[:], in_=pb[k][:, 0:8], func=AF.Exp, bias=nm),
                         waits=[(ve, vw)], incs=[(se, 1)])
                    sw = se.n
                    tk = P.op("vector", lambda v, m=sμ: v.reduce_sum(
                        m, sm_scr[:], axis=AXX), waits=[(se, sw)], incs=[(ve, 1)])
                    bankw[bk].r.extend(tk)
                    P.op("vector", lambda v, m=sμ: v.reciprocal(m, m),
                         waits=[(ve, ve.n)], incs=[(ve, 1)])
                    P.op("vector", lambda v, m=sμ: v.tensor_scalar(
                        sm_scr[:], sm_scr[:], m, None, ALU.mult),
                         waits=[(ve, ve.n)], incs=[(ve, 1)])
                    ctk = c_tok[:, t8 * 8:t8 * 8 + 8]
                    P.op("vector", lambda v, ct=ctk: v.tensor_tensor(
                        ct, sm_scr[:], gbias_s[:], ALU.add),
                         waits=[(ve, ve.n), (ld, ld_base)], incs=[(ve, 1)])
                    P.op("vector", lambda v, ct=ctk, m=m1c: v.reduce_max(
                        m, ct, axis=AXX), waits=[(ve, ve.n)], incs=[(ve, 1)])
                    P.op("vector", lambda v, ct=ctk, m=m1c: v.tensor_scalar(
                        sm_scr[:], ct, m, None, ALU.is_ge),
                         waits=[(ve, ve.n)], incs=[(ve, 1)])
                    P.op("vector", lambda v, ct=ctk: v.scalar_tensor_tensor(
                        sm_scr[:], sm_scr[:], -1e30, ct, ALU.mult, ALU.add),
                         waits=[(ve, ve.n)], incs=[(ve, 1)])
                    P.op("vector", lambda v, m=m2c: v.reduce_max(
                        m, sm_scr[:], axis=AXX),
                         waits=[(ve, ve.n)], incs=[(ve, 1)])
                    P.op("vector", lambda v, ct=ctk, m=m2c: v.tensor_scalar(
                        sm_scr[:], ct, m, None, ALU.is_ge),
                         waits=[(ve, ve.n)], incs=[(ve, 1)])
                    P.op("vector", lambda v, ct=ctk: v.tensor_tensor(
                        ct, ct, gbias_s[:], ALU.subtract),
                         waits=[(ve, ve.n)], incs=[(ve, 1)])
                    P.op("vector", lambda v, ct=ctk: v.tensor_tensor(
                        ct, ct, sm_scr[:], ALU.mult),
                         waits=[(ve, ve.n)], incs=[(ve, 1)])
                    vw = ve.n
                    P.op("tensor", lambda te, ct=ctk, k=2 + bk: te.transpose(
                        pb[k][0:8, 0:128], ct, ident_f[:]),
                         waits=[(ve, vw)] + bankw[2 + bk].take(), incs=[(pe, 1)])
                    pw = pe.n
                    tk = P.op("vector", lambda v, t=t8, k=2 + bk: v.tensor_copy(
                        c_fm[0:8, t * 128:t * 128 + 128],
                        pb[k][0:8, 0:128]),
                         waits=[(pe, pw)] + (cfm_w.take() if t8 == 0 else []),
                         incs=[(ve, 1)])
                    bankw[2 + bk].r.extend(tk)
                    ctok_w.r.extend(tk)
                cfm_done = ve.n

                lctr = [0]

                def wstream(src4, parity, d4):
                    P.op("gpsimd", lambda g, s=src4, d=d4: g.dma_start(
                        out=d, in_=s),
                         waits=wb_w[parity].take(), incs=[(wdp[parity], 16)])
                    return wdp[parity].n

                for ei, e in enumerate(e_order):
                    par1 = lctr[0] % 3
                    wd1s = wdp[par1]
                    wd1 = wstream(
                        w1T_d[e].rearrange("(dc p) (ic n) -> p dc ic n",
                                           p=128, n=128),
                        par1,
                        wbuf[par1].ap().rearrange("p (dc ic n) -> p dc ic n",
                                                  dc=DC, ic=IC))
                    lctr[0] += 1
                    par3 = lctr[0] % 3
                    wd3s = wdp[par3]
                    wd3 = wstream(
                        w3T_d[e].rearrange("(dc p) (ic n) -> p dc ic n",
                                           p=128, n=128),
                        par3,
                        wbuf[par3].ap().rearrange("p (dc ic n) -> p dc ic n",
                                                  dc=DC, ic=IC))
                    lctr[0] += 1
                    par2 = lctr[0] % 3
                    wd2s = wdp[par2]
                    wd2 = wstream(
                        w2T_d[e].rearrange("(ic p) (dc n) -> p ic dc n",
                                           p=128, n=128),
                        par2,
                        wbuf[par2].ap().rearrange("p (ic dc n) -> p ic dc n",
                                                  ic=IC, dc=DC))
                    lctr[0] += 1

                    if e != 8:
                        P.op("sync", lambda s, ee=e: s.dma_start(
                            out=c_row[0:1, :],
                            in_=c_fm[ee:ee + 1, 0:HALF]),
                             waits=[(ve, cfm_done)] + crow_w.take(),
                             incs=[(lcr, 16)])
                        wd_cr = lcr.n
                        for t2 in range(2):
                            tkp = P.op("tensor", lambda te, t=t2: te.matmul(
                                pb[4][:, 0:512], ones1_f[0:1, :],
                                c_row[0:1, t * 512:t * 512 + 512],
                                start=True, stop=True),
                                 waits=[(lcr, wd_cr)] + bankw[4].take(),
                                 incs=[(pe, 1)])
                            crow_w.r.extend(tkp)
                            pw = pe.n
                            tk = P.op("vector", lambda v, t=t2: v.tensor_copy(
                                cbc[:, t * 512:t * 512 + 512], pb[4][:, 0:512]),
                                 waits=[(pe, pw)] + (cbc_w.take()
                                                     if t2 == 0 else []),
                                 incs=[(ve, 1)])
                            bankw[4].r.extend(tk)
                        cbc_done = ve.n
                    else:
                        cbc_done = 0

                    for ic in range(IC):
                        for t2 in range(2):
                            bk = (ic * 2 + t2) % 2
                            for dc in range(DC):
                                P.op("tensor", lambda te, i=ic, t=t2, d=dc, k=bk,
                                     p=par1: te.matmul(
                                         pb[k][:, 0:512],
                                         wbuf[p][:, d * 2048 + i * 128:
                                                 d * 2048 + i * 128 + 128],
                                         xt_half[:, d * HALF + t * 512:
                                                 d * HALF + t * 512 + 512],
                                         start=(d == 0), stop=(d == DC - 1)),
                                     waits=([(wd1s, wd1), (lxh, xt_done_ld)]
                                            + bankw[bk].take()) if dc == 0 else [],
                                     incs=[(pe, 1)] if dc == DC - 1 else [])
                            pw = pe.n
                            tk = P.op("scalar", lambda s, i=ic, t=t2, k=bk: s.activation(
                                out=h1s[:, i * HALF + t * 512:i * HALF + t * 512 + 512],
                                in_=pb[k][:, 0:512], func=AF.Silu),
                                 waits=[(pe, pw)] + (h1_w.take()
                                                     if ic == 0 and t2 == 0 else []),
                                 incs=[(se, 1)])
                            bankw[bk].r.extend(tk)
                            wb_w[par1].r.extend(tk)
                    for ic in range(IC):
                        for t2 in range(2):
                            bk = (ic * 2 + t2) % 2
                            for dc in range(DC):
                                P.op("tensor", lambda te, i=ic, t=t2, d=dc, k=bk,
                                     p=par3: te.matmul(
                                         pb[k][:, 0:512],
                                         wbuf[p][:, d * 2048 + i * 128:
                                                 d * 2048 + i * 128 + 128],
                                         xt_half[:, d * HALF + t * 512:
                                                 d * HALF + t * 512 + 512],
                                         start=(d == 0), stop=(d == DC - 1)),
                                     waits=([(wd3s, wd3)] + bankw[bk].take())
                                     if dc == 0 else [],
                                     incs=[(pe, 1)] if dc == DC - 1 else [])
                            pw = pe.n
                            hsl = h1s[:, ic * HALF + t2 * 512:
                                      ic * HALF + t2 * 512 + 512]
                            if e != 8:
                                tkc = P.op("vector", lambda v, t=t2, k=bk: v.tensor_tensor(
                                    h3c[:], pb[k][:, 0:512],
                                    cbc[:, t * 512:t * 512 + 512], ALU.mult),
                                     waits=[(pe, pw), (ve, cbc_done), (se, se.n)],
                                     incs=[(ve, 1)])
                                crow_w.r.extend(tkc)
                                tk = P.op("vector", lambda v, hs=hsl: v.tensor_tensor(
                                    hs, h3c[:], hs, ALU.mult), incs=[(ve, 1)])
                            else:
                                tk = P.op("vector", lambda v, hs=hsl, k=bk: v.tensor_tensor(
                                    hs, pb[k][:, 0:512], hs, ALU.mult),
                                     waits=[(pe, pw), (se, se.n)], incs=[(ve, 1)])
                            bankw[bk].r.extend(tk)
                            wb_w[par3].r.extend(tk)
                            if e != 8:
                                cbc_w.r.extend(tk)
                    ve_p2 = ve.n
                    if ei == 0:
                        ydc_tick = {}
                    for dc in range(DC):
                        for t2 in range(2):
                            bk = (dc * 2 + t2) % 2
                            for ic in range(IC):
                                P.op("tensor", lambda te, i=ic, t=t2, d=dc, k=bk,
                                     p=par2: te.matmul(
                                         pb[k][:, 0:512],
                                         wbuf[p][:, i * 1024 + d * 128:
                                                 i * 1024 + d * 128 + 128],
                                         h1s[:, i * HALF + t * 512:
                                             i * HALF + t * 512 + 512],
                                         start=(i == 0), stop=(i == IC - 1)),
                                     waits=([(wd2s, wd2), (ve, ve_p2)]
                                            + bankw[bk].take()) if ic == 0 else [],
                                     incs=[(pe, 1)] if ic == IC - 1 else [])
                            pw = pe.n
                            ysl = y_f[:, dc * HALF + t2 * 512:
                                      dc * HALF + t2 * 512 + 512]
                            if ei == 0:
                                tk = P.op("vector", lambda v, ys=ysl, k=bk: v.tensor_copy(
                                    ys, pb[k][:, 0:512]),
                                     waits=[(pe, pw)] + (y_w.take()
                                                         if dc == 0 and t2 == 0 else []),
                                     incs=[(ve, 1)])
                            else:
                                tk = P.op("vector", lambda v, ys=ysl, k=bk: v.tensor_tensor(
                                    ys, ys, pb[k][:, 0:512], ALU.add),
                                     waits=[(pe, pw)], incs=[(ve, 1)])
                            bankw[bk].r.extend(tk)
                            wb_w[par2].r.extend(tk)
                            h1_w.r.extend(tk)
                            ydc_tick[dc] = ve.n
                y_done = ve.n

                for sub in range(8):
                    sp = sub % 2
                    tok0 = htok0 + sub * 128
                    yt, xr = ytm[sp], x1r
                    for dc in range(DC):
                        bk = 2 + dc % 2
                        P.op("tensor", lambda te, d=dc, s4=sub, k=bk: te.transpose(
                            pb[k][:, 0:128],
                            y_f[:, d * HALF + s4 * 128:d * HALF + s4 * 128 + 128],
                            ident_f[:]),
                             waits=[(ve, ydc_tick[dc])] + bankw[bk].take(),
                             incs=[(pe, 1)])
                        pw = pe.n
                        tk = P.op("vector", lambda v, d=dc, k=bk, t=yt: v.tensor_copy(
                            t[:, d * 128:d * 128 + 128], pb[k][:, 0:128]),
                             waits=[(pe, pw)] + (ytm_w[sp].take()
                                                 if dc == 0 else []),
                             incs=[(ve, 1)])
                        bankw[bk].r.extend(tk)
                        y_w.r.extend(tk)
                    tick = ln_tile(yt[:], yt[:])
                    ytm_w[sp].r.extend(tick)
                    P.op("sync", lambda s, o=tok0, t=xr: s.dma_start(
                        out=t[:], in_=x1sp[o:o + 128, :]),
                         waits=x1r_w.take() + [(ls1[0], ls1[0].n),
                                               (ls1[1], ls1[1].n)],
                         incs=[(lr1, 16)])
                    lw = lr1.n
                    tk = P.op("vector", lambda v, t=xr, u=yt: v.tensor_tensor(
                        t[:], t[:], u[:], ALU.add),
                         waits=[(lr1, lw), (ve, ve.n)], incs=[(ve, 1)])
                    ytm_w[sp].r.extend(tk)
                    vw = ve.n
                    P.op("sync", lambda s, o=tok0, t=xr: s.dma_start(
                        out=out_d[o:o + 128, :], in_=t[:]),
                         waits=[(ve, vw)], incs=[(od, 16)])
                    x1r_w.r.append((od, od.n))
                xth_w.r.append((pe, pe.n))

            if stage == "fulldbg":
                vend, pend, send = ve.n, pe.n, se.n
                for (dst, srcb) in ((f_y, y_f), (f_cfm, c_fm), (f_xth, xt_half),
                                    (f_h1s, h1s), (f_cbc, cbc)):
                    P.op("sync", lambda s, d=dst, sr=srcb: s.dma_start(
                        out=d[:, :], in_=sr.ap()),
                         waits=[(ve, vend), (pe, pend), (se, send)],
                         incs=[(od, 16)])
            P.op("sync", lambda s: s.nop(), waits=[(od, od.n)])

        P.realize(nc)
    return nc


_NC_CACHE = {}


def _get_nc(stage="full"):
    if stage not in _NC_CACHE:
        _NC_CACHE[stage] = build_program(stage)
    return _NC_CACHE[stage]


def stage_inputs(inputs):
    f = lambda a: np.ascontiguousarray(np.asarray(a, dtype=np.float32))
    x = f(inputs["L0_attn_x"])
    wq = f(inputs["L0_attn_Wq"])
    wk = f(inputs["L0_attn_Wk"])
    wv = f(inputs["L0_attn_Wv"])
    wg = f(inputs["L0_attn_Wg"])
    wo = f(inputs["L0_attn_Wo"])
    gw = f(inputs["L0_ffn_gate_w"])
    gb = f(inputs["L0_ffn_gate_bias"])
    w1 = f(inputs["L0_ffn_w1"])
    w2 = f(inputs["L0_ffn_w2"])
    w3 = f(inputs["L0_ffn_w3"])
    sw1 = f(inputs["L0_ffn_sw1"])
    sw2 = f(inputs["L0_ffn_sw2"])
    sw3 = f(inputs["L0_ffn_sw3"])

    w1T = np.ascontiguousarray(
        np.concatenate([w1.transpose(0, 2, 1), sw1.T[None]], axis=0))
    w3T = np.ascontiguousarray(
        np.concatenate([w3.transpose(0, 2, 1), sw3.T[None]], axis=0))
    w2T = np.ascontiguousarray(
        np.concatenate([w2.transpose(0, 2, 1), sw2.T[None]], axis=0))
    gwT = np.ascontiguousarray(gw.T)
    gbias = np.ascontiguousarray(np.tile(gb[None, :], (128, 1)))

    in_maps = []
    for c in range(NCORE):
        b, s = c // 4, c % 4
        u = np.zeros((128, 6), dtype=np.float32)
        for j in range(3):
            if j < s:
                u[:, j] = 1.0
            else:
                u[:, 3 + j] = 1.0
        u[:, 3:6] = 1.0 - u[:, 0:3]
        in_maps.append({
            "xsl": np.ascontiguousarray(x[b, s * NT:(s + 1) * NT, :]),
            "wq": wq, "wk": wk, "wv": wv, "wg": wg, "wo": wo,
            "gwT": gwT, "gbias": gbias,
            "w1T": w1T, "w3T": w3T, "w2T": w2T,
            "um": np.ascontiguousarray(u),
        })
    return in_maps


def kernel(**inputs):
    in_maps = stage_inputs(inputs)
    nc = _get_nc("full")
    res = run_bass_kernel_spmd(nc, in_maps, core_ids=list(range(NCORE)))
    out = np.empty((2, 8192, D), dtype=np.float32)
    for c in range(NCORE):
        b, s = c // 4, c % 4
        out[b, s * NT:(s + 1) * NT, :] = res.results[c]["out"]
    return out


# version 26
# speedup vs baseline: 1.0225x; 1.0076x over previous
import numpy as np
from contextlib import ExitStack

import concourse.bass as bass
import concourse.mybir as mybir
from concourse.bass_utils import run_bass_kernel_spmd

F32 = mybir.dt.float32
BF16 = mybir.dt.bfloat16
I8 = mybir.dt.int8
AF = mybir.ActivationFunctionType
ALU = mybir.AluOpType
AXX = mybir.AxisListType.X

NCORE = 8
NT = 2048
D = 1024
DC = 8
H = 8
DK = 128
DV = 128
NG = 4
GT = 512
CH = 128
II = 2048
IC = 16
NE = 9
HALF = 1024


class _Sem:
    __slots__ = ("name", "n", "h")
    def __init__(self, name):
        self.name, self.n, self.h = name, 0, None


class _Buf:
    __slots__ = ("r",)
    def __init__(self):
        self.r = []
    def take(self):
        r, self.r = self.r, []
        return r


class _Plan:
    def __init__(self):
        self.sems = []
        self.blocks = [[]]
        self.self_sems = {}

    def sem(self, name):
        s = _Sem(name)
        self.sems.append(s)
        return s

    def barrier(self):
        if self.blocks[-1]:
            self.blocks.append([])

    def op(self, eng, fn, waits=(), incs=()):
        ss = self.self_sems.get(eng)
        waits = list(waits)
        incs = list(incs)
        if ss is not None:
            waits.append((ss, ss.n))
            if not any(s is ss for s, _ in incs):
                incs.append((ss, 1))
        wd = {}
        for s, v in waits:
            if v <= 0:
                continue
            assert v <= s.n, f"wait-ahead on {s.name}: {v} > {s.n}"
            wd[s] = max(wd.get(s, 0), v)
        self.blocks[-1].append((eng, fn, tuple(wd.items()), tuple(incs)))
        out = []
        for s, k in incs:
            s.n += k
            out.append((s, s.n))
        return out

    def realize(self, nc):
        with ExitStack() as es:
            for s in self.sems:
                s.h = es.enter_context(nc.semaphore(s.name))
            for ops in self.blocks:
                engs = []
                for o in ops:
                    if o[0] not in engs:
                        engs.append(o[0])
                with nc.Block() as blk:
                    for en in engs:
                        eng_ops = [o for o in ops if o[0] == en]

                        def body(eng, eng_ops=eng_ops):
                            for _, fn, waits, incs in eng_ops:
                                for s, v in waits:
                                    eng.wait_ge(s.h, v)
                                ins = fn(eng)
                                for s, k in incs:
                                    ins = ins.then_inc(s.h, k)
                        getattr(blk, en)(body)


def build_program(stage="full"):
    nc = bass.Bass("TRN2", target_bir_lowering=False)
    P = _Plan()
    ld = P.sem("ld")
    wd = P.sem("wd")
    lx = [P.sem("lxa"), P.sem("lxb")]
    lcc = P.sem("lcc")
    lx2 = [P.sem("lx2a"), P.sem("lx2b")]
    ls1 = [P.sem("ls1a"), P.sem("ls1b")]
    lxt = [P.sem("lxta"), P.sem("lxtb")]
    lxh = P.sem("lxh")
    lr1 = P.sem("lr1")
    lcr = P.sem("lcr")
    wdp = [P.sem("wda"), P.sem("wdb"), P.sem("wdc")]
    pe = P.sem("pe")
    ve = P.sem("ve")
    se = P.sem("se")
    ge = P.sem("ge")
    cc = P.sem("cc")
    od = P.sem("od")
    P.self_sems = {"vector": ve, "scalar": se}

    xsl = nc.dram_tensor("xsl", [NT, D], F32, kind="ExternalInput").ap()
    wq_d = nc.dram_tensor("wq", [D, D], F32, kind="ExternalInput").ap()
    wk_d = nc.dram_tensor("wk", [D, D], F32, kind="ExternalInput").ap()
    wv_d = nc.dram_tensor("wv", [D, D], F32, kind="ExternalInput").ap()
    wg_d = nc.dram_tensor("wg", [D, D], F32, kind="ExternalInput").ap()
    wo_d = nc.dram_tensor("wo", [D, D], F32, kind="ExternalInput").ap()
    gwT_d = nc.dram_tensor("gwT", [D, H], F32, kind="ExternalInput").ap()
    gbias_d = nc.dram_tensor("gbias", [128, 8], F32, kind="ExternalInput").ap()
    w1T_d = nc.dram_tensor("w1T", [NE, D, II], F32, kind="ExternalInput").ap()
    w3T_d = nc.dram_tensor("w3T", [NE, D, II], F32, kind="ExternalInput").ap()
    w2T_d = nc.dram_tensor("w2T", [NE, II, D], F32, kind="ExternalInput").ap()
    um_d = nc.dram_tensor("um", [128, 6], F32, kind="ExternalInput").ap()
    out_d = nc.dram_tensor("out", [NT, D], F32, kind="ExternalOutput").ap()

    if stage == "fulldbg":
        f_y = nc.dram_tensor("f_y", [128, DC * HALF], F32, kind="ExternalOutput").ap()
        f_cfm = nc.dram_tensor("f_cfm", [8, NT], F32, kind="ExternalOutput").ap()
        f_xth = nc.dram_tensor("f_xth", [128, DC * HALF], BF16, kind="ExternalOutput").ap()
        f_h1s = nc.dram_tensor("f_h1s", [128, IC * HALF], BF16, kind="ExternalOutput").ap()
        f_cbc = nc.dram_tensor("f_cbc", [128, HALF], F32, kind="ExternalOutput").ap()
    if stage == "x1":
        dbg_o = nc.dram_tensor("dbg_o", [128, H * NT], BF16, kind="ExternalOutput").ap()
        dbg_q = nc.dram_tensor("dbg_q", [128, H * NT], BF16, kind="ExternalOutput").ap()
        dbg_s = nc.dram_tensor("dbg_s", [128, H * 128], F32, kind="ExternalOutput").ap()
        dbg_p = nc.dram_tensor("dbg_p", [128, H * 16], F32, kind="ExternalOutput").ap()
        dbg_u = nc.dram_tensor("dbg_u", [128, GT], F32, kind="ExternalOutput").ap()
        dbg_qn = nc.dram_tensor("dbg_qn", [128, GT], BF16, kind="ExternalOutput").ap()
        dbg_kn = nc.dram_tensor("dbg_kn", [128, GT], BF16, kind="ExternalOutput").ap()
        dbg_xnf = nc.dram_tensor("dbg_xnf", [128, DC * GT], BF16, kind="ExternalOutput").ap()
        dbg_v = nc.dram_tensor("dbg_v", [128, 4 * 1024], BF16, kind="ExternalOutput").ap()
    x1sp = nc.dram_tensor("x1sp", [NT, D], F32).ap()
    xtsp = nc.dram_tensor("xtsp", [128, DC, NT], BF16).ap()
    ccin = nc.dram_tensor("ccin", [128, 1032], F32)
    ccout = nc.dram_tensor("ccout", [512, 1032], F32)

    es = ExitStack()
    sb = lambda n, s, d=F32: es.enter_context(nc.sbuf_tensor(n, s, d))

    with es:
        ident_f = sb("ident_f", [128, 128])
        ident_b = sb("ident_b", [128, 128], BF16)
        mask_f = sb("mask_f", [128, 128])
        ones_f = sb("ones_f", [128, 128])
        zeros_b = sb("zeros_b", [128, 128], BF16)
        z512 = sb("z512", [128, 512])
        zcol = sb("zcol", [128, 1])
        epscol = sb("epscol", [128, 1])
        ones1_b = sb("ones1_b", [1, 128], BF16)
        ones1_f = sb("ones1_f", [1, 128])
        onesk_b = sb("onesk_b", [128, 1], BF16)
        gbias_s = sb("gbias_s", [128, 8])
        um_s = sb("um_s", [128, 6])
        sqdump = sb("sqdump", [128, 1024], BF16)
        cols = sb("cols", [128, 16])

        pb = [es.enter_context(nc.psum_tensor(f"pb{i}", [128, 512], F32))
              for i in range(7)]
        pbt_b = es.enter_context(nc.psum_tensor("pbt_b", [128, 128], BF16))
        bankw = [_Buf() for _ in range(8)]

        P.op("gpsimd", lambda g: g.memset(ones_f[:], 1.0), incs=[(ge, 1)])

        P.op("gpsimd", lambda g: g.memset(zeros_b[:], 0.0), incs=[(ge, 1)])
        P.op("gpsimd", lambda g: g.memset(z512[:], 0.0), incs=[(ge, 1)])
        P.op("gpsimd", lambda g: g.memset(zcol[:], 0.0), incs=[(ge, 1)])
        P.op("gpsimd", lambda g: g.memset(epscol[:], 1e-5), incs=[(ge, 1)])
        P.op("gpsimd", lambda g: g.memset(ones1_b[:], 1.0), incs=[(ge, 1)])
        P.op("gpsimd", lambda g: g.memset(ones1_f[:], 1.0), incs=[(ge, 1)])
        P.op("gpsimd", lambda g: g.memset(onesk_b[:], 1.0), incs=[(ge, 1)])
        P.op("gpsimd", lambda g: g.affine_select(
            ident_f[:], ones_f[:], [[1, 128]], ALU.is_equal, 0.0,
            channel_multiplier=-1), waits=[(ge, 8)], incs=[(ge, 1)])
        P.op("gpsimd", lambda g: g.affine_select(
            mask_f[:], ones_f[:], [[1, 128]], ALU.is_ge, 0.0,
            channel_multiplier=-1), waits=[(ge, 9)], incs=[(ge, 1)])
        P.op("vector", lambda v: v.tensor_copy(ident_b[:], ident_f[:]),
             waits=[(ge, 10)], incs=[(ve, 1)])
        P.op("vector", lambda v: v.memset(pb[0][64:128, 0:64], 0.0),
             incs=[(ve, 1)])
        P.op("sync", lambda s: s.dma_start(out=gbias_s[:], in_=gbias_d[:, :]),
             incs=[(ld, 16)])
        P.op("sync", lambda s: s.dma_start(out=um_s[:], in_=um_d[:, :]),
             incs=[(ld, 16)])
        ld_base = ld.n

        col_i = [0]
        def col():
            c = col_i[0] % 16
            col_i[0] += 1
            return cols[:, c:c + 1]

        def mk_dma(eng_dst, eng_src):
            return lambda e: e.dma_start(out=eng_dst, in_=eng_src)

        def ln_tile(src, dst, extra_waits=()):
            sc, mc, lc, rc = col(), col(), col(), col()
            P.op("vector", lambda v: v.reduce_sum(sc, src, axis=AXX),
                 waits=list(extra_waits), incs=[(ve, 1)])
            P.op("vector", lambda v: v.tensor_scalar(
                mc, sc, 1.0 / D, None, ALU.mult),
                 waits=[(ve, ve.n)], incs=[(ve, 1)])
            P.op("vector", lambda v: v.tensor_scalar(
                dst, src, mc, None, ALU.subtract),
                 waits=[(ve, ve.n)], incs=[(ve, 1)])
            t0 = ve.n
            P.op("scalar", lambda s: s.activation(
                out=sqdump[:], in_=dst, func=AF.Square, accum_out=lc),
                 waits=[(ve, t0)], incs=[(se, 1)])
            P.op("scalar", lambda s: s.activation(
                out=lc, in_=lc, func=AF.Ln, scale=1.0 / D, bias=epscol[:]),
                 waits=[(se, se.n)], incs=[(se, 1)])
            P.op("scalar", lambda s: s.activation(
                out=rc, in_=lc, func=AF.Exp, scale=-0.5),
                 waits=[(se, se.n)], incs=[(se, 1)])
            t1 = se.n
            tick = P.op("vector", lambda v: v.tensor_scalar(
                dst, dst, rc, None, ALU.mult), waits=[(se, t1)], incs=[(ve, 1)])
            return tick

        with ExitStack() as ctx1:
            sb1 = lambda n, s, d=F32: ctx1.enter_context(nc.sbuf_tensor(n, s, d))
            o_all = sb1("o_all", [128, H * NT], BF16)
            qh_all = sb1("qh_all", [128, H * NT], BF16)
            S_f = sb1("S_f", [128, H * 128])
            S_b = sb1("S_b", [128, H * 128], BF16)
            P_all = sb1("P_all", [128, H * 16])
            Pcur = sb1("Pcur", [128, H])
            Dsl = sb1("Dsl", [128, H])
            o_w = _Buf()
            qh_w = _Buf()

            P.op("vector", lambda v: v.memset(S_f[:], 0.0), incs=[(ve, 1)])
            P.op("vector", lambda v: v.memset(S_b[:], 0.0), incs=[(ve, 1)])
            P.op("vector", lambda v: v.memset(Pcur[:], 1.0), incs=[(ve, 1)])
            sb_tick = {h: 0 for h in range(H)}

            with ExitStack() as ctxA:
                sba = lambda n, s, d=F32: ctxA.enter_context(nc.sbuf_tensor(n, s, d))
                wq_s = sba("wq_s", [128, DC * D], BF16)
                wk_s = sba("wk_s", [128, DC * D], BF16)
                wv_s = sba("wv_s", [128, DC * D], BF16)
                wg_s = sba("wg_s", [128, DC * D], BF16)
                x_st = [sba(f"x_st{i}", [128, 1024]) for i in range(2)]
                xn_st = [sba(f"xn_st{i}", [128, 1024]) for i in range(2)]
                xnf = sba("xnf", [128, DC * GT], BF16)
                v_tm = sba("v_tm", [128, 4 * 1024], BF16)
                qn = [sba(f"qn{i}", [128, GT], BF16) for i in range(4)]
                kn = [sba(f"kn{i}", [128, GT], BF16) for i in range(4)]
                t_st = sba("t_st", [128, GT])
                u_st = sba("u_st", [128, GT])
                U_st = [sba(f"U_st{i}", [128, GT]) for i in range(4)]
                sq_b = sba("sq_b", [128, GT], BF16)
                ssq_s = u_st
                rn_b = sba("rn_b", [1, GT], BF16)
                d1m = sba("d1m", [128, GT])
                e1a = sba("e1a", [128, GT])
                qt_a = [sba(f"qt_{i}", [128, GT], BF16) for i in range(2)]
                kt_a = [sba(f"kt_{i}", [128, GT], BF16) for i in range(2)]
                kh_a = [sba(f"kh_{i}", [128, GT], BF16) for i in range(2)]
                scm = [sba(f"scm{i}", [128, 8]) for i in range(2)]
                am_p = [sba(f"am_{i}", [128, 128], BF16) for i in range(2)]
                khtm_p = [sba(f"khtm_{i}", [128, 128], BF16) for i in range(2)]
                e12_w = [_Buf(), _Buf(), _Buf(), _Buf()]
                x_w = [_Buf(), _Buf()]
                xn_w = [_Buf(), _Buf()]
                xnf_w = _Buf()
                vtm_w = _Buf()
                head_w = _Buf()

                def wload(dst, src):
                    s3 = src.rearrange("(dc p) n -> p dc n", p=128)
                    return P.op("gpsimd",
                                lambda g, d=dst, s=s3: g.dma_start(out=d.ap().rearrange(
                                    "p (dc n) -> p dc n", dc=DC), in_=s),
                                incs=[(wd, 16)])
                wload(wq_s, wq_d)
                wload(wk_s, wk_d)
                wload(wv_s, wv_d)
                wload(wg_s, wg_d)
                wd_attn = wd.n

                for g in range(NG):
                    for t in range(4):
                        bi = t % 2
                        tok0 = g * GT + t * 128
                        P.op("sync", lambda s, b=bi, o=tok0: s.dma_start(
                            out=x_st[b][:], in_=xsl[o:o + 128, :]),
                             waits=x_w[bi].take() + xn_w[bi].take(),
                             incs=[(lx[bi], 16)])
                        lw = lx[bi].n
                        tick = ln_tile(x_st[bi][:], xn_st[bi][:],
                                       extra_waits=[(lx[bi], lw)])
                        x_w[bi].r.extend(tick)
                        for dc in range(DC):
                            bk = dc % 2
                            P.op("tensor", lambda te, b=bi, d=dc, k=bk: te.transpose(
                                pb[k][:, 0:128], xn_st[b][:, d * 128:(d + 1) * 128],
                                ident_f[:]),
                                 waits=[(ve, ve.n)] + bankw[bk].take(),
                                 incs=[(pe, 1)])
                            pw = pe.n
                            tk = P.op("vector", lambda v, b=bi, d=dc, k=bk, tt=t: v.tensor_copy(
                                xnf[:, d * GT + tt * 128:d * GT + tt * 128 + 128],
                                pb[k][:, 0:128]),
                                 waits=[(pe, pw)] + xnf_w.take(), incs=[(ve, 1)])
                            bankw[bk].r.extend(tk)
                            xn_w[bi].r.extend(tk)
                    xnf_done = ve.n

                    for t in range(4):
                        for hf in range(2):
                            bk = 2 + hf
                            for dc in range(DC):
                                P.op("tensor", lambda te, tt=t, h2=hf, d=dc, k=bk: te.matmul(
                                    pb[k][:, 0:512],
                                    xnf[:, d * GT + tt * 128:d * GT + tt * 128 + 128],
                                    wv_s[:, d * D + h2 * 512:d * D + h2 * 512 + 512],
                                    start=(d == 0), stop=(d == DC - 1)),
                                     waits=([(ve, xnf_done), (wd, wd_attn)]
                                            + bankw[bk].take()) if dc == 0 else [],
                                     incs=[(pe, 1)] if dc == DC - 1 else [])
                            pw = pe.n
                            tk = P.op("vector", lambda v, tt=t, h2=hf, k=bk: v.tensor_copy(
                                v_tm[:, tt * 1024 + h2 * 512:tt * 1024 + h2 * 512 + 512],
                                pb[k][:, 0:512]),
                                 waits=[(pe, pw)] + vtm_w.take(), incs=[(ve, 1)])
                            bankw[bk].r.extend(tk)

                    U_done = {}
                    scale_done = {}

                    def emit_proj(h):
                        par = h % 4
                        for dc in range(DC):
                            P.op("tensor", lambda te, d=dc, hh=h: te.matmul(
                                pb[4][:, 0:512],
                                wg_s[:, d * D + hh * 128:d * D + hh * 128 + 128],
                                xnf[:, d * GT:(d + 1) * GT],
                                start=(d == 0), stop=(d == DC - 1)),
                                 waits=([(ve, xnf_done)] + bankw[4].take())
                                 if dc == 0 else [],
                                 incs=[(pe, 1)] if dc == DC - 1 else [])
                        pw = pe.n
                        P.op("scalar", lambda s: s.activation(
                            out=t_st[:], in_=pb[4][:, 0:512], func=AF.Exp,
                            scale=-1.0), waits=[(pe, pw)], incs=[(se, 1)])
                        tk = P.op("scalar", lambda s: s.activation(
                            out=u_st[:], in_=t_st[:], func=AF.Ln, bias=1.0),
                             incs=[(se, 1)])
                        bankw[4].r.extend(tk)
                        sw = se.n
                        P.op("vector", lambda v, p=par: v.tensor_tensor_scan(
                            U_st[p][:], u_st[:], z512[:], 0.0, ALU.add, ALU.add),
                             waits=[(se, sw)], incs=[(ve, 1)])
                        U_done[h] = ve.n

                        for (wmat, dstn, bq) in ((wq_s, qn[par], 2 + (h % 2)),
                                                 (wk_s, kn[par], 3 - (h % 2))):
                            for dc in range(DC):
                                P.op("tensor", lambda te, d=dc, hh=h, w=wmat, k=bq: te.matmul(
                                    pb[k][:, 0:512],
                                    w[:, d * D + hh * 128:d * D + hh * 128 + 128],
                                    xnf[:, d * GT:(d + 1) * GT],
                                    start=(d == 0), stop=(d == DC - 1)),
                                     waits=([(ve, xnf_done)] + bankw[bq].take())
                                     if dc == 0 else [],
                                     incs=[(pe, 1)] if dc == DC - 1 else [])
                            pw = pe.n
                            P.op("scalar", lambda s, k=bq: s.activation(
                                out=sq_b[:], in_=pb[k][:, 0:512], func=AF.Square),
                                 waits=[(pe, pw)] + bankw[5].take(), incs=[(se, 1)])
                            sw = se.n
                            P.op("tensor", lambda te: te.matmul(
                                pb[5][0:1, 0:512], onesk_b[:], sq_b[:],
                                start=True, stop=True),
                                 waits=[(se, sw)], incs=[(pe, 1)])
                            pw = pe.n
                            tk = P.op("vector", lambda v: v.tensor_scalar_max(
                                ssq_s[0:1, :], pb[5][0:1, 0:512], 1e-24),
                                 waits=[(pe, pw)], incs=[(ve, 1)])
                            vw = ve.n
                            P.op("scalar", lambda s: s.activation(
                                out=ssq_s[0:1, :], in_=ssq_s[0:1, :], func=AF.Ln),
                                 waits=[(ve, vw)], incs=[(se, 1)])
                            P.op("scalar", lambda s: s.activation(
                                out=rn_b[0:1, :], in_=ssq_s[0:1, :], func=AF.Exp,
                                scale=-0.5),
                                 waits=[(se, se.n)], incs=[(se, 1)])
                            sw = se.n
                            P.op("tensor", lambda te: te.matmul(
                                pb[5][:, 0:512], ones1_b[0:1, :], rn_b[0:1, :],
                                start=True, stop=True),
                                 waits=[(se, sw)], incs=[(pe, 1)])
                            pw = pe.n
                            P.op("vector", lambda v, dst=dstn, k=bq: v.tensor_copy(
                                dst[:], pb[k][:, 0:512]),
                                 waits=[(pe, pw)], incs=[(ve, 1)])
                            tk = P.op("vector", lambda v, dst=dstn: v.tensor_tensor(
                                dst[:], dst[:], pb[5][:, 0:512], ALU.mult),
                                 waits=[(ve, ve.n)], incs=[(ve, 1)])
                            bankw[5].r.extend(tk)
                            bankw[bq].r.extend(tk)

                    def emit_scalings(h):
                        par = h % 4
                        sl = h % 2
                        ew = e12_w[sl].take()
                        for c in range(4):
                            P.op("vector", lambda v, c0=c * 128, p=par: v.tensor_scalar(
                                d1m[:, c0:c0 + 128], U_st[p][:, c0:c0 + 128],
                                U_st[p][:, c0 + 63:c0 + 64], None, ALU.subtract),
                                 waits=[(ve, U_done[h])] + (ew if c == 0 else []),
                                 incs=[(ve, 1)])
                        for c in range(4):
                            up = (zcol[:, 0:1] if c == 0
                                  else U_st[par][:, c * 128 - 1:c * 128])
                            P.op("vector", lambda v, c2=c, u=up, p=par, s2=sl: v.tensor_tensor(
                                scm[s2][:, c2:c2 + 1], u,
                                U_st[p][:, c2 * 128 + 63:c2 * 128 + 64],
                                ALU.subtract), incs=[(ve, 1)])
                        vw = ve.n
                        P.op("scalar", lambda s, s2=sl: s.activation(
                            out=scm[s2][:, 0:4], in_=scm[s2][:, 0:4], func=AF.Exp),
                             waits=[(ve, vw)], incs=[(se, 1)])
                        P.op("scalar", lambda s: s.activation(
                            out=e1a[:], in_=d1m[:], func=AF.Exp, scale=-1.0),
                             incs=[(se, 1)])
                        P.op("scalar", lambda s: s.activation(
                            out=d1m[:], in_=d1m[:], func=AF.Exp),
                             waits=[(se, se.n)], incs=[(se, 1)])
                        sw = se.n
                        P.op("vector", lambda v, p=par, s2=sl: v.tensor_tensor(
                            qt_a[s2][:], qn[p][:], e1a[:], ALU.mult),
                             waits=[(se, sw)], incs=[(ve, 1)])
                        P.op("vector", lambda v, p=par, s2=sl: v.tensor_tensor(
                            kt_a[s2][:], kn[p][:], d1m[:], ALU.mult),
                             incs=[(ve, 1)])
                        for c in range(4):
                            c0 = c * 128
                            qh_ap = qh_all[:, h * NT + g * GT + c0:
                                           h * NT + g * GT + c0 + 128]
                            P.op("vector", lambda v, qa=qh_ap, c0=c0, s2=sl: v.tensor_scalar(
                                qa, qt_a[s2][:, c0:c0 + 128],
                                scm[s2][:, c0 // 128:c0 // 128 + 1], None, ALU.mult),
                                 waits=qh_w.take(), incs=[(ve, 1)])
                            P.op("vector", lambda v, c0=c0, s2=sl: v.tensor_scalar(
                                kh_a[s2][:, c0:c0 + 128], kt_a[s2][:, c0:c0 + 128],
                                e1a[:, c0 + 127:c0 + 128], None, ALU.mult),
                                 incs=[(ve, 1)])
                            P.op("vector", lambda v, c2=c, s2=sl: v.tensor_scalar(
                                scm[s2][:, 4 + c2:5 + c2], e1a[:, c2 * 128 + 127:
                                                              c2 * 128 + 128],
                                scm[s2][:, c2:c2 + 1], None, ALU.mult),
                                 incs=[(ve, 1)])
                        scale_done[h] = ve.n

                    def chunk_body(h, c):
                        cp = h % 2
                        am_b, khtm_b = am_p[cp], khtm_p[cp]
                        nch = g * 4 + c
                        c0 = c * 128
                        qh_ap = qh_all[:, h * NT + g * GT + c0:
                                       h * NT + g * GT + c0 + 128]
                        qt_s = qt_a[cp][:, c0:c0 + 128]
                        kt_s = kt_a[cp][:, c0:c0 + 128]
                        kh_s = kh_a[cp][:, c0:c0 + 128]
                        dcol = scm[cp][:, 4 + c:5 + c]
                        bw0 = bankw[0].take()
                        P.op("tensor", lambda te, t=kt_s, u=qt_s: te.matmul(
                            pb[0][0:64, 0:64], t[:, 0:64], u[:, 0:64],
                            start=True, stop=True),
                             waits=[(ve, scale_done[h])] + bw0, incs=[(pe, 1)])
                        P.op("tensor", lambda te, t=kt_s, u=qt_s: te.matmul(
                            pb[0][64:128, 64:128], t[:, 64:128],
                            u[:, 64:128], start=True, stop=True),
                             incs=[(pe, 1)])
                        P.op("tensor", lambda te, t=kt_s, u=qt_s: te.matmul(
                            pb[0][0:64, 64:128], t[:, 0:64],
                            u[:, 64:128], start=True, stop=True),
                             incs=[(pe, 1)])
                        P.op("tensor", lambda te, t=kh_s: te.transpose(
                            pbt_b[:, 0:128], t, ident_b[:]),
                             waits=bankw[7].take(), incs=[(pe, 1)])
                        pw = pe.n
                        tk = P.op("vector", lambda v, t=am_b: v.tensor_tensor(
                            t[:], pb[0][:, 0:128], mask_f[:], ALU.mult),
                             waits=[(pe, pw)], incs=[(ve, 1)])
                        bankw[0].r.extend(tk)
                        tk = P.op("scalar", lambda s, t=khtm_b: s.copy(
                            t[:], pbt_b[:, 0:128]),
                             waits=[(pe, pw)], incs=[(se, 1)])
                        bankw[7].r.extend(tk)
                        kh_se = se.n
                        vw = ve.n
                        vsl = v_tm[:, c * 1024 + h * 128:c * 1024 + h * 128 + 128]
                        P.op("tensor", lambda te, hh=h, qa=qh_ap: te.matmul(
                            pb[6][:, 0:128],
                            S_b[:, hh * 128:hh * 128 + 128],
                            qa, start=True, stop=False),
                             waits=[(ve, vw), (se, sb_tick[h])] + bankw[6].take(),
                             incs=[(pe, 1)])
                        P.op("tensor", lambda te, vs=vsl, t=am_b: te.matmul(
                            pb[6][:, 0:128], vs, t[:],
                            start=False, stop=True), incs=[(pe, 1)])
                        o_pe = pe.n
                        P.op("tensor", lambda te, vs=vsl, t=khtm_b: te.matmul(
                            pb[1][:, 0:128], t[:], vs,
                            start=True, stop=True),
                             waits=[(se, kh_se)] + bankw[1].take(),
                             incs=[(pe, 1)])
                        m_pe = pe.n
                        o_ap = o_all[:, h * NT + g * GT + c0:
                                     h * NT + g * GT + c0 + 128]
                        tk = P.op("scalar", lambda s, oa=o_ap: s.copy(
                            oa, pb[6][:, 0:128]),
                             waits=[(pe, o_pe)] + o_w.take(), incs=[(se, 1)])
                        bankw[6].r.extend(tk)
                        pidx = P_all[:, h * 16 + nch:h * 16 + nch + 1]
                        P.op("vector", lambda v, pi=pidx, hh=h: v.tensor_copy(
                            pi, Pcur[:, hh:hh + 1]), incs=[(ve, 1)])
                        tk = P.op("vector", lambda v, hh=h, dc2=dcol:
                                  v.scalar_tensor_tensor(
                                      S_f[:, hh * 128:hh * 128 + 128],
                                      S_f[:, hh * 128:hh * 128 + 128],
                                      dc2, pb[1][:, 0:128],
                                      ALU.mult, ALU.add),
                                  waits=[(pe, m_pe)], incs=[(ve, 1)])
                        bankw[1].r.extend(tk)
                        sv = ve.n
                        tk = P.op("scalar", lambda s, hh=h: s.copy(
                            S_b[:, hh * 128:hh * 128 + 128],
                            S_f[:, hh * 128:hh * 128 + 128]),
                             waits=[(ve, sv)], incs=[(se, 1)])
                        sb_tick[h] = se.n
                        tk2 = P.op("vector", lambda v, hh=h, dc2=dcol: v.tensor_tensor(
                            Pcur[:, hh:hh + 1], Pcur[:, hh:hh + 1], dc2,
                            ALU.mult), incs=[(ve, 1)])
                        if c == 3:
                            e12_w[h % 2].r.extend(tk2)

                    def chunks_pair(hA, hB):
                        emit_scalings(hA)
                        chunk_body(hA, 0)
                        emit_scalings(hB)
                        for c in range(3):
                            chunk_body(hA, c + 1)
                            chunk_body(hB, c)
                        chunk_body(hB, 3)

                    for i in range(4):
                        emit_proj(2 * i)
                        emit_proj(2 * i + 1)
                        if i >= 1:
                            chunks_pair(2 * i - 2, 2 * i - 1)
                    chunks_pair(6, 7)
                P.op("vector", lambda v: v.tensor_copy(Dsl[:], Pcur[:]),
                     waits=[(ve, ve.n)], incs=[(ve, 1)])
                if stage == "x1":
                    vend = ve.n
                    for (dst, srcb) in ((dbg_o, o_all), (dbg_q, qh_all),
                                        (dbg_s, S_f), (dbg_p, P_all),
                                        (dbg_u, U_st[3]), (dbg_qn, qn[3]),
                                        (dbg_kn, kn[3]), (dbg_xnf, xnf),
                                        (dbg_v, v_tm)):
                        P.op("sync", lambda s, d=dst, sr=srcb: s.dma_start(
                            out=d[:, :], in_=sr[:]),
                             waits=[(ve, vend)], incs=[(od, 16)])

            P.barrier()
            with ExitStack() as ctxC:
                sbc = lambda n, s, d=F32: ctxC.enter_context(nc.sbuf_tensor(n, s, d))
                sbb = lambda n, s, d=F32: ctxC.enter_context(nc.sbuf_tensor(n, s, d))
                ccst = sbb("ccst", [128, 1032])
                gth = sbb("gth", [128, 3 * 1032])
                Sst_f = sbb("Sst_f", [128, H * 128])
                Sst_b = sbb("Sst_b", [128, H * 128], BF16)
                DmS = sbb("DmS", [128, 8])
                mm_scr = sbb("mm_scr", [128, 128])
                qp_b = [sbb(f"qp_b{i}", [128, 512], BF16) for i in range(2)]

                P.op("vector", lambda v: v.tensor_copy(ccst[:, 0:1024], S_f[:]),
                     incs=[(ve, 1)])
                P.op("vector", lambda v: v.tensor_copy(ccst[:, 1024:1032], Dsl[:]),
                     waits=[(ve, ve.n)], incs=[(ve, 1)])
                vw = ve.n
                P.op("sync", lambda s: s.dma_start(out=ccin.ap()[:, :], in_=ccst[:]),
                     waits=[(ve, vw)], incs=[(lcc, 16)])
                lw = lcc.n
                P.op("gpsimd", lambda g: g.collective_compute(
                    "AllGather", ALU.bypass,
                    replica_groups=[[0, 1, 2, 3], [4, 5, 6, 7]],
                    ins=[ccin.ap().opt()], outs=[ccout.ap().opt()]),
                     waits=[(lcc, lw)], incs=[(cc, 1)])
                for j in range(3):
                    P.op("sync", lambda s, jj=j: s.dma_start(
                        out=gth[:, jj * 1032:(jj + 1) * 1032],
                        in_=ccout.ap()[jj * 128:(jj + 1) * 128, :]),
                         waits=[(cc, 1)], incs=[(lcc, 16)])
                lw = lcc.n
                P.op("vector", lambda v: v.memset(Sst_f[:], 0.0),
                     waits=[(lcc, lw), (ld, ld_base)], incs=[(ve, 1)])
                for j in range(3):
                    P.op("vector", lambda v, jj=j: v.tensor_scalar(
                        DmS[:], gth[:, jj * 1032 + 1024:jj * 1032 + 1032],
                        um_s[:, jj:jj + 1], um_s[:, 3 + jj:4 + jj],
                        ALU.mult, ALU.add), incs=[(ve, 1)])
                    dms_tick = ve.n
                    for h in range(H):
                        P.op("vector", lambda v, jj=j, hh=h: v.tensor_scalar(
                            mm_scr[:],
                            gth[:, jj * 1032 + hh * 128:jj * 1032 + hh * 128 + 128],
                            um_s[:, jj:jj + 1], None, ALU.mult), incs=[(ve, 1)])
                        P.op("vector", lambda v, hh=h: v.scalar_tensor_tensor(
                            Sst_f[:, hh * 128:hh * 128 + 128],
                            Sst_f[:, hh * 128:hh * 128 + 128],
                            DmS[:, hh:hh + 1], mm_scr[:],
                            ALU.mult, ALU.add),
                             waits=[(ve, dms_tick)] if h == 0 else [],
                             incs=[(ve, 1)])
                P.op("vector", lambda v: v.tensor_copy(Sst_b[:], Sst_f[:]),
                     incs=[(ve, 1)])
                corr_done = {}
                for nb in range(4):
                    for h in range(H):
                        np_ = 4 + (h % 2)
                        bw = bankw[np_].take()
                        for j in range(4):
                            n = nb * 4 + j
                            P.op("vector", lambda v, hh=h, nn=n, jj=j,
                                 q2=qp_b[h % 2]: v.tensor_scalar(
                                     q2[:, jj * 128:jj * 128 + 128],
                                     qh_all[:, hh * NT + nn * 128:
                                            hh * NT + nn * 128 + 128],
                                     P_all[:, hh * 16 + nn:hh * 16 + nn + 1],
                                     None, ALU.mult),
                                 waits=bw if j == 0 else [], incs=[(ve, 1)])
                        vw = ve.n
                        P.op("tensor", lambda te, hh=h, k=np_, q2=qp_b[h % 2]: te.matmul(
                            pb[k][:, 0:512], Sst_b[:, hh * 128:hh * 128 + 128],
                            q2[:], start=True, stop=True),
                             waits=[(ve, vw)], incs=[(pe, 1)])
                        pw = pe.n
                        oa = o_all[:, h * NT + nb * 512:h * NT + nb * 512 + 512]
                        tk = P.op("vector", lambda v, o2=oa, k=np_: v.tensor_tensor(
                            o2, o2, pb[k][:, 0:512], ALU.add),
                             waits=[(pe, pw)], incs=[(ve, 1)])
                        bankw[np_].r.extend(tk)
                    corr_done[nb] = ve.n
                wo_s = sbc("wo_s", [128, DC * D], BF16)
                am_f = sbc("am_f", [128, DC * 512])
                amtm = [sbc(f"amtm{i}", [128, 1024]) for i in range(2)]
                x_st2 = [sbc(f"x_st2_{i}", [128, 1024]) for i in range(2)]
                x1_st = [sbc(f"x1_st{i}", [128, 1024]) for i in range(2)]
                xt_st = [sbc(f"xt_st{i}", [128, 1024]) for i in range(2)]
                xtf_st = [sbc(f"xtf_st{i}", [128, DC * 128], BF16) for i in range(2)]
                amf_w = _Buf()
                amtm_w = [_Buf(), _Buf()]
                x1_w = [_Buf(), _Buf()]
                xt_w = [_Buf(), _Buf()]
                xtf_w = [_Buf(), _Buf()]
                x2_w = [_Buf(), _Buf()]

                s3 = wo_d.rearrange("(hc p) n -> p hc n", p=128)
                P.op("gpsimd", lambda g: g.dma_start(
                    out=wo_s.ap().rearrange("p (hc n) -> p hc n", hc=DC), in_=s3),
                     incs=[(wd, 16)])
                wd_wo = wd.n

                for tt in range(4):
                    for dc in range(DC):
                        bk = dc % 2
                        for h in range(H):
                            P.op("tensor", lambda te, hh=h, d=dc, t4=tt, k=bk: te.matmul(
                                pb[k][:, 0:512],
                                wo_s[:, hh * D + d * 128:hh * D + d * 128 + 128],
                                o_all[:, hh * NT + t4 * 512:hh * NT + t4 * 512 + 512],
                                start=(hh == 0), stop=(hh == H - 1)),
                                 waits=([(wd, wd_wo), (ve, corr_done[tt])]
                                        + bankw[bk].take()) if h == 0 else [],
                                 incs=[(pe, 1)] if h == H - 1 else [])
                        pw = pe.n
                        tk = P.op("vector", lambda v, d=dc, k=bk: v.tensor_copy(
                            am_f[:, d * 512:d * 512 + 512], pb[k][:, 0:512]),
                             waits=[(pe, pw)] + (amf_w.take() if dc == 0 else []),
                             incs=[(ve, 1)])
                        bankw[bk].r.extend(tk)
                    amf_done = ve.n
                    def am_part(sub):
                        sp = sub % 2
                        tok0 = tt * 512 + sub * 128
                        amt, xs2, x1s = amtm[sp], x_st2[sp], x1_st[sp]
                        for dc in range(DC):
                            bk = 2 + dc % 2
                            P.op("tensor", lambda te, d=dc, s4=sub, k=bk: te.transpose(
                                pb[k][:, 0:128],
                                am_f[:, d * 512 + s4 * 128:d * 512 + s4 * 128 + 128],
                                ident_f[:]),
                                 waits=[(ve, amf_done)] + bankw[bk].take(),
                                 incs=[(pe, 1)])
                            pw = pe.n
                            tk = P.op("vector", lambda v, d=dc, k=bk, t=amt: v.tensor_copy(
                                t[:, d * 128:d * 128 + 128], pb[k][:, 0:128]),
                                 waits=[(pe, pw)] + (amtm_w[sp].take()
                                                     if dc == 0 else []),
                                 incs=[(ve, 1)])
                            bankw[bk].r.extend(tk)
                        amf_w.r.append((ve, ve.n))
                        tick = ln_tile(amt[:], amt[:])
                        amtm_w[sp].r.extend(tick)
                        P.op("sync", lambda s, o=tok0, t=xs2: s.dma_start(
                            out=t[:], in_=xsl[o:o + 128, :]),
                             waits=x2_w[sp].take(), incs=[(lx2[sp], 16)])
                        lw = lx2[sp].n
                        tk = P.op("vector", lambda v, t=x1s, u=xs2, w2=amt: v.tensor_tensor(
                            t[:], u[:], w2[:], ALU.add),
                             waits=[(lx2[sp], lw), (ve, ve.n)] + x1_w[sp].take(),
                             incs=[(ve, 1)])
                        x2_w[sp].r.extend(tk)
                        amtm_w[sp].r.extend(tk)
                        vw = ve.n
                        P.op("sync", lambda s, o=tok0, t=x1s: s.dma_start(
                            out=x1sp[o:o + 128, :], in_=t[:]),
                             waits=[(ve, vw)], incs=[(ls1[sp], 16)])
                        x1_w[sp].r.append((ls1[sp], ls1[sp].n))
                        return vw

                    def xt_part(sub, x1vw):
                        sp = sub % 2
                        tok0 = tt * 512 + sub * 128
                        x1s, xts, xtf = x1_st[sp], xt_st[sp], xtf_st[sp]
                        P.op("scalar", lambda s, t=xts, u=x1s: s.copy(t[:], u[:]),
                             waits=[(ve, x1vw)] + xt_w[sp].take(), incs=[(se, 1)])
                        x1_w[sp].r.append((se, se.n))
                        tick = ln_tile(xts[:], xts[:], extra_waits=[(se, se.n)])
                        for dc in range(DC):
                            bk = 2 + dc % 2
                            P.op("tensor", lambda te, d=dc, k=bk, t=xts: te.transpose(
                                pb[k][:, 0:128], t[:, d * 128:d * 128 + 128],
                                ident_f[:]),
                                 waits=[(ve, ve.n)] + bankw[bk].take(),
                                 incs=[(pe, 1)])
                            pw = pe.n
                            tk = P.op("vector", lambda v, d=dc, k=bk, t=xtf: v.tensor_copy(
                                t[:, d * 128:d * 128 + 128], pb[k][:, 0:128]),
                                 waits=[(pe, pw)] + (xtf_w[sp].take()
                                                     if dc == 0 else []),
                                 incs=[(ve, 1)])
                            bankw[bk].r.extend(tk)
                            xt_w[sp].r.extend(tk)
                        vw = ve.n
                        P.op("sync", lambda s, o=tok0, t=xtf: s.dma_start(
                            out=xtsp[:, :, o:o + 128],
                            in_=t.ap().rearrange("p (dc n) -> p dc n", dc=DC)),
                             waits=[(ve, vw)], incs=[(lxt[sp], 16)])
                        xtf_w[sp].r.append((lxt[sp], lxt[sp].n))

                    x1vws = {}
                    for sub in range(4):
                        x1vws[sub] = am_part(sub)
                        if sub >= 1:
                            xt_part(sub - 1, x1vws[sub - 1])
                    xt_part(3, x1vws[3])

        P.barrier()
        if stage == "x1":
            with ExitStack() as ctxX:
                xo = ctxX.enter_context(nc.sbuf_tensor("xo", [128, 1024], F32))
                first = [(ls1[0], ls1[0].n), (ls1[1], ls1[1].n)]
                for t in range(16):
                    P.op("sync", lambda s, o=t * 128: s.dma_start(
                        out=xo[:], in_=x1sp[o:o + 128, :]),
                         waits=[(od, od.n)] + first, incs=[(lr1, 16)])
                    first = []
                    lw = lr1.n
                    P.op("sync", lambda s, o=t * 128: s.dma_start(
                        out=out_d[o:o + 128, :], in_=xo[:]),
                         waits=[(lr1, lw)], incs=[(od, 16)])
                P.op("sync", lambda s: s.nop(), waits=[(od, od.n)])
            P.realize(nc)
            return nc

        with ExitStack() as ctxD:
            sbd = lambda n, s, d=F32: ctxD.enter_context(nc.sbuf_tensor(n, s, d))
            xt_half = sbd("xt_half", [128, DC * HALF], BF16)
            h1s = sbd("h1s", [128, IC * HALF], BF16)
            y_f = sbd("y_f", [128, DC * HALF])
            wA = sbd("wA", [128, 16 * 1024], BF16)
            wB = sbd("wB", [128, 16 * 1024], BF16)
            wC = sbd("wC", [128, 16 * 1024], BF16)
            cbc = sbd("cbc", [128, HALF])
            gwT_s = sbd("gwT_s", [128, DC * 8], BF16)
            c_tok = sbd("c_tok", [128, 16 * 8])
            c_fm = sbd("c_fm", [8, HALF])
            sm_scr = sbd("sm_scr", [128, 8])
            h3c = sbd("h3c", [128, 512], BF16)
            c_row = cbc
            ytm = [sbd(f"ytm{i}", [128, 1024]) for i in range(2)]
            x1r = sbd("x1r", [128, 1024])
            wbuf = [wA, wB, wC]
            wb_w = [_Buf(), _Buf(), _Buf()]
            cfence = [(ls1[0], ls1[0].n), (ls1[1], ls1[1].n),
                      (lxt[0], lxt[0].n), (lxt[1], lxt[1].n),
                      (lx2[0], lx2[0].n), (lx2[1], lx2[1].n)]
            wb_w[0].r += cfence
            wb_w[1].r += cfence
            wb_w[2].r += cfence
            xth_w = _Buf()
            h1_w = _Buf()
            y_w = _Buf()
            cbc_w = _Buf()
            crow_w = _Buf()
            ctok_w = _Buf()
            cfm_w = _Buf()
            ytm_w = [_Buf(), _Buf()]
            x1r_w = _Buf()
            outst_w = _Buf()

            P.op("gpsimd", lambda g: g.dma_start(
                out=gwT_s.ap().rearrange("p (dc n) -> p dc n", dc=DC),
                in_=gwT_d.rearrange("(dc p) n -> p dc n", p=128)),
                 waits=list(cfence), incs=[(wd, 16)])
            wd_gw = wd.n

            e_order = [8] + list(range(8))

            for half in range(2):
                htok0 = half * HALF
                P.op("sync", lambda s, o=htok0: s.dma_start(
                    out=xt_half.ap().rearrange("p (dc n) -> p dc n", dc=DC),
                    in_=xtsp[:, :, o:o + HALF]),
                     waits=xth_w.take() + list(cfence), incs=[(lxh, 16)])
                xt_done_ld = lxh.n

                for t8 in range(8):
                    bk = t8 % 2
                    for dc in range(DC):
                        P.op("tensor", lambda te, t=t8, d=dc, k=bk: te.matmul(
                            pb[k][:, 0:8],
                            xt_half[:, d * HALF + t * 128:d * HALF + t * 128 + 128],
                            gwT_s[:, d * 8:d * 8 + 8],
                            start=(d == 0), stop=(d == DC - 1)),
                             waits=([(lxh, xt_done_ld), (wd, wd_gw)]
                                    + bankw[bk].take()) if dc == 0 else [],
                             incs=[(pe, 1)] if dc == DC - 1 else [])
                    pw = pe.n
                    mx, sμ, m1c, m2c = col(), col(), col(), col()
                    P.op("vector", lambda v, k=bk, m=mx: v.reduce_max(
                        m, pb[k][:, 0:8], axis=AXX),
                         waits=[(pe, pw)] + ctok_w.take(), incs=[(ve, 1)])
                    nmx = col()
                    P.op("vector", lambda v, m=mx, nm=nmx: v.tensor_scalar(
                        nm, m, -1.0, None, ALU.mult),
                         waits=[(ve, ve.n)], incs=[(ve, 1)])
                    vw = ve.n
                    P.op("scalar", lambda s, k=bk, nm=nmx: s.activation(
                        out=sm_scr[:], in_=pb[k][:, 0:8], func=AF.Exp, bias=nm),
                         waits=[(ve, vw)], incs=[(se, 1)])
                    sw = se.n
                    tk = P.op("vector", lambda v, m=sμ: v.reduce_sum(
                        m, sm_scr[:], axis=AXX), waits=[(se, sw)], incs=[(ve, 1)])
                    bankw[bk].r.extend(tk)
                    P.op("vector", lambda v, m=sμ: v.reciprocal(m, m),
                         waits=[(ve, ve.n)], incs=[(ve, 1)])
                    P.op("vector", lambda v, m=sμ: v.tensor_scalar(
                        sm_scr[:], sm_scr[:], m, None, ALU.mult),
                         waits=[(ve, ve.n)], incs=[(ve, 1)])
                    ctk = c_tok[:, t8 * 8:t8 * 8 + 8]
                    P.op("vector", lambda v, ct=ctk: v.tensor_tensor(
                        ct, sm_scr[:], gbias_s[:], ALU.add),
                         waits=[(ve, ve.n), (ld, ld_base)], incs=[(ve, 1)])
                    P.op("vector", lambda v, ct=ctk, m=m1c: v.reduce_max(
                        m, ct, axis=AXX), waits=[(ve, ve.n)], incs=[(ve, 1)])
                    P.op("vector", lambda v, ct=ctk, m=m1c: v.tensor_scalar(
                        sm_scr[:], ct, m, None, ALU.is_ge),
                         waits=[(ve, ve.n)], incs=[(ve, 1)])
                    P.op("vector", lambda v, ct=ctk: v.scalar_tensor_tensor(
                        sm_scr[:], sm_scr[:], -1e30, ct, ALU.mult, ALU.add),
                         waits=[(ve, ve.n)], incs=[(ve, 1)])
                    P.op("vector", lambda v, m=m2c: v.reduce_max(
                        m, sm_scr[:], axis=AXX),
                         waits=[(ve, ve.n)], incs=[(ve, 1)])
                    P.op("vector", lambda v, ct=ctk, m=m2c: v.tensor_scalar(
                        sm_scr[:], ct, m, None, ALU.is_ge),
                         waits=[(ve, ve.n)], incs=[(ve, 1)])
                    P.op("vector", lambda v, ct=ctk: v.tensor_tensor(
                        ct, ct, gbias_s[:], ALU.subtract),
                         waits=[(ve, ve.n)], incs=[(ve, 1)])
                    P.op("vector", lambda v, ct=ctk: v.tensor_tensor(
                        ct, ct, sm_scr[:], ALU.mult),
                         waits=[(ve, ve.n)], incs=[(ve, 1)])
                    vw = ve.n
                    P.op("tensor", lambda te, ct=ctk, k=2 + bk: te.transpose(
                        pb[k][0:8, 0:128], ct, ident_f[:]),
                         waits=[(ve, vw)] + bankw[2 + bk].take(), incs=[(pe, 1)])
                    pw = pe.n
                    tk = P.op("vector", lambda v, t=t8, k=2 + bk: v.tensor_copy(
                        c_fm[0:8, t * 128:t * 128 + 128],
                        pb[k][0:8, 0:128]),
                         waits=[(pe, pw)] + (cfm_w.take() if t8 == 0 else []),
                         incs=[(ve, 1)])
                    bankw[2 + bk].r.extend(tk)
                    ctok_w.r.extend(tk)
                cfm_done = ve.n

                lctr = [0]

                def wstream(src4, parity, d4):
                    P.op("gpsimd", lambda g, s=src4, d=d4: g.dma_start(
                        out=d, in_=s),
                         waits=wb_w[parity].take(), incs=[(wdp[parity], 16)])
                    return wdp[parity].n

                for ei, e in enumerate(e_order):
                    par1 = lctr[0] % 3
                    wd1s = wdp[par1]
                    wd1 = wstream(
                        w1T_d[e].rearrange("(dc p) (ic n) -> p dc ic n",
                                           p=128, n=128),
                        par1,
                        wbuf[par1].ap().rearrange("p (dc ic n) -> p dc ic n",
                                                  dc=DC, ic=IC))
                    lctr[0] += 1
                    par3 = lctr[0] % 3
                    wd3s = wdp[par3]
                    wd3 = wstream(
                        w3T_d[e].rearrange("(dc p) (ic n) -> p dc ic n",
                                           p=128, n=128),
                        par3,
                        wbuf[par3].ap().rearrange("p (dc ic n) -> p dc ic n",
                                                  dc=DC, ic=IC))
                    lctr[0] += 1
                    par2 = lctr[0] % 3
                    wd2s = wdp[par2]
                    wd2 = wstream(
                        w2T_d[e].rearrange("(ic p) (dc n) -> p ic dc n",
                                           p=128, n=128),
                        par2,
                        wbuf[par2].ap().rearrange("p (ic dc n) -> p ic dc n",
                                                  ic=IC, dc=DC))
                    lctr[0] += 1

                    if e != 8:
                        P.op("sync", lambda s, ee=e: s.dma_start(
                            out=c_row[0:1, :],
                            in_=c_fm[ee:ee + 1, 0:HALF]),
                             waits=[(ve, cfm_done)] + crow_w.take(),
                             incs=[(lcr, 16)])
                        wd_cr = lcr.n
                        for t2 in range(2):
                            tkp = P.op("tensor", lambda te, t=t2: te.matmul(
                                pb[4][:, 0:512], ones1_f[0:1, :],
                                c_row[0:1, t * 512:t * 512 + 512],
                                start=True, stop=True),
                                 waits=[(lcr, wd_cr)] + bankw[4].take(),
                                 incs=[(pe, 1)])
                            crow_w.r.extend(tkp)
                            pw = pe.n
                            tk = P.op("vector", lambda v, t=t2: v.tensor_copy(
                                cbc[:, t * 512:t * 512 + 512], pb[4][:, 0:512]),
                                 waits=[(pe, pw)] + (cbc_w.take()
                                                     if t2 == 0 else []),
                                 incs=[(ve, 1)])
                            bankw[4].r.extend(tk)
                        cbc_done = ve.n
                    else:
                        cbc_done = 0

                    for ic in range(IC):
                        for t2 in range(2):
                            bk = (ic * 2 + t2) % 2
                            for dc in range(DC):
                                P.op("tensor", lambda te, i=ic, t=t2, d=dc, k=bk,
                                     p=par1: te.matmul(
                                         pb[k][:, 0:512],
                                         wbuf[p][:, d * 2048 + i * 128:
                                                 d * 2048 + i * 128 + 128],
                                         xt_half[:, d * HALF + t * 512:
                                                 d * HALF + t * 512 + 512],
                                         start=(d == 0), stop=(d == DC - 1)),
                                     waits=([(wd1s, wd1), (lxh, xt_done_ld)]
                                            + bankw[bk].take()) if dc == 0 else [],
                                     incs=[(pe, 1)] if dc == DC - 1 else [])
                            pw = pe.n
                            tk = P.op("scalar", lambda s, i=ic, t=t2, k=bk: s.activation(
                                out=h1s[:, i * HALF + t * 512:i * HALF + t * 512 + 512],
                                in_=pb[k][:, 0:512], func=AF.Silu),
                                 waits=[(pe, pw)] + (h1_w.take()
                                                     if ic == 0 and t2 == 0 else []),
                                 incs=[(se, 1)])
                            bankw[bk].r.extend(tk)
                            wb_w[par1].r.extend(tk)
                    for ic in range(IC):
                        for t2 in range(2):
                            bk = (ic * 2 + t2) % 2
                            for dc in range(DC):
                                P.op("tensor", lambda te, i=ic, t=t2, d=dc, k=bk,
                                     p=par3: te.matmul(
                                         pb[k][:, 0:512],
                                         wbuf[p][:, d * 2048 + i * 128:
                                                 d * 2048 + i * 128 + 128],
                                         xt_half[:, d * HALF + t * 512:
                                                 d * HALF + t * 512 + 512],
                                         start=(d == 0), stop=(d == DC - 1)),
                                     waits=([(wd3s, wd3)] + bankw[bk].take())
                                     if dc == 0 else [],
                                     incs=[(pe, 1)] if dc == DC - 1 else [])
                            pw = pe.n
                            hsl = h1s[:, ic * HALF + t2 * 512:
                                      ic * HALF + t2 * 512 + 512]
                            if e != 8:
                                tkc = P.op("vector", lambda v, t=t2, k=bk: v.tensor_tensor(
                                    h3c[:], pb[k][:, 0:512],
                                    cbc[:, t * 512:t * 512 + 512], ALU.mult),
                                     waits=[(pe, pw), (ve, cbc_done), (se, se.n)],
                                     incs=[(ve, 1)])
                                crow_w.r.extend(tkc)
                                tk = P.op("vector", lambda v, hs=hsl: v.tensor_tensor(
                                    hs, h3c[:], hs, ALU.mult), incs=[(ve, 1)])
                            else:
                                tk = P.op("vector", lambda v, hs=hsl, k=bk: v.tensor_tensor(
                                    hs, pb[k][:, 0:512], hs, ALU.mult),
                                     waits=[(pe, pw), (se, se.n)], incs=[(ve, 1)])
                            bankw[bk].r.extend(tk)
                            wb_w[par3].r.extend(tk)
                            if e != 8:
                                cbc_w.r.extend(tk)
                    ve_p2 = ve.n
                    if ei == 0:
                        ydc_tick = {}
                    for dc in range(DC):
                        for t2 in range(2):
                            bk = (dc * 2 + t2) % 2
                            for ic in range(IC):
                                P.op("tensor", lambda te, i=ic, t=t2, d=dc, k=bk,
                                     p=par2: te.matmul(
                                         pb[k][:, 0:512],
                                         wbuf[p][:, i * 1024 + d * 128:
                                                 i * 1024 + d * 128 + 128],
                                         h1s[:, i * HALF + t * 512:
                                             i * HALF + t * 512 + 512],
                                         start=(i == 0), stop=(i == IC - 1)),
                                     waits=([(wd2s, wd2), (ve, ve_p2)]
                                            + bankw[bk].take()) if ic == 0 else [],
                                     incs=[(pe, 1)] if ic == IC - 1 else [])
                            pw = pe.n
                            ysl = y_f[:, dc * HALF + t2 * 512:
                                      dc * HALF + t2 * 512 + 512]
                            if ei == 0:
                                tk = P.op("vector", lambda v, ys=ysl, k=bk: v.tensor_copy(
                                    ys, pb[k][:, 0:512]),
                                     waits=[(pe, pw)] + (y_w.take()
                                                         if dc == 0 and t2 == 0 else []),
                                     incs=[(ve, 1)])
                            else:
                                tk = P.op("vector", lambda v, ys=ysl, k=bk: v.tensor_tensor(
                                    ys, ys, pb[k][:, 0:512], ALU.add),
                                     waits=[(pe, pw)], incs=[(ve, 1)])
                            bankw[bk].r.extend(tk)
                            wb_w[par2].r.extend(tk)
                            h1_w.r.extend(tk)
                            ydc_tick[dc] = ve.n
                y_done = ve.n

                for sub in range(8):
                    sp = sub % 2
                    tok0 = htok0 + sub * 128
                    yt, xr = ytm[sp], x1r
                    for dc in range(DC):
                        bk = 2 + dc % 2
                        P.op("tensor", lambda te, d=dc, s4=sub, k=bk: te.transpose(
                            pb[k][:, 0:128],
                            y_f[:, d * HALF + s4 * 128:d * HALF + s4 * 128 + 128],
                            ident_f[:]),
                             waits=[(ve, ydc_tick[dc])] + bankw[bk].take(),
                             incs=[(pe, 1)])
                        pw = pe.n
                        tk = P.op("vector", lambda v, d=dc, k=bk, t=yt: v.tensor_copy(
                            t[:, d * 128:d * 128 + 128], pb[k][:, 0:128]),
                             waits=[(pe, pw)] + (ytm_w[sp].take()
                                                 if dc == 0 else []),
                             incs=[(ve, 1)])
                        bankw[bk].r.extend(tk)
                        y_w.r.extend(tk)
                    tick = ln_tile(yt[:], yt[:])
                    ytm_w[sp].r.extend(tick)
                    P.op("sync", lambda s, o=tok0, t=xr: s.dma_start(
                        out=t[:], in_=x1sp[o:o + 128, :]),
                         waits=x1r_w.take() + [(ls1[0], ls1[0].n),
                                               (ls1[1], ls1[1].n)],
                         incs=[(lr1, 16)])
                    lw = lr1.n
                    tk = P.op("vector", lambda v, t=xr, u=yt: v.tensor_tensor(
                        t[:], t[:], u[:], ALU.add),
                         waits=[(lr1, lw), (ve, ve.n)], incs=[(ve, 1)])
                    ytm_w[sp].r.extend(tk)
                    vw = ve.n
                    P.op("sync", lambda s, o=tok0, t=xr: s.dma_start(
                        out=out_d[o:o + 128, :], in_=t[:]),
                         waits=[(ve, vw)], incs=[(od, 16)])
                    x1r_w.r.append((od, od.n))
                xth_w.r.append((pe, pe.n))

            if stage == "fulldbg":
                vend, pend, send = ve.n, pe.n, se.n
                for (dst, srcb) in ((f_y, y_f), (f_cfm, c_fm), (f_xth, xt_half),
                                    (f_h1s, h1s), (f_cbc, cbc)):
                    P.op("sync", lambda s, d=dst, sr=srcb: s.dma_start(
                        out=d[:, :], in_=sr.ap()),
                         waits=[(ve, vend), (pe, pend), (se, send)],
                         incs=[(od, 16)])
            P.op("sync", lambda s: s.nop(), waits=[(od, od.n)])

        P.realize(nc)
    return nc


_NC_CACHE = {}


def _get_nc(stage="full"):
    if stage not in _NC_CACHE:
        _NC_CACHE[stage] = build_program(stage)
    return _NC_CACHE[stage]


def stage_inputs(inputs):
    f = lambda a: np.ascontiguousarray(np.asarray(a, dtype=np.float32))
    x = f(inputs["L0_attn_x"])
    wq = f(inputs["L0_attn_Wq"])
    wk = f(inputs["L0_attn_Wk"])
    wv = f(inputs["L0_attn_Wv"])
    wg = f(inputs["L0_attn_Wg"])
    wo = f(inputs["L0_attn_Wo"])
    gw = f(inputs["L0_ffn_gate_w"])
    gb = f(inputs["L0_ffn_gate_bias"])
    w1 = f(inputs["L0_ffn_w1"])
    w2 = f(inputs["L0_ffn_w2"])
    w3 = f(inputs["L0_ffn_w3"])
    sw1 = f(inputs["L0_ffn_sw1"])
    sw2 = f(inputs["L0_ffn_sw2"])
    sw3 = f(inputs["L0_ffn_sw3"])

    w1T = np.ascontiguousarray(
        np.concatenate([w1.transpose(0, 2, 1), sw1.T[None]], axis=0))
    w3T = np.ascontiguousarray(
        np.concatenate([w3.transpose(0, 2, 1), sw3.T[None]], axis=0))
    w2T = np.ascontiguousarray(
        np.concatenate([w2.transpose(0, 2, 1), sw2.T[None]], axis=0))
    gwT = np.ascontiguousarray(gw.T)
    gbias = np.ascontiguousarray(np.tile(gb[None, :], (128, 1)))

    in_maps = []
    for c in range(NCORE):
        b, s = c // 4, c % 4
        u = np.zeros((128, 6), dtype=np.float32)
        for j in range(3):
            if j < s:
                u[:, j] = 1.0
            else:
                u[:, 3 + j] = 1.0
        u[:, 3:6] = 1.0 - u[:, 0:3]
        in_maps.append({
            "xsl": np.ascontiguousarray(x[b, s * NT:(s + 1) * NT, :]),
            "wq": wq, "wk": wk, "wv": wv, "wg": wg, "wo": wo,
            "gwT": gwT, "gbias": gbias,
            "w1T": w1T, "w3T": w3T, "w2T": w2T,
            "um": np.ascontiguousarray(u),
        })
    return in_maps


def kernel(**inputs):
    in_maps = stage_inputs(inputs)
    nc = _get_nc("full")
    res = run_bass_kernel_spmd(nc, in_maps, core_ids=list(range(NCORE)))
    out = np.empty((2, 8192, D), dtype=np.float32)
    for c in range(NCORE):
        b, s = c // 4, c % 4
        out[b, s * NT:(s + 1) * NT, :] = res.results[c]["out"]
    return out
